# version 09
# baseline (speedup 1.0000x reference)
import math
import numpy as np
import ml_dtypes
import concourse.bass as bass
import concourse.mybir as mybir
from concourse.bass_utils import run_bass_kernel_spmd

F32 = mybir.dt.float32
BF16 = mybir.dt.bfloat16
AF = mybir.ActivationFunctionType
ALU = mybir.AluOpType
AX = mybir.AxisListType

D = 2048
H = 16
NOPE = 128
ROPE = 64
QK = 192
VD = 128
KVL = 512
QL = 1536
INTER = 8192
EPS_LN = 1e-5
EPS_RMS = 1e-6
ATT_SCALE = QK ** -0.5
THETA = 5e5

NC_ = 8
GROUPS = [[0, 1, 2, 3], [4, 5, 6, 7]]
HPC = 4

BF = 2
FP = 4


class Sem:
    def __init__(self, h):
        self.h = h
        self.n = 0

    def bump(self, k=1):
        self.n += k
        return self.n


class Prog:

    ENGS = ("pe", "act", "dve", "pool", "sp")

    def __init__(self):
        self.q = {e: [] for e in self.ENGS}

    def wait(self, eng, sem, val):
        if val > 0:
            self.q[eng].append(lambda e, h=sem.h, v=val: e.wait_ge(h, v))

    def op(self, eng, build, inc=None, inck=1):
        if inc is None:
            self.q[eng].append(build)
            return None
        v = inc.bump(inck)
        def f(e, b=build, h=inc.h, k=inck):
            inst = b(e)
            inst.then_inc(h, k)
        self.q[eng].append(f)
        return v

    def drain(self, eng):
        self.q[eng].append(lambda e: e.drain())


def ceil_div(a, b):
    return (a + b - 1) // b


def build_program(T=8192, debug=False, taps=()):
    OWN = T // 4
    NT = OWN // 128
    NW = T // 512
    NKT = T // 128
    NB = OWN // 512
    QBLK = T // 2048
    assert OWN % 512 == 0 and T % 2048 == 0

    nc = bass.Bass("TRN2", target_bir_lowering=False, debug=debug)

    def din(name, shape, dt):
        return nc.dram_tensor(name, shape, dt, kind="ExternalInput")

    x_own = din("x_own", [OWN, D], F32)
    x_batch = din("x_batch", [T, D], F32)
    q_a_w = din("q_a_w", [D, QL], F32)
    q_b_w = din("q_b_w", [QL, H * QK], F32)
    kv_a_w = din("kv_a_w", [D, KVL + ROPE], F32)
    kv_b_w = din("kv_b_w", [KVL, H * 256], F32)
    o_w = din("o_w", [D, D], F32)
    w_up = din("w_up", [INTER, D], F32)
    w_gate = din("w_gate", [INTER, D], F32)
    w_down = din("w_down", [D, INTER], F32)
    wq_cols = din("wq_cols", [128, QL // 128], F32)
    wkv_cols = din("wkv_cols", [128, KVL // 128], F32)
    su_ug_cols = din("su_ug_cols", [128, 16], F32)
    su_dp_cols = din("su_dp_cols", [128, 64], F32)
    wsc_ug_cols = din("wsc_ug_cols", [128, 128], F32)
    wsc_dp_cols = din("wsc_dp_cols", [128, 16], F32)
    cos_all = din("cos_all", [T, 32], F32)
    sin_all = din("sin_all", [T, 32], F32)
    cos16_own = din("cos16_own", [OWN, 512], F32)
    sin16_own = din("sin16_own", [OWN, 512], F32)
    h128 = din("h128", [128, 128], F32)
    ones_bf = din("ones_bf", [128, 128], BF16)
    ident_bf = din("ident_bf", [128, 128], BF16)
    ident_f32 = din("ident_f32", [128, 128], F32)
    masks_in = din("masks_in", [128, (OWN // 512) * 16 * 512], BF16)
    fconst = din("fconst", [128, 4], F32)

    out = nc.dram_tensor("out", [OWN, D], F32, kind="ExternalOutput")

    qspill = nc.dram_tensor("qspill", [OWN, H * QK], BF16)
    ospill = nc.dram_tensor("ospill", [H * VD, OWN], BF16)
    x2d = nc.dram_tensor("x2d", [OWN, D], F32)
    y2d = nc.dram_tensor("y2d", [INTER, OWN], BF16)
    wdTd = nc.dram_tensor("wdTd", [INTER, D], BF16)

    tap_outs = {}
    for tname in taps:
        src = {"qspill": qspill, "ospill": ospill, "x2d": x2d,
               "y2d": y2d, "wdTd": wdTd}[tname]
        tap_outs[tname] = (nc.dram_tensor("dbg_" + tname, list(src.shape),
                                          src.dtype, kind="ExternalOutput"), src)

    P = Prog()

    import contextlib
    ctx = contextlib.ExitStack()
    with ctx:
        AR1 = ctx.enter_context(nc.sbuf_tensor("AR1", [128, 24576], F32))
        AR2 = ctx.enter_context(nc.sbuf_tensor("AR2", [128, 20480], F32))

        def v32(ar, off, n):
            return ar[:, off:off + n]

        def vbf(ar, off, n):
            return ar.ap().bitcast(BF16)[:, 2 * off:2 * off + n]

        onesb = ctx.enter_context(nc.sbuf_tensor("onesb", [128, 128], BF16))
        identb = ctx.enter_context(nc.sbuf_tensor("identb", [128, 128], BF16))
        identf = ctx.enter_context(nc.sbuf_tensor("identf", [128, 128], F32))
        h128f = ctx.enter_context(nc.sbuf_tensor("h128f", [128, 128], F32))
        hsu_sb = ctx.enter_context(nc.sbuf_tensor("hsu_sb", [128, 16 * 128], BF16))
        h128b_sb = ctx.enter_context(nc.sbuf_tensor("h128b_sb", [128, 128], BF16))
        colscr = ctx.enter_context(nc.sbuf_tensor("colscr", [128, 256], F32))
        rowscr = ctx.enter_context(nc.sbuf_tensor("rowscr", [16, 512], F32))
        upsc_cols = ctx.enter_context(nc.sbuf_tensor("upsc_cols", [128, 64], F32))
        wscg_cols = ctx.enter_context(nc.sbuf_tensor("wscg_cols", [128, 64], F32))
        wscdp_sb = ctx.enter_context(nc.sbuf_tensor("wscdp_sb", [128, 16], F32))
        wqc_sb = ctx.enter_context(nc.sbuf_tensor("wqc_sb", [128, 12], F32))
        wkvc_sb = ctx.enter_context(nc.sbuf_tensor("wkvc_sb", [128, 4], F32))
        suug_sb = ctx.enter_context(nc.sbuf_tensor("suug_sb", [128, 16], F32))
        bnst = ctx.enter_context(nc.sbuf_tensor("bnst", [128, 24], F32))
        mvt = ctx.enter_context(nc.sbuf_tensor("mvt", [128, 2], F32))
        rstdt = ctx.enter_context(nc.sbuf_tensor("rstdt", [128, 4], F32))
        sstat = ctx.enter_context(nc.sbuf_tensor("sstat", [128, 16], F32))
        fc = ctx.enter_context(nc.sbuf_tensor("fc", [128, 4], F32))
        nb_bf = ctx.enter_context(nc.sbuf_tensor("nb_bf", [1, 16], BF16))
        onesf = ctx.enter_context(nc.sbuf_tensor("onesf", [128, 1], F32))
        recb = ctx.enter_context(nc.sbuf_tensor("recb", [1, 512], BF16))

        g1 = ctx.enter_context(nc.psum_tensor("g1", [128, 1024], F32))
        g2 = ctx.enter_context(nc.psum_tensor("g2", [128, 1024], F32))
        g3 = ctx.enter_context(nc.psum_tensor("g3", [128, 512], F32))
        g4 = ctx.enter_context(nc.psum_tensor("g4", [128, 512], F32))
        g5 = ctx.enter_context(nc.psum_tensor("g5", [128, 512], F32))
        g6 = ctx.enter_context(nc.psum_tensor("g6", [128, 512], F32))
        psA = g3.ap().bitcast(BF16)
        psB = g4.ap().bitcast(BF16)
        ps1 = g5.ap()
        ps2 = g2.ap()[:, 0:512]
        ps3 = g6.ap()
        ps4 = g1.ap()[:, 0:512]

        def mksem(name):
            return Sem(ctx.enter_context(nc.semaphore(name)))
        s_c = mksem("s_c")
        s_x0, s_x1 = mksem("s_x0"), mksem("s_x1")
        s_wa = mksem("s_wa")
        s_mk = mksem("s_mk")
        s_kb = mksem("s_kb")
        s_pe, s_act, s_dve = mksem("s_pe"), mksem("s_act"), mksem("s_dve")
        s_out = mksem("s_out")
        s_po = mksem("s_po")
        s_bar = mksem("s_bar")
        sx = [s_x0, s_x1]
        s_qs0, s_qs1 = mksem("s_qs0"), mksem("s_qs1")
        s_cs0, s_cs1 = mksem("s_cs0"), mksem("s_cs1")
        s_kv = mksem("s_kv")
        s_qr0, s_qr1 = mksem("s_qr0"), mksem("s_qr1")
        s_c20 = mksem("s_c20")
        s_c21 = mksem("s_c21")
        s_oh0, s_oh1 = mksem("s_oh0"), mksem("s_oh1")
        s_ow = mksem("s_ow")
        s_wu0, s_wu1 = mksem("s_wu0"), mksem("s_wu1")
        s_y0, s_y1 = mksem("s_y0"), mksem("s_y1")
        s_wd0, s_wd1 = mksem("s_wd0"), mksem("s_wd1")
        s_x2r0, s_x2r1 = mksem("s_x2r0"), mksem("s_x2r1")
        s_wt0, s_wt1 = mksem("s_wt0"), mksem("s_wt1")
        s_y2w0, s_y2w1 = mksem("s_y2w0"), mksem("s_y2w1")
        s_wl0, s_wl1 = mksem("s_wl0"), mksem("s_wl1")
        s_pl = mksem("s_pl")

        ALL_SEMS = [s_c, s_x0, s_x1, s_wa, s_mk, s_kb, s_out, s_po,
                    s_qs0, s_qs1, s_cs0, s_cs1, s_kv, s_qr0, s_qr1, s_c20, s_c21,
                    s_oh0, s_oh1, s_ow, s_wu0, s_wu1, s_y0, s_y1,
                    s_wd0, s_wd1, s_x2r0, s_x2r1, s_wt0, s_wt1,
                    s_y2w0, s_y2w1, s_wl0, s_wl1]

        def barrier():
            for e in P.ENGS:
                P.op(e, lambda en: en.nop(), inc=s_bar)
            tgt = s_bar.n
            for e in P.ENGS:
                P.wait(e, s_bar, tgt)
                for s in ALL_SEMS + [s_pe, s_act, s_dve, s_pl]:
                    P.wait(e, s, s.n)

        LNS = math.log(ATT_SCALE)
        NCH = OWN // 512
        LIMS = [16 * (s + 1) for s in range(NCH)]

        P.op("sp", lambda e: e.dma_start(out=onesb[:, :], in_=ones_bf.ap()), inc=s_c, inck=16)
        P.op("sp", lambda e: e.dma_start(out=identb[:, :], in_=ident_bf.ap()), inc=s_c, inck=16)
        P.op("sp", lambda e: e.dma_start(out=identf[:, :], in_=ident_f32.ap()), inc=s_c, inck=16)
        P.op("sp", lambda e: e.dma_start(out=h128f[:, :], in_=h128.ap()), inc=s_c, inck=16)
        P.op("sp", lambda e: e.dma_start(out=wqc_sb[:, :], in_=wq_cols.ap()), inc=s_c, inck=16)
        P.op("sp", lambda e: e.dma_start(out=wkvc_sb[:, :], in_=wkv_cols.ap()), inc=s_c, inck=16)
        P.op("sp", lambda e: e.dma_start(out=suug_sb[:, :], in_=su_ug_cols.ap()), inc=s_c, inck=16)
        P.op("sp", lambda e: e.dma_start(out=upsc_cols[:, :], in_=wsc_ug_cols.ap()[:, 0:64]), inc=s_c, inck=16)
        P.op("sp", lambda e: e.dma_start(out=wscg_cols[:, :], in_=wsc_ug_cols.ap()[:, 64:128]), inc=s_c, inck=16)
        P.op("sp", lambda e: e.dma_start(out=wscdp_sb[:, :], in_=wsc_dp_cols.ap()), inc=s_c, inck=16)
        P.op("sp", lambda e: e.dma_start(out=colscr[:, 0:64], in_=su_dp_cols.ap()), inc=s_c, inck=16)
        P.op("sp", lambda e: e.dma_start(out=fc[:, :], in_=fconst.ap()), inc=s_c, inck=16)
        SC = s_c.n

        P.wait("dve", s_c, SC)
        P.op("dve", lambda e: e.memset(onesf[:, :], 1.0), inc=None)
        P.drain("dve")
        P.op("dve", lambda e: e.tensor_tensor(out=upsc_cols[:, :],
                                              in0=upsc_cols[:, :],
                                              in1=colscr[:, 0:64], op=ALU.mult),
             inc=s_dve)
        for c in range(16):
            P.op("dve", lambda e, c=c: e.tensor_scalar(
                out=hsu_sb[:, c * 128:(c + 1) * 128], in0=h128f[:, :],
                scalar1=suug_sb[:, c:c + 1], scalar2=1.0 / math.sqrt(2048.0),
                op0=ALU.mult, op1=ALU.mult), inc=s_dve)
        P.op("dve", lambda e: e.tensor_scalar(
            out=h128b_sb[:, :], in0=h128f[:, :], scalar1=1.0 / math.sqrt(8192.0),
            scalar2=None, op0=ALU.mult), inc=s_dve)
        consts_done = s_dve.n
        P.wait("pe", s_c, SC)
        P.wait("act", s_c, SC)

        qaw_bf = vbf(AR1, 0, 16 * 1536)
        qan_own = vbf(AR1, 12288, 12 * OWN)
        qatmp = v32(AR2, 4096, 1536)
        ln1tm = vbf(AR2, 5632, 2048)
        ln1fm = vbf(AR2, 6656, 2048)
        sqscr = v32(AR2, 7680, 512)

        P.op("pool", lambda e: e.dma_start(
            out=AR1.ap().bitcast(BF16)[:, 0:16 * 1536].rearrange(
                "p (c n) -> p c n", c=16),
            in_=q_a_w.ap().rearrange("(c p) n -> p c n", p=128)),
            inc=s_wa, inck=16)
        AW = s_wa.n

        xt_free = [None, None]
        fm_evac = 0
        qan_ev = 0
        for i in range(NT):
            sl = i % 2
            if xt_free[sl] is not None:
                P.wait("sp", *xt_free[sl])
            P.op("sp", lambda e, i=i, sl=sl: e.dma_start(
                out=AR2[:, sl * 2048:sl * 2048 + 2048],
                in_=x_own[i * 128:(i + 1) * 128, :]), inc=sx[sl], inck=16)
            x_ld = sx[sl].n
            P.wait("dve", sx[sl], x_ld)
            if i == 0:
                P.wait("dve", s_dve, consts_done)
            for j in range(4):
                P.op("dve", lambda e, sl=sl, j=j: e.bn_stats(
                    out=bnst[:, j * 6:(j + 1) * 6],
                    in_=AR2[:, sl * 2048 + j * 512:sl * 2048 + (j + 1) * 512]),
                    inc=None)
            P.drain("dve")
            P.op("dve", lambda e: e.bn_aggr(out=mvt[:, 0:2], in_=bnst[:, :]),
                 inc=s_dve)
            bn_done = s_dve.n
            P.wait("act", s_dve, bn_done)
            P.op("act", lambda e: e.activation(out=rstdt[:, 0:1], in_=mvt[:, 1:2],
                                               func=AF.Ln, bias=fc[:, 0:1]), inc=None)
            P.drain("act")
            P.op("act", lambda e: e.activation(out=rstdt[:, 1:2], in_=rstdt[:, 0:1],
                                               func=AF.Exp, scale=-0.5), inc=s_act)
            rstd_done = s_act.n
            P.wait("dve", s_act, rstd_done)
            P.op("dve", lambda e, sl=sl: e.tensor_scalar(
                out=ln1tm, in0=AR2[:, sl * 2048:sl * 2048 + 2048],
                scalar1=mvt[:, 0:1], scalar2=rstdt[:, 1:2],
                op0=ALU.subtract, op1=ALU.mult), inc=s_dve)
            ln1_done = s_dve.n
            xt_free[sl] = (s_dve, ln1_done)
            P.wait("pe", s_dve, ln1_done)
            if i > 0:
                P.wait("pe", s_act, fm_evac)
            for j in range(16):
                P.op("pe", lambda e, j=j: e.transpose(
                    (psA if j < 8 else psB)[:, (j % 8) * 128:(j % 8) * 128 + 128],
                    ln1tm[:, j * 128:(j + 1) * 128], identb[:, :]),
                    inc=(s_pe if j == 15 else None))
            tp_done = s_pe.n
            P.wait("act", s_pe, tp_done)
            P.op("act", lambda e: e.copy(out=ln1fm[:, 0:1024], in_=psA[:, :]), inc=None)
            P.drain("act")
            P.op("act", lambda e: e.copy(out=ln1fm[:, 1024:2048], in_=psB[:, :]),
                 inc=s_act)
            fm_evac = s_act.n
            P.wait("pe", s_act, fm_evac)
            if i == 0:
                P.wait("pe", s_wa, AW)
            for c in range(16):
                for sidx, pst in enumerate((ps1, ps3, ps4)):
                    P.op("pe", lambda e, c=c, sidx=sidx, pst=pst: e.matmul(
                        pst[:, :], ln1fm[:, c * 128:(c + 1) * 128],
                        qaw_bf[:, c * 1536 + sidx * 512:c * 1536 + (sidx + 1) * 512],
                        start=(c == 0), stop=(c == 15)),
                        inc=(s_pe if (c == 15 and sidx == 2) else None))
            qa_done = s_pe.n
            P.wait("act", s_pe, qa_done)
            P.op("act", lambda e: e.copy(out=qatmp[:, 0:512], in_=ps1[:, :]), inc=None)
            P.op("act", lambda e: e.copy(out=qatmp[:, 512:1024], in_=ps3[:, :]), inc=None)
            P.drain("act")
            P.op("act", lambda e: e.copy(out=qatmp[:, 1024:1536], in_=ps4[:, :]),
                 inc=s_act)
            qatmp_done = s_act.n
            P.wait("dve", s_act, qatmp_done)
            for k3 in range(3):
                P.op("dve", lambda e, k3=k3: e.scalar_tensor_tensor(
                    out=sqscr, in0=qatmp[:, k3 * 512:(k3 + 1) * 512], scalar=1.0,
                    in1=qatmp[:, k3 * 512:(k3 + 1) * 512],
                    op0=ALU.mult, op1=ALU.mult, accum_out=sstat[:, k3:k3 + 1]),
                    inc=None)
                P.drain("dve")
            P.op("dve", lambda e: e.tensor_tensor(
                out=sstat[:, 3:4], in0=sstat[:, 0:1], in1=sstat[:, 1:2],
                op=ALU.add), inc=None)
            P.drain("dve")
            P.op("dve", lambda e: e.tensor_tensor(
                out=sstat[:, 4:5], in0=sstat[:, 3:4], in1=sstat[:, 2:3],
                op=ALU.add), inc=s_dve)
            qssq_done = s_dve.n
            P.wait("act", s_dve, qssq_done)
            P.op("act", lambda e: e.activation(out=rstdt[:, 3:4], in_=sstat[:, 4:5],
                                               func=AF.Ln, scale=1.0 / QL,
                                               bias=fc[:, 1:2]), inc=None)
            P.drain("act")
            P.op("act", lambda e: e.activation(out=rstdt[:, 2:3], in_=rstdt[:, 3:4],
                                               func=AF.Exp, scale=-0.5, bias=fc[:, 2:3]),
                 inc=s_act)
            qrstd_done = s_act.n
            P.wait("dve", s_act, qrstd_done)
            P.op("dve", lambda e: e.tensor_scalar(
                out=ln1tm[:, 0:1536], in0=qatmp, scalar1=rstdt[:, 2:3],
                scalar2=None, op0=ALU.mult), inc=s_dve)
            qan_done = s_dve.n
            P.wait("pe", s_dve, qan_done)
            P.wait("pe", s_act, qan_ev)
            for j in range(12):
                P.op("pe", lambda e, j=j: e.transpose(
                    (psA if j < 8 else psB)[:, (j % 8) * 128:(j % 8) * 128 + 128],
                    ln1tm[:, j * 128:(j + 1) * 128], identb[:, :]),
                    inc=(s_pe if j == 11 else None))
            qtp_done = s_pe.n
            P.wait("act", s_pe, qtp_done)
            P.op("act", lambda e, i=i: e.copy(
                out=qan_own.rearrange("p (c t) -> p c t", c=12)[:, 0:8, i * 128:(i + 1) * 128],
                in_=psA[:, :].rearrange("p (c t) -> p c t", c=8)), inc=None)
            P.drain("act")
            P.op("act", lambda e, i=i: e.copy(
                out=qan_own.rearrange("p (c t) -> p c t", c=12)[:, 8:12, i * 128:(i + 1) * 128],
                in_=psB[:, 0:512].rearrange("p (c t) -> p c t", c=4)),
                inc=s_act)
            qan_ev = s_act.n
            fm_evac = s_act.n

        barrier()
        qbw_bf = vbf(AR2, 0, 12 * 3072)
        qtm = [vbf(AR1, s * 1536, 3072) for s in range(2)]
        ropescr = v32(AR1, 3072, 2048)
        ssq_all = v32(AR1, 5120, 256)
        cs_sl = [(v32(AR1, 5376 + s * 1024, 512), v32(AR1, 5888 + s * 1024, 512))
                 for s in range(2)]
        scs = [s_cs0, s_cs1]
        sqs = [s_qs0, s_qs1]

        P.op("pool", lambda e: e.dma_start(
            out=AR2.ap().bitcast(BF16)[:, 0:12 * 3072].rearrange(
                "p (c n) -> p c n", c=12),
            in_=q_b_w.ap().rearrange("(c p) n -> p c n", p=128)),
            inc=s_wa, inck=16)
        AW2 = s_wa.n
        P.wait("dve", s_wa, AW2)
        for c in range(12):
            P.op("dve", lambda e, c=c: e.tensor_scalar(
                out=qbw_bf[:, c * 3072:(c + 1) * 3072],
                in0=qbw_bf[:, c * 3072:(c + 1) * 3072],
                scalar1=wqc_sb[:, c:c + 1], scalar2=None, op0=ALU.mult),
                inc=(s_dve if c == 11 else None))
        qbw_ready = s_dve.n

        psq4 = [g3.ap()[:, 0:384], g4.ap()[:, 0:384],
                g5.ap()[:, 0:384], g6.ap()[:, 0:384]]
        qtm_free = [None, None]
        cs_free = [None, None]
        qev_done = 0
        for tq in range(NT):
            sl = tq % 2
            if cs_free[sl] is not None:
                P.wait("sp", *cs_free[sl])
            P.op("sp", lambda e, tq=tq, sl=sl: e.dma_start(
                out=cs_sl[sl][0][:, :],
                in_=cos16_own[tq * 128:(tq + 1) * 128, :]), inc=scs[sl], inck=16)
            P.op("sp", lambda e, tq=tq, sl=sl: e.dma_start(
                out=cs_sl[sl][1][:, :],
                in_=sin16_own[tq * 128:(tq + 1) * 128, :]), inc=scs[sl], inck=16)
            cs_ld = scs[sl].n
            if tq == 0:
                P.wait("pe", s_dve, qbw_ready)
            P.wait("pe", s_act, qev_done)
            if qtm_free[sl] is not None:
                P.wait("pe", *qtm_free[sl])
            for half in range(2):
                if half == 1:
                    P.wait("pe", s_act, s_act.n)
                for kv4 in range(4):
                    kv8 = half * 4 + kv4
                    for l in range(12):
                        P.op("pe", lambda e, kv4=kv4, kv8=kv8, l=l, tq=tq: e.matmul(
                            psq4[kv4],
                            qan_own[:, l * OWN + tq * 128:l * OWN + (tq + 1) * 128],
                            qbw_bf[:, l * 3072 + kv8 * 384:l * 3072 + (kv8 + 1) * 384],
                            start=(l == 0), stop=(l == 11)),
                            inc=(s_pe if (kv4 == 3 and l == 11) else None))
                qb_done = s_pe.n
                P.wait("act", s_pe, qb_done)
                if half == 0:
                    P.wait("act", s_act, qev_done)
                for kv4 in range(4):
                    kv8 = half * 4 + kv4
                    if kv4 == 3:
                        P.drain("act")
                    P.op("act", lambda e, kv4=kv4, kv8=kv8, sl=sl: e.copy(
                        out=qtm[sl][:, kv8 * 384:(kv8 + 1) * 384], in_=psq4[kv4]),
                        inc=(s_act if kv4 == 3 else None))
            qev_done = s_act.n
            P.wait("dve", s_act, qev_done)
            P.wait("dve", scs[sl], cs_ld)
            qa16 = qtm[sl].rearrange("p (h d) -> p h d", h=16)
            qe = qa16[:, :, 128:192:2]
            qo = qa16[:, :, 129:192:2]
            ct = cs_sl[sl][0].rearrange("p (h d) -> p h d", h=16)
            st_ = cs_sl[sl][1].rearrange("p (h d) -> p h d", h=16)
            P.op("dve", lambda e, qe=qe, ct=ct: e.tensor_tensor(
                out=AR1[:, 3072:3584].rearrange("p (h d) -> p h d", h=16),
                in0=qe, in1=ct, op=ALU.mult), inc=None)
            P.op("dve", lambda e, qo=qo, st_=st_: e.tensor_tensor(
                out=AR1[:, 3584:4096].rearrange("p (h d) -> p h d", h=16),
                in0=qo, in1=st_, op=ALU.mult), inc=None)
            P.op("dve", lambda e, qo=qo, ct=ct: e.tensor_tensor(
                out=AR1[:, 4096:4608].rearrange("p (h d) -> p h d", h=16),
                in0=qo, in1=ct, op=ALU.mult), inc=None)
            P.op("dve", lambda e, qe=qe, st_=st_: e.tensor_tensor(
                out=AR1[:, 4608:5120].rearrange("p (h d) -> p h d", h=16),
                in0=qe, in1=st_, op=ALU.mult), inc=None)
            P.drain("dve")
            P.op("dve", lambda e, qe=qe: e.tensor_tensor(
                out=qe,
                in0=AR1[:, 3072:3584].rearrange("p (h d) -> p h d", h=16),
                in1=AR1[:, 3584:4096].rearrange("p (h d) -> p h d", h=16),
                op=ALU.subtract), inc=None)
            P.op("dve", lambda e, qo=qo: e.tensor_tensor(
                out=qo,
                in0=AR1[:, 4096:4608].rearrange("p (h d) -> p h d", h=16),
                in1=AR1[:, 4608:5120].rearrange("p (h d) -> p h d", h=16),
                op=ALU.add), inc=None)
            P.drain("dve")
            for hh in range(16):
                P.op("dve", lambda e, hh=hh, sl=sl, tq=tq: e.scalar_tensor_tensor(
                    out=ropescr[:, (hh % 4) * 512:(hh % 4) * 512 + 192],
                    in0=qtm[sl][:, hh * 192:(hh + 1) * 192], scalar=1.0,
                    in1=qtm[sl][:, hh * 192:(hh + 1) * 192],
                    op0=ALU.mult, op1=ALU.mult,
                    accum_out=ssq_all[:, tq * 16 + hh:tq * 16 + hh + 1]),
                    inc=(s_dve if hh == 15 else None))
                if hh % 4 == 3:
                    P.drain("dve")
            rope_done = s_dve.n
            cs_free[sl] = (s_dve, rope_done)
            P.wait("sp", s_dve, rope_done)
            P.op("sp", lambda e, tq=tq, sl=sl: e.dma_start(
                out=qspill[tq * 128:(tq + 1) * 128, :], in_=qtm[sl][:, :]),
                inc=sqs[sl], inck=16)
            qtm_free[sl] = (sqs[sl], sqs[sl].n)

        a2d = s_dve.n
        P.wait("dve", s_dve, a2d)
        for hh in range(16):
            P.op("dve", lambda e, hh=hh: e.reduce_max(
                out=colscr[:, 64 + hh:65 + hh],
                in_=ssq_all[:, 0:NT * 16].rearrange(
                    "p (t h) -> p h t", h=16)[:, hh, :],
                axis=AX.X), inc=(s_dve if hh == 15 else None))
        mq1 = s_dve.n
        P.wait("pe", s_dve, mq1)
        P.op("pe", lambda e: e.transpose(g5[0:16, 0:128], colscr[:, 64:80],
                                         identf[:, :]), inc=s_pe)
        mq2 = s_pe.n
        P.wait("dve", s_pe, mq2)
        P.op("dve", lambda e: e.reduce_max(out=colscr[0:16, 80:81],
                                           in_=g5.ap()[0:16, 0:128],
                                           axis=AX.X), inc=s_dve)
        mq3 = s_dve.n
        P.wait("pe", s_dve, mq3)
        P.op("pe", lambda e: e.transpose(g6[0:1, 0:16], colscr[0:16, 80:81],
                                         identf[0:16, 0:16]), inc=s_pe)
        mq4 = s_pe.n
        P.wait("act", s_pe, mq4)
        P.op("act", lambda e: e.activation(out=rowscr[0:1, 0:16],
                                           in_=g6.ap()[0:1, 0:16],
                                           func=AF.Ln, bias=fc[0:1, 3:4]), inc=None)
        P.drain("act")
        P.op("act", lambda e: e.activation(out=rowscr[0:1, 16:32],
                                           in_=rowscr[0:1, 0:16],
                                           func=AF.Exp, scale=0.5), inc=s_act)
        maxq_done = s_act.n

        barrier()
        kvcfm = [vbf(AR2, c * (T // 2), T) for c in range(4)]
        kropeF = vbf(AR2, 2 * T, T)
        kvaw_bf = vbf(AR1, 0, 16 * 576)
        kvtmp = v32(AR1, 8704, 512)
        ln1tmK = vbf(AR1, 9216, 2048)
        ln1fmK = vbf(AR1, 10240, 2048)
        kvc_tm = vbf(AR1, 11264, 512)
        krope_tm = vbf(AR1, 11520, 64)
        sqscrK = v32(AR1, 11552, 512)
        ropescrK = v32(AR1, 12064, 128)
        cosb = v32(AR1, 12288, NKT * 32)
        sinb = v32(AR1, 12288 + NKT * 32, NKT * 32)

        P.op("pool", lambda e: e.dma_start(
            out=AR1.ap().bitcast(BF16)[:, 0:16 * 576].rearrange(
                "p (c n) -> p c n", c=16),
            in_=kv_a_w.ap().rearrange("(c p) n -> p c n", p=128)),
            inc=s_wa, inck=16)
        AW3 = s_wa.n
        P.op("sp", lambda e: e.dma_start(
            out=cosb.rearrange("p (n c) -> p n c", c=32),
            in_=cos_all.ap().rearrange("(n p) c -> p n c", p=128)),
            inc=s_kv, inck=16)
        P.op("sp", lambda e: e.dma_start(
            out=sinb.rearrange("p (n c) -> p n c", c=32),
            in_=sin_all.ap().rearrange("(n p) c -> p n c", p=128)),
            inc=s_kv, inck=16)
        CS_LD = s_kv.n

        xt_free = [None, None]
        fm_evacK = 0
        kvst_done = 0
        for i in range(NKT):
            sl = i % 2
            if xt_free[sl] is not None:
                P.wait("sp", *xt_free[sl])
            P.op("sp", lambda e, i=i, sl=sl: e.dma_start(
                out=AR1[:, 4608 + sl * 2048:4608 + sl * 2048 + 2048],
                in_=x_batch[i * 128:(i + 1) * 128, :]), inc=sx[sl], inck=16)
            x_ld = sx[sl].n
            P.wait("dve", sx[sl], x_ld)
            for j in range(4):
                P.op("dve", lambda e, sl=sl, j=j: e.bn_stats(
                    out=bnst[:, j * 6:(j + 1) * 6],
                    in_=AR1[:, 4608 + sl * 2048 + j * 512:4608 + sl * 2048 + (j + 1) * 512]),
                    inc=None)
            P.drain("dve")
            P.op("dve", lambda e: e.bn_aggr(out=mvt[:, 0:2], in_=bnst[:, :]),
                 inc=s_dve)
            bn_done = s_dve.n
            P.wait("act", s_dve, bn_done)
            P.op("act", lambda e: e.activation(out=rstdt[:, 0:1], in_=mvt[:, 1:2],
                                               func=AF.Ln, bias=fc[:, 0:1]), inc=None)
            P.drain("act")
            P.op("act", lambda e: e.activation(out=rstdt[:, 1:2], in_=rstdt[:, 0:1],
                                               func=AF.Exp, scale=-0.5), inc=s_act)
            rstd_done = s_act.n
            P.wait("dve", s_act, rstd_done)
            P.op("dve", lambda e, sl=sl: e.tensor_scalar(
                out=ln1tmK, in0=AR1[:, 4608 + sl * 2048:4608 + sl * 2048 + 2048],
                scalar1=mvt[:, 0:1], scalar2=rstdt[:, 1:2],
                op0=ALU.subtract, op1=ALU.mult), inc=s_dve)
            ln1_done = s_dve.n
            xt_free[sl] = (s_dve, ln1_done)
            P.wait("pe", s_dve, ln1_done)
            if i > 0:
                P.wait("pe", s_act, fm_evacK)
            for j in range(16):
                P.op("pe", lambda e, j=j: e.transpose(
                    (psA if j < 8 else psB)[:, (j % 8) * 128:(j % 8) * 128 + 128],
                    ln1tmK[:, j * 128:(j + 1) * 128], identb[:, :]),
                    inc=(s_pe if j == 15 else None))
            tp_done = s_pe.n
            P.wait("act", s_pe, tp_done)
            P.op("act", lambda e: e.copy(out=ln1fmK[:, 0:1024], in_=psA[:, :]), inc=None)
            P.drain("act")
            P.op("act", lambda e: e.copy(out=ln1fmK[:, 1024:2048], in_=psB[:, :]),
                 inc=s_act)
            fm_evacK = s_act.n
            P.wait("pe", s_act, fm_evacK)
            if i == 0:
                P.wait("pe", s_wa, AW3)
            for c in range(16):
                P.op("pe", lambda e, c=c: e.matmul(
                    ps1[:, :], ln1fmK[:, c * 128:(c + 1) * 128],
                    kvaw_bf[:, c * 576:c * 576 + 512],
                    start=(c == 0), stop=(c == 15)), inc=None)
            for c in range(16):
                P.op("pe", lambda e, c=c: e.matmul(
                    ps2[:, 0:64], ln1fmK[:, c * 128:(c + 1) * 128],
                    kvaw_bf[:, c * 576 + 512:c * 576 + 576],
                    start=(c == 0), stop=(c == 15)),
                    inc=(s_pe if c == 15 else None))
            kva_done = s_pe.n
            P.wait("act", s_pe, kva_done)
            P.op("act", lambda e: e.copy(out=kvtmp, in_=ps1[:, :]), inc=s_act)
            kvtmp_done = s_act.n
            P.wait("dve", s_act, kvtmp_done)
            P.op("dve", lambda e: e.scalar_tensor_tensor(
                out=sqscrK, in0=kvtmp, scalar=1.0, in1=kvtmp,
                op0=ALU.mult, op1=ALU.mult, accum_out=rstdt[:, 2:3]), inc=s_dve)
            kvssq_done = s_dve.n
            P.wait("act", s_dve, kvssq_done)
            P.op("act", lambda e: e.activation(out=rstdt[:, 3:4], in_=rstdt[:, 2:3],
                                               func=AF.Ln, scale=1.0 / KVL,
                                               bias=fc[:, 1:2]), inc=None)
            P.drain("act")
            P.op("act", lambda e: e.activation(out=rstdt[:, 2:3], in_=rstdt[:, 3:4],
                                               func=AF.Exp, scale=-0.5), inc=s_act)
            kvrstd_done = s_act.n
            P.wait("dve", s_act, kvrstd_done)
            P.op("dve", lambda e: e.tensor_scalar(
                out=kvc_tm, in0=kvtmp, scalar1=rstdt[:, 2:3], scalar2=None,
                op0=ALU.mult), inc=s_dve)
            kvc_done = s_dve.n
            if i == 0:
                P.wait("dve", s_kv, CS_LD)
            ctK = cosb[:, i * 32:(i + 1) * 32]
            stK = sinb[:, i * 32:(i + 1) * 32]
            ke = ps2[:, 0:64:2]
            ko = ps2[:, 1:64:2]
            P.op("dve", lambda e, ke=ke, ctK=ctK: e.tensor_tensor(
                out=ropescrK[:, 0:32], in0=ke, in1=ctK, op=ALU.mult), inc=None)
            P.op("dve", lambda e, ko=ko, stK=stK: e.tensor_tensor(
                out=ropescrK[:, 32:64], in0=ko, in1=stK, op=ALU.mult), inc=None)
            P.drain("dve")
            P.op("dve", lambda e: e.tensor_tensor(
                out=krope_tm[:, 0:64:2], in0=ropescrK[:, 0:32],
                in1=ropescrK[:, 32:64], op=ALU.subtract), inc=None)
            P.op("dve", lambda e, ko=ko, ctK=ctK: e.tensor_tensor(
                out=ropescrK[:, 64:96], in0=ko, in1=ctK, op=ALU.mult), inc=None)
            P.op("dve", lambda e, ke=ke, stK=stK: e.tensor_tensor(
                out=ropescrK[:, 96:128], in0=ke, in1=stK, op=ALU.mult), inc=None)
            P.drain("dve")
            P.op("dve", lambda e: e.tensor_tensor(
                out=krope_tm[:, 1:64:2], in0=ropescrK[:, 64:96],
                in1=ropescrK[:, 96:128], op=ALU.add), inc=s_dve)
            krope_done = s_dve.n
            P.wait("pe", s_dve, krope_done)
            if i > 0:
                P.wait("pe", s_act, kvst_done)
            for j in range(4):
                P.op("pe", lambda e, j=j: e.transpose(
                    psA[:, j * 128:(j + 1) * 128], kvc_tm[:, j * 128:(j + 1) * 128],
                    identb[:, :]), inc=None)
            P.op("pe", lambda e: e.transpose(
                psA[0:64, 512:640], krope_tm[:, 0:64], identb[:, :]),
                inc=s_pe)
            kvtp_done = s_pe.n
            P.wait("act", s_pe, kvtp_done)
            for cc in range(4):
                P.op("act", lambda e, cc=cc, i=i: e.copy(
                    out=kvcfm[cc][:, i * 128:(i + 1) * 128],
                    in_=psA[:, cc * 128:(cc + 1) * 128]), inc=None)
            P.drain("act")
            P.op("act", lambda e, i=i: e.copy(
                out=kropeF[0:64, i * 128:(i + 1) * 128],
                in_=psA[0:64, 512:640]), inc=s_act)
            kvst_done = s_act.n

        barrier()
        V2 = vbf(AR1, 0, NKT * 256)
        K_h = [vbf(AR1, 8192 + s * 4096, T) for s in range(2)]
        kvb2f = vbf(AR1, 16384, 4 * 512)
        masksl = vbf(AR1, 17408, 8 * 512)
        qtw = [vbf(AR1, 19456 + s * 384, 768) for s in range(2)]
        Qn_w = [vbf(AR1, 20224 + s * 256, 512) for s in range(2)]
        Qr_w = [vbf(AR1, 20736 + s * 256, 512) for s in range(2)]
        expS = [vbf(AR1, 21248 + s * 512, 1024) for s in range(2)]
        bcastf = v32(AR1, 22272, 512)
        recf = bcastf
        dacc = [v32(AR1, 22784 + s * 512, 512) for s in range(2)]
        outh_sl = [vbf(AR1, 23808 + s * 256, 512) for s in range(2)]
        sqr = [s_qr0, s_qr1]
        assert 23808 + 512 <= 24576

        SQRT_QK = math.sqrt(float(QK))
        exp_bank_free = [0, 0]
        v2_evac = 0
        kh_evac = 0
        g5_free = 0
        g12_expfree = [0, 0]
        expS_free = [0, 0]
        qtw_free = [None, None]
        qw_tp_ev = 0
        mask_used = 0
        kvb_fold_done = 0
        kvb_pe_used = 0
        outh_free = [None, None]
        psO_free = [0, 0]
        g3_ev = 0
        for hp in range(8):
            P.wait("pool", s_kb, s_kb.n)
            P.wait("pool", s_dve, kvb_fold_done)
            P.wait("pool", s_pe, kvb_pe_used)
            P.op("pool", lambda e, hp=hp: e.dma_start(
                out=kvb2f.rearrange("p (c n) -> p c n", c=4),
                in_=kv_b_w.ap().rearrange("(c p) n -> p c n", p=128)[
                    :, :, hp * 512:(hp + 1) * 512]), inc=s_kb, inck=16)
            KB = s_kb.n
            P.wait("dve", s_kb, KB)
            for cc in range(4):
                P.op("dve", lambda e, cc=cc: e.tensor_scalar(
                    out=kvb2f[:, cc * 512:(cc + 1) * 512],
                    in0=kvb2f[:, cc * 512:(cc + 1) * 512],
                    scalar1=wkvc_sb[:, cc:cc + 1], scalar2=None, op0=ALU.mult),
                    inc=(s_dve if cc == 3 else None))
            kvb_fold_done = s_dve.n
            P.wait("pe", s_dve, kvb_fold_done)
            P.wait("pe", s_dve, g5_free)
            for kt in range(NKT):
                psV = (g3, g4)[kt % 2]
                P.wait("pe", s_act, exp_bank_free[kt % 2])
                for cc in range(4):
                    P.op("pe", lambda e, kt=kt, cc=cc, psV=psV: e.matmul(
                        psV[:, 0:256],
                        kvcfm[cc][:, kt * 128:(kt + 1) * 128],
                        kvb2f[:, cc * 512:(cc + 1) * 512].rearrange(
                            "p (h z) -> p h z", h=2)[:, :, 128:256],
                        start=(cc == 0), stop=(cc == 3)),
                        inc=(s_pe if cc == 3 else None))
                vkt = s_pe.n
                P.wait("act", s_pe, vkt)
                P.op("act", lambda e, kt=kt, psV=psV: e.copy(
                    out=V2[:, kt * 256:(kt + 1) * 256], in_=psV.ap()[:, 0:256]),
                    inc=s_act)
                v2_evac = s_act.n
                exp_bank_free[kt % 2] = s_act.n
            for hl in range(2):
                for win in range(T // 512):
                    psK = (g3, g4)[win % 2]
                    P.wait("pe", s_act, exp_bank_free[win % 2])
                    for cc in range(4):
                        P.op("pe", lambda e, win=win, cc=cc, psK=psK, hl=hl: e.matmul(
                            psK[:, :],
                            kvb2f[:, cc * 512 + hl * 256:cc * 512 + hl * 256 + 128],
                            kvcfm[cc][:, win * 512:(win + 1) * 512],
                            start=(cc == 0), stop=(cc == 3)),
                            inc=(s_pe if cc == 3 else None))
                    kkt = s_pe.n
                    P.wait("act", s_pe, kkt)
                    P.op("act", lambda e, win=win, psK=psK, hl=hl: e.copy(
                        out=K_h[hl][:, win * 512:(win + 1) * 512], in_=psK.ap()),
                        inc=s_act)
                    kh_evac = s_act.n
                    exp_bank_free[win % 2] = s_act.n
            kvb_pe_used = s_pe.n
            P.wait("dve", s_act, kh_evac)
            if hp == 0:
                P.op("dve", lambda e: e.memset(colscr[:, 8:9], 0.0), inc=None)
                P.drain("dve")
                P.op("dve", lambda e: e.reduce_max(
                    out=colscr[0:64, 8:9], in_=kropeF[0:64, :], axis=AX.X,
                    apply_absolute_value=True), inc=None)
                P.drain("dve")
                P.wait("dve", s_act, maxq_done)
            for hl in range(2):
                h = hp * 2 + hl
                P.op("dve", lambda e, hl=hl: e.reduce_max(
                    out=colscr[:, 9:10], in_=K_h[hl][:, :], axis=AX.X,
                    apply_absolute_value=True), inc=None)
                P.drain("dve")
                P.op("dve", lambda e: e.tensor_tensor(
                    out=colscr[:, 10:11], in0=colscr[:, 9:10],
                    in1=colscr[:, 8:9], op=ALU.max), inc=s_dve)
                mk1 = s_dve.n
                P.wait("pe", s_dve, mk1)
                P.wait("pe", s_act, s_act.n)
                P.op("pe", lambda e: e.transpose(
                    g6[0:1, 0:128], colscr[:, 10:11], identf[:, :]), inc=s_pe)
                mk2 = s_pe.n
                P.wait("dve", s_pe, mk2)
                P.op("dve", lambda e: e.reduce_max(
                    out=rowscr[0:1, 40:41], in_=g6.ap()[0:1, 0:128], axis=AX.X),
                    inc=None)
                P.drain("dve")
                P.op("dve", lambda e, h=h: e.tensor_tensor(
                    out=rowscr[0:1, 41:42], in0=rowscr[0:1, 16 + h:17 + h],
                    in1=rowscr[0:1, 40:41], op=ALU.mult), inc=None)
                P.drain("dve")
                P.op("dve", lambda e, hl=hl: e.tensor_scalar(
                    out=nb_bf[0:1, hl:hl + 1], in0=rowscr[0:1, 41:42],
                    scalar1=-SQRT_QK, scalar2=None, op0=ALU.mult), inc=s_dve)
                negc_done = s_dve.n
                P.wait("pe", s_dve, negc_done)
                P.wait("pe", s_dve, psO_free[1])
                P.op("pe", lambda e, hl=hl: e.matmul(
                    g6[:, 0:1], onesb[0:1, 0:128], nb_bf[0:1, hl:hl + 1],
                    start=True, stop=True), inc=s_pe)
                negb_mm = s_pe.n
                P.wait("act", s_pe, negb_mm)
                P.op("act", lambda e, hl=hl: e.copy(out=colscr[:, 12 + hl:13 + hl],
                                                    in_=g6.ap()[:, 0:1]), inc=s_act)
            negc_bc = s_act.n

            for w in range(NCH):
                lim = LIMS[w]
                kmax2 = lim // 2
                mstart = lim - 16
                for hl in range(2):
                    h = hp * 2 + hl
                    sl = hl
                    if qtw_free[sl] is not None:
                        P.wait("sp", *qtw_free[sl])
                    P.op("sp", lambda e, h=h, w=w, sl=sl: e.dma_start(
                        out=qtw[sl].rearrange("p (tt d) -> p tt d", tt=4),
                        in_=qspill[w * 512:(w + 1) * 512,
                                   h * 192:(h + 1) * 192].rearrange(
                            "(tt p) d -> p tt d", p=128)), inc=sqr[sl], inck=16)
                    q_ld = sqr[sl].n
                    P.wait("pe", sqr[sl], q_ld)
                    P.wait("pe", s_act, qw_tp_ev)
                    if w == 0 and hl == 0:
                        P.wait("pe", s_act, max(kh_evac, v2_evac, negc_bc))
                        P.wait("pe", s_dve, g5_free)
                    for tt in range(4):
                        P.op("pe", lambda e, tt=tt, sl=sl: e.transpose(
                            psA[:, tt * 128:(tt + 1) * 128],
                            qtw[sl][:, tt * 192:tt * 192 + 128], identb[:, :]),
                            inc=None)
                    for tt in range(4):
                        P.op("pe", lambda e, tt=tt, sl=sl: e.transpose(
                            psB[0:64, tt * 128:(tt + 1) * 128],
                            qtw[sl][:, tt * 192 + 128:(tt + 1) * 192], identb[:, :]),
                            inc=(s_pe if tt == 3 else None))
                    qtp2 = s_pe.n
                    qtw_free[sl] = (s_pe, qtp2)
                    P.wait("act", s_pe, qtp2)
                    P.op("act", lambda e, hl=hl: e.copy(out=Qn_w[hl][:, :],
                                                        in_=psA[:, 0:512]),
                         inc=None)
                    P.drain("act")
                    P.op("act", lambda e, hl=hl: e.copy(out=Qr_w[hl][0:64, :],
                                                        in_=psB[0:64, 0:512]),
                         inc=s_act)
                    qw_tp_ev = s_act.n

                P.wait("pe", s_act, qw_tp_ev)
                for dw in range(16):
                    P.op("pe", lambda e: e.matmul(
                        g3[:, :], onesb[:, :], hsu_sb[:, 0:512],
                        start=True, stop=True),
                        inc=(s_pe if dw == 15 else None))
                warm_done = s_pe.n
                mk_ld = [0, 0]
                att_scores = {}
                att_expd = {}
                att_maskd = {}
                att_dacc = {}
                psS_h = (g1, g2)
                psO_h = (g5, g6)
                mhalf_loaded = [False, False]

                def load_mask_half(mh, w=w):
                    P.wait("sp", s_dve, s_dve.n)
                    P.op("sp", lambda e, w=w, mh=mh: e.dma_start(
                        out=masksl[:, :],
                        in_=masks_in.ap()[:, w * 8192 + mh * 4096:
                                          w * 8192 + (mh + 1) * 4096]),
                        inc=s_mk, inck=16)
                    mk_ld[mh] = s_mk.n

                def emit_attnv(hl, j, kmax2=kmax2):
                    first = (j == 0)
                    last = (j == kmax2 - 1)
                    P.wait("pe", s_act, att_expd[(hl, j)])
                    if first:
                        P.wait("pe", s_dve, psO_free[hl])
                    if (hl, j) in att_maskd:
                        P.wait("pe", s_dve, att_maskd[(hl, j)])
                    for sub in range(2):
                        kt = 2 * j + sub
                        P.op("pe", lambda e, kt=kt, hl=hl, sub=sub,
                             first=first, last=last: e.matmul(
                            psO_h[hl][:, :],
                            V2[:, kt * 256 + hl * 128:kt * 256 + (hl + 1) * 128],
                            expS[hl][:, sub * 512:(sub + 1) * 512],
                            start=(first and sub == 0),
                            stop=(last and sub == 1)),
                            inc=(s_pe if sub == 1 else None))
                    expS_free[hl] = s_pe.n

                for kg in range(kmax2):
                    need_mh0 = (2 * kg + 1 >= mstart) and not mhalf_loaded[0]
                    need_mh1 = (2 * kg + 1 >= mstart + 8) and not mhalf_loaded[1]
                    if need_mh0:
                        load_mask_half(0)
                        mhalf_loaded[0] = True
                    if need_mh1:
                        load_mask_half(1)
                        mhalf_loaded[1] = True
                    for hl in range(2):
                        psS = psS_h[hl]
                        if kg >= 1:
                            emit_attnv(hl, kg - 1)
                        P.wait("pe", s_act, g12_expfree[hl])
                        if kg == 0:
                            P.wait("pe", s_act, qw_tp_ev)
                        for sub in range(2):
                            kt = 2 * kg + sub
                            P.op("pe", lambda e, kt=kt, psS=psS, sub=sub, hl=hl: e.matmul(
                                psS[:, sub * 512:(sub + 1) * 512],
                                K_h[hl][:, kt * 128:(kt + 1) * 128], Qn_w[hl][:, :],
                                start=True, stop=False), inc=None)
                            P.op("pe", lambda e, kt=kt, psS=psS, sub=sub, hl=hl: e.matmul(
                                psS[:, sub * 512:(sub + 1) * 512],
                                kropeF[0:64, kt * 128:(kt + 1) * 128],
                                Qr_w[hl][0:64, :],
                                start=False, stop=True),
                                inc=(s_pe if sub == 1 else None))
                        att_scores[(hl, kg)] = s_pe.n
                        P.wait("act", s_pe, att_scores[(hl, kg)])
                        P.wait("act", s_pe, expS_free[hl])
                        if (hl, kg - 1) in att_dacc:
                            P.wait("act", s_dve, att_dacc[(hl, kg - 1)])
                        P.op("act", lambda e, hl=hl, psS=psS: e.activation(
                            out=expS[hl][:, :], in_=psS[:, :], func=AF.Exp,
                            bias=colscr[:, 12 + hl:13 + hl]), inc=s_act)
                        att_expd[(hl, kg)] = s_act.n
                        g12_expfree[hl] = s_act.n
                        diag_subs = [sub for sub in range(2)
                                     if 2 * kg + sub >= mstart]
                        P.wait("dve", s_act, att_expd[(hl, kg)])
                        if diag_subs:
                            for sub in diag_subs:
                                kt = 2 * kg + sub
                                pat = kt - mstart
                                P.wait("dve", s_mk, mk_ld[pat // 8])
                                P.op("dve", lambda e, hl=hl, sub=sub, pat=pat: e.tensor_tensor(
                                    out=expS[hl][:, sub * 512:(sub + 1) * 512],
                                    in0=expS[hl][:, sub * 512:(sub + 1) * 512],
                                    in1=masksl[:, (pat % 8) * 512:(pat % 8) * 512 + 512],
                                    op=ALU.mult),
                                    inc=s_dve)
                            att_maskd[(hl, kg)] = s_dve.n
                            P.drain("dve")
                        if kg == 0:
                            P.op("dve", lambda e, hl=hl: e.tensor_copy(
                                out=dacc[hl][:, :], in_=expS[hl][:, 0:512]),
                                inc=None)
                        else:
                            P.op("dve", lambda e, hl=hl: e.tensor_tensor(
                                out=dacc[hl][:, :], in0=dacc[hl][:, :],
                                in1=expS[hl][:, 0:512], op=ALU.add), inc=None)
                        P.drain("dve")
                        P.op("dve", lambda e, hl=hl: e.tensor_tensor(
                            out=dacc[hl][:, :], in0=dacc[hl][:, :],
                            in1=expS[hl][:, 512:1024], op=ALU.add), inc=s_dve)
                        P.drain("dve")
                        att_dacc[(hl, kg)] = s_dve.n
                mask_used = s_dve.n
                for hl in range(2):
                    emit_attnv(hl, kmax2 - 1)
                for hl in range(2):
                    h = hp * 2 + hl
                    attv_done = expS_free[hl]
                    P.wait("pe", s_dve, att_dacc[(hl, kmax2 - 1)])
                    P.wait("pe", s_act, qw_tp_ev)
                    P.wait("pe", s_act, g3_ev)
                    P.op("pe", lambda e, hl=hl: e.matmul(
                        g3[0:1, 0:512], onesf[:, 0:1], dacc[hl][:, :],
                        start=True, stop=True), inc=s_pe)
                    dn_red = s_pe.n
                    P.wait("dve", s_pe, dn_red)
                    P.drain("dve")
                    P.op("dve", lambda e: e.reciprocal(
                        out=recf[0:1, :], in_=g3.ap()[0:1, 0:512]), inc=s_dve)
                    P.drain("dve")
                    P.op("dve", lambda e: e.tensor_copy(
                        out=recb[0:1, :], in_=recf[0:1, :]), inc=s_dve)
                    recb_done = s_dve.n
                    P.wait("pe", s_dve, recb_done)
                    P.op("pe", lambda e: e.matmul(
                        g3[:, 0:512], onesb[0:1, 0:128],
                        recb[0:1, :], start=True, stop=True), inc=s_pe)
                    bc_mm = s_pe.n
                    P.wait("act", s_pe, bc_mm)
                    P.op("act", lambda e: e.copy(out=bcastf[:, :],
                                                 in_=g3.ap()[:, 0:512]), inc=s_act)
                    bc_ev = s_act.n
                    g3_ev = s_act.n
                    P.wait("dve", s_act, bc_ev)
                    P.wait("dve", s_pe, attv_done)
                    if outh_free[hl] is not None:
                        P.wait("dve", *outh_free[hl])
                    P.op("dve", lambda e, hl=hl: e.tensor_tensor(
                        out=outh_sl[hl][:, :], in0=psO_h[hl].ap()[:, :],
                        in1=bcastf[:, :], op=ALU.mult), inc=s_dve)
                    g5_free = s_dve.n
                    psO_free[hl] = s_dve.n
                    P.wait("sp", s_dve, g5_free)
                    sc2h = (s_c20, s_c21)[hl]
                    P.op("sp", lambda e, h=h, w=w, hl=hl: e.dma_start(
                        out=ospill[h * 128:(h + 1) * 128, w * 512:(w + 1) * 512],
                        in_=outh_sl[hl][:, :]), inc=sc2h, inck=16)
                    outh_free[hl] = (sc2h, sc2h.n)
                qw_tp_ev = s_act.n

        barrier()
        oww = vbf(AR1, 0, 16 * 2048)
        ohw = [vbf(AR1, 16384 + s * 2048, 16 * 256) for s in range(2)]
        aof32 = v32(AR2, 12288, 4096)
        xb2 = [v32(AR2, s * 2048, 2048) for s in range(2)]
        atm = v32(AR2, 4096, 2048)
        lnout = v32(AR2, 6144, 2048)
        x2t = [v32(AR2, 8192 + s * 2048, 2048) for s in range(2)]
        soh = [s_oh0, s_oh1]
        sx2 = [s_x2r0, s_x2r1]

        P.op("pool", lambda e: e.dma_start(
            out=oww.rearrange("p (c n) -> p c n", c=16),
            in_=o_w.ap().rearrange("(c p) n -> p c n", p=128)),
            inc=s_ow, inck=16)
        OW_LD = s_ow.n
        def psdc(dc):
            if dc < 4:
                return g1, (dc % 4) * 256
            if dc < 8:
                return g2, (dc % 4) * 256
            return (g3, g4, g5, g6)[(dc - 8) // 2], ((dc - 8) % 2) * 256

        ohw_free = [None, None]
        x2t_free = [None, None]
        xb2_free = [None, None]
        b2_ev = 0
        for w in range(OWN // 256):
            sl = w % 2
            if ohw_free[sl] is not None:
                P.wait("sp", *ohw_free[sl])
            P.op("sp", lambda e, w=w, sl=sl: e.dma_start(
                out=ohw[sl].rearrange("p (hh t) -> p hh t", hh=16),
                in_=ospill[:, :].rearrange("(hh p) t -> p hh t", p=128)[
                    :, :, w * 256:(w + 1) * 256]), inc=soh[sl], inck=16)
            oh_ld = soh[sl].n
            P.wait("pe", soh[sl], oh_ld)
            if w == 0:
                P.wait("pe", s_ow, OW_LD)
            else:
                P.wait("pe", s_act, b2_ev)
            for dc in range(16):
                pst, col = psdc(dc)
                for hh in range(16):
                    P.op("pe", lambda e, dc=dc, hh=hh, pst=pst, col=col, sl=sl: e.matmul(
                        pst[:, col:col + 256],
                        oww[:, hh * 2048 + dc * 128:hh * 2048 + (dc + 1) * 128],
                        ohw[sl][:, hh * 256:(hh + 1) * 256],
                        start=(hh == 0), stop=(hh == 15)),
                        inc=(s_pe if (dc == 15 and hh == 15) else None))
            ow_done = s_pe.n
            ohw_free[sl] = (s_pe, ow_done)
            P.wait("act", s_pe, ow_done)
            P.op("act", lambda e: e.copy(out=aof32[:, 0:1024], in_=g1[:, :]), inc=None)
            P.op("act", lambda e: e.copy(out=aof32[:, 1024:2048], in_=g2[:, :]), inc=None)
            P.op("act", lambda e: e.copy(out=aof32[:, 2048:2560], in_=g3[:, :]), inc=None)
            P.op("act", lambda e: e.copy(out=aof32[:, 2560:3072], in_=g4[:, :]), inc=None)
            P.op("act", lambda e: e.copy(out=aof32[:, 3072:3584], in_=g5[:, :]), inc=None)
            P.drain("act")
            P.op("act", lambda e: e.copy(out=aof32[:, 3584:4096], in_=g6[:, :]),
                 inc=s_act)
            b2_ev = s_act.n
            for tau in range(2):
                i_own = w * 2 + tau
                if xb2_free[sl] is not None and tau == 0:
                    P.wait("sp", *xb2_free[sl])
                if tau == 0:
                    P.op("sp", lambda e, w=w, sl=sl: e.dma_start(
                        out=AR2[:, sl * 2048:sl * 2048 + 2048],
                        in_=x_own[w * 256:w * 256 + 128, :]), inc=sx[0] if False else sx2[sl], inck=16)
                else:
                    P.op("sp", lambda e, w=w, sl=sl: e.dma_start(
                        out=AR2[:, sl * 2048:sl * 2048 + 2048],
                        in_=x_own[w * 256 + 128:w * 256 + 256, :]), inc=sx2[sl], inck=16)
                xld = sx2[sl].n
                P.wait("pe", s_act, b2_ev)
                if tau == 1 or w > 0:
                    P.wait("pe", s_dve, s_dve.n)
                for dc in range(16):
                    P.op("pe", lambda e, dc=dc, tau=tau: e.transpose(
                        (g1 if dc < 8 else g2)[:, (dc % 8) * 128:(dc % 8) * 128 + 128],
                        aof32[:, dc * 256 + tau * 128:dc * 256 + (tau + 1) * 128],
                        identf[:, :]), inc=(s_pe if dc == 15 else None))
                tp2 = s_pe.n
                P.wait("act", s_pe, tp2)
                P.op("act", lambda e: e.copy(out=atm[:, 0:1024], in_=g1[:, :]), inc=None)
                P.drain("act")
                P.op("act", lambda e: e.copy(out=atm[:, 1024:2048], in_=g2[:, :]),
                     inc=s_act)
                atm_ev = s_act.n
                P.wait("dve", s_act, atm_ev)
                for j in range(4):
                    P.op("dve", lambda e, j=j: e.bn_stats(
                        out=bnst[:, j * 6:(j + 1) * 6],
                        in_=atm[:, j * 512:(j + 1) * 512]), inc=None)
                P.drain("dve")
                P.op("dve", lambda e: e.bn_aggr(out=mvt[:, 0:2], in_=bnst[:, :]),
                     inc=s_dve)
                bn2 = s_dve.n
                P.wait("act", s_dve, bn2)
                P.op("act", lambda e: e.activation(out=rstdt[:, 0:1], in_=mvt[:, 1:2],
                                                   func=AF.Ln, bias=fc[:, 0:1]), inc=None)
                P.drain("act")
                P.op("act", lambda e: e.activation(out=rstdt[:, 1:2], in_=rstdt[:, 0:1],
                                                   func=AF.Exp, scale=-0.5), inc=s_act)
                rstd2 = s_act.n
                P.wait("dve", s_act, rstd2)
                P.op("dve", lambda e: e.tensor_scalar(
                    out=lnout[:, :], in0=atm[:, :], scalar1=mvt[:, 0:1],
                    scalar2=rstdt[:, 1:2], op0=ALU.subtract, op1=ALU.mult),
                    inc=None)
                P.drain("dve")
                P.wait("dve", sx2[sl], xld)
                if x2t_free[sl] is not None:
                    P.wait("dve", *x2t_free[sl])
                P.op("dve", lambda e, sl=sl: e.tensor_tensor(
                    out=x2t[sl][:, :], in0=lnout[:, :],
                    in1=AR2[:, sl * 2048:sl * 2048 + 2048], op=ALU.add),
                    inc=s_dve)
                x2_done = s_dve.n
                xb2_free[sl] = (s_dve, x2_done)
                P.wait("sp", s_dve, x2_done)
                P.op("sp", lambda e, i_own=i_own, sl=sl: e.dma_start(
                    out=x2d[i_own * 128:(i_own + 1) * 128, :], in_=x2t[sl][:, :]),
                    inc=s_out, inck=16)
                x2t_free[sl] = (s_out, s_out.n)
            b2_ev = s_act.n

        barrier()
        wdnat = [vbf(AR1, s * 4096, 8192) for s in range(2)]
        wdTs = [vbf(AR1, 8192 + s * 4096, 8192) for s in range(2)]
        swd = [s_wd0, s_wd1]
        wdn_free = [None, None]
        wdts_free = [None, None]
        tpd_bank = [0, 0]
        for ob in range(16):
            sl = ob % 2
            if wdn_free[sl] is not None:
                P.wait("pool", *wdn_free[sl])
            P.op("pool", lambda e, ob=ob, sl=sl: e.dma_start(
                out=wdnat[sl][:, :], in_=w_down[ob * 128:(ob + 1) * 128, :]),
                inc=swd[sl], inck=16)
            wd_ld = swd[sl].n
            P.wait("pe", swd[sl], wd_ld)
            if wdts_free[sl] is not None:
                P.wait("pe", *wdts_free[sl])
            tpd = 0
            for grp in range(8):
                pst = (psA, psB)[grp % 2]
                P.wait("pe", s_act, tpd_bank[grp % 2])
                for j in range(8):
                    fc_ = grp * 8 + j
                    P.op("pe", lambda e, fc_=fc_, j=j, pst=pst, sl=sl: e.transpose(
                        pst[:, j * 128:(j + 1) * 128],
                        wdnat[sl][:, fc_ * 128:(fc_ + 1) * 128], identb[:, :]),
                        inc=(s_pe if j == 7 else None))
                tg = s_pe.n
                P.wait("act", s_pe, tg)
                P.op("act", lambda e, grp=grp, pst=pst, sl=sl: e.copy(
                    out=wdTs[sl][:, grp * 1024:(grp + 1) * 1024], in_=pst[:, :]),
                    inc=s_act)
                tpd = s_act.n
                tpd_bank[grp % 2] = s_act.n
            wdn_free[sl] = (s_pe, s_pe.n)
            swt = [s_wt0, s_wt1]
            P.wait("sp", s_act, tpd)
            P.op("sp", lambda e, ob=ob, sl=sl: e.dma_start(
                out=wdTd[:, ob * 128:(ob + 1) * 128].rearrange(
                    "(fc p) o -> p fc o", p=128),
                in_=wdTs[sl].rearrange("p (fc o) -> p fc o", fc=64)),
                inc=swt[sl], inck=16)
            wdts_free[sl] = (swt[sl], swt[sl].n)

        barrier()
        NWC = OWN // 512
        h_sb = vbf(AR1, 0, 16 * OWN)
        ln3fm = vbf(AR1, 16384, 16 * 512)
        ln3tm = vbf(AR1, 20480, 2048)
        xb3 = v32(AR1, 21504, 2048)
        fA = [v32(AR2, c * 512, 512) for c in range(16)]
        fB = [v32(AR2, 8192 + c * 512, 512) for c in range(16)]

        xb3_free = None
        ln3_ev = 0
        fw_ev_bank = [0, 0]
        for w in range(NWC):
            for tau in range(4):
                i_own = w * 4 + tau
                if xb3_free is not None:
                    P.wait("sp", *xb3_free)
                P.op("sp", lambda e, i_own=i_own: e.dma_start(
                    out=xb3[:, :], in_=x2d[i_own * 128:(i_own + 1) * 128, :]),
                    inc=s_x2r0, inck=16)
                x2ld = s_x2r0.n
                P.wait("dve", s_x2r0, x2ld)
                for j in range(4):
                    P.op("dve", lambda e, j=j: e.bn_stats(
                        out=bnst[:, j * 6:(j + 1) * 6],
                        in_=xb3[:, j * 512:(j + 1) * 512]), inc=None)
                P.drain("dve")
                P.op("dve", lambda e: e.bn_aggr(out=mvt[:, 0:2], in_=bnst[:, :]),
                     inc=s_dve)
                bn3 = s_dve.n
                P.wait("act", s_dve, bn3)
                P.op("act", lambda e: e.activation(out=rstdt[:, 0:1], in_=mvt[:, 1:2],
                                                   func=AF.Ln, bias=fc[:, 0:1]), inc=None)
                P.drain("act")
                P.op("act", lambda e: e.activation(out=rstdt[:, 1:2], in_=rstdt[:, 0:1],
                                                   func=AF.Exp, scale=-0.5), inc=s_act)
                rstd3 = s_act.n
                P.wait("dve", s_act, rstd3)
                P.op("dve", lambda e: e.tensor_scalar(
                    out=ln3tm[:, :], in0=xb3[:, :], scalar1=mvt[:, 0:1],
                    scalar2=rstdt[:, 1:2], op0=ALU.subtract, op1=ALU.mult),
                    inc=s_dve)
                ln3_done = s_dve.n
                xb3_free = (s_dve, ln3_done)
                P.wait("pe", s_dve, ln3_done)
                P.wait("pe", s_act, ln3_ev)
                for j in range(16):
                    P.op("pe", lambda e, j=j: e.transpose(
                        (psA if j < 8 else psB)[:, (j % 8) * 128:(j % 8) * 128 + 128],
                        ln3tm[:, j * 128:(j + 1) * 128], identb[:, :]),
                        inc=(s_pe if j == 15 else None))
                tp3 = s_pe.n
                P.wait("act", s_pe, tp3)
                P.op("act", lambda e, tau=tau: e.copy(
                    out=ln3fm.rearrange("p (c t) -> p c t", c=16)[
                        :, 0:8, tau * 128:(tau + 1) * 128],
                    in_=psA[:, :].rearrange("p (c t) -> p c t", c=8)), inc=None)
                P.drain("act")
                P.op("act", lambda e, tau=tau: e.copy(
                    out=ln3fm.rearrange("p (c t) -> p c t", c=16)[
                        :, 8:16, tau * 128:(tau + 1) * 128],
                    in_=psB[:, :].rearrange("p (c t) -> p c t", c=8)), inc=s_act)
                ln3_ev = s_act.n
            P.wait("pe", s_act, ln3_ev)
            if w > 0:
                P.wait("act", s_dve, h_done)
            for c in range(16):
                pst = (g5, g6)[c % 2]
                P.wait("pe", s_act, fw_ev_bank[c % 2])
                P.op("pe", lambda e, c=c, pst=pst: e.matmul(
                    pst[:, :], hsu_sb[:, c * 128:(c + 1) * 128],
                    ln3fm[:, c * 512:(c + 1) * 512], start=True, stop=True),
                    inc=s_pe)
                fmm = s_pe.n
                P.wait("act", s_pe, fmm)
                P.op("act", lambda e, c=c, pst=pst: e.copy(
                    out=fA[c][:, :], in_=pst.ap()), inc=s_act)
                fw_ev_bank[c % 2] = s_act.n
            P.wait("dve", s_act, s_act.n)
            cur, nxt = fA, fB
            for st in range(4):
                step = 1 << st
                for c in range(16):
                    pa, pb = c & ~step, c | step
                    if st == 3:
                        dst = h_sb[:, c * OWN + w * 512:c * OWN + w * 512 + 512]
                    else:
                        dst = nxt[c][:, :]
                    opk = ALU.subtract if (c & step) else ALU.add
                    last_bt = (st == 3 and c == 15)
                    P.op("dve", lambda e, dst=dst, pa=pa, pb=pb, cur=cur, opk=opk: e.tensor_tensor(
                        out=dst, in0=cur[pa][:, :], in1=cur[pb][:, :],
                        op=opk), inc=(s_dve if last_bt else None))
                P.drain("dve")
                cur, nxt = nxt, cur
            h_done = s_dve.n

        barrier()
        wnat_up = [vbf(AR2, s * 1024, 2048) for s in range(2)]
        wnat_gt = [vbf(AR2, 2048 + s * 1024, 2048) for s in range(2)]
        uplhsT = [vbf(AR2, 4096 + s * 1024, 2048) for s in range(2)]
        gtlhsT = [vbf(AR2, 6144 + s * 1024, 2048) for s in range(2)]
        sgate = v32(AR2, 8192, 512)
        upsc = v32(AR2, 8704, 512)
        midb = vbf(AR2, 9216, 512)
        y2st = [vbf(AR2, 9472 + s * 1024, 2048) for s in range(2)]
        swu = [s_wu0, s_wu1]
        sy2w = [s_y2w0, s_y2w1]
        wn_free = [None, None]
        lh_free = [None, None]
        y2_free = [None, None]
        c2_tp_bank = [0, 0]
        mid_ev = 0
        y2_ev = 0
        for c2 in range(64):
            sl = c2 % 2
            if wn_free[sl] is not None:
                P.wait("pool", *wn_free[sl])
            P.op("pool", lambda e, c2=c2, sl=sl: e.dma_start(
                out=wnat_up[sl][:, :], in_=w_up[c2 * 128:(c2 + 1) * 128, :]),
                inc=swu[sl], inck=16)
            P.op("pool", lambda e, c2=c2, sl=sl: e.dma_start(
                out=wnat_gt[sl][:, :], in_=w_gate[c2 * 128:(c2 + 1) * 128, :]),
                inc=swu[sl], inck=16)
            wu_ld = swu[sl].n
            P.wait("pe", swu[sl], wu_ld)
            if lh_free[sl] is not None:
                P.wait("pe", *lh_free[sl])
            for j in range(16):
                P.wait("pe", s_act, c2_tp_bank[j % 2])
                P.op("pe", lambda e, j=j, sl=sl: e.transpose(
                    (psA, psB)[j % 2][:, (j // 2) * 128:(j // 2) * 128 + 128],
                    wnat_up[sl][:, j * 128:(j + 1) * 128], identb[:, :]),
                    inc=(s_pe if j >= 14 else None))
            tpu = s_pe.n
            P.wait("act", s_pe, tpu)
            P.op("act", lambda e, sl=sl: e.copy(
                out=uplhsT[sl].rearrange("p (c t) -> p c t", c=16)[:, 0:16:2, :],
                in_=psA[:, :].rearrange("p (c t) -> p c t", c=8)), inc=None)
            P.drain("act")
            P.op("act", lambda e, sl=sl: e.copy(
                out=uplhsT[sl].rearrange("p (c t) -> p c t", c=16)[:, 1:16:2, :],
                in_=psB[:, :].rearrange("p (c t) -> p c t", c=8)), inc=s_act)
            c2_tp_bank[0] = s_act.n
            c2_tp_bank[1] = s_act.n
            for j in range(16):
                P.wait("pe", s_act, c2_tp_bank[j % 2])
                P.op("pe", lambda e, j=j, sl=sl: e.transpose(
                    (psA, psB)[j % 2][:, (j // 2) * 128:(j // 2) * 128 + 128],
                    wnat_gt[sl][:, j * 128:(j + 1) * 128], identb[:, :]),
                    inc=(s_pe if j >= 14 else None))
            tpg = s_pe.n
            wn_free[sl] = (s_pe, tpg)
            P.wait("act", s_pe, tpg)
            P.op("act", lambda e, sl=sl: e.copy(
                out=gtlhsT[sl].rearrange("p (c t) -> p c t", c=16)[:, 0:16:2, :],
                in_=psA[:, :].rearrange("p (c t) -> p c t", c=8)), inc=None)
            P.drain("act")
            P.op("act", lambda e, sl=sl: e.copy(
                out=gtlhsT[sl].rearrange("p (c t) -> p c t", c=16)[:, 1:16:2, :],
                in_=psB[:, :].rearrange("p (c t) -> p c t", c=8)), inc=s_act)
            c2_tp_bank[0] = s_act.n
            c2_tp_bank[1] = s_act.n
            lh_ready = s_act.n
            ysl = c2 % 2
            if y2_free[ysl] is not None:
                P.wait("act", *y2_free[ysl])
            for w in range(NWC):
                P.wait("pe", s_act, lh_ready)
                if c2 == 0 and w == 0:
                    P.wait("pe", s_dve, h_done)
                P.wait("pe", s_dve, mid_ev)
                P.wait("pe", s_act, y2_ev)
                for c in range(16):
                    P.op("pe", lambda e, c=c, sl=sl, w=w: e.matmul(
                        g5[:, :], uplhsT[sl][:, c * 128:(c + 1) * 128],
                        h_sb[:, c * OWN + w * 512:c * OWN + (w + 1) * 512],
                        start=(c == 0), stop=(c == 15)), inc=None)
                for c in range(16):
                    P.op("pe", lambda e, c=c, sl=sl, w=w: e.matmul(
                        g6[:, :], gtlhsT[sl][:, c * 128:(c + 1) * 128],
                        h_sb[:, c * OWN + w * 512:c * OWN + (w + 1) * 512],
                        start=(c == 0), stop=(c == 15)),
                        inc=(s_pe if c == 15 else None))
                ug_done = s_pe.n
                P.wait("act", s_pe, ug_done)
                P.op("act", lambda e, c2=c2: e.activation(
                    out=sgate[:, :], in_=g6[:, :], func=AF.Silu,
                    scale=wscg_cols[:, c2:c2 + 1]), inc=s_act)
                silu_done = s_act.n
                P.wait("dve", s_pe, ug_done)
                P.op("dve", lambda e, c2=c2: e.tensor_scalar(
                    out=upsc[:, :], in0=g5[:, :], scalar1=upsc_cols[:, c2:c2 + 1],
                    scalar2=None, op0=ALU.mult), inc=None)
                P.drain("dve")
                P.wait("dve", s_act, silu_done)
                P.op("dve", lambda e: e.tensor_tensor(
                    out=midb[:, :], in0=upsc[:, :], in1=sgate[:, :],
                    op=ALU.mult), inc=s_dve)
                mid_ev = s_dve.n
                P.wait("pe", s_dve, mid_ev)
                P.op("pe", lambda e: e.matmul(
                    g1[:, 0:512], h128b_sb[:, :], midb[:, :],
                    start=True, stop=True), inc=s_pe)
                ymm = s_pe.n
                P.wait("act", s_pe, ymm)
                P.op("act", lambda e, w=w, ysl=ysl: e.copy(
                    out=y2st[ysl][:, w * 512:(w + 1) * 512],
                    in_=g1.ap()[:, 0:512]), inc=s_act)
                y2_ev = s_act.n
            P.wait("sp", s_act, y2_ev)
            P.op("sp", lambda e, c2=c2, ysl=ysl: e.dma_start(
                out=y2d[c2 * 128:(c2 + 1) * 128, 0:OWN],
                in_=y2st[ysl][:, 0:OWN]), inc=sy2w[ysl], inck=16)
            y2_free[ysl] = (sy2w[ysl], sy2w[ysl].n)

        barrier()
        NW3 = OWN // 256
        z_sb = [vbf(AR1, s * 8192, 64 * 256) for s in range(2)]
        ffo = vbf(AR1, 16384, 16 * 256)
        atm3 = v32(AR1, 18432, 2048)
        x2b = v32(AR1, 20480, 2048)
        wdl = [vbf(AR1, 22528 + s * 1024, 2048) for s in range(2)]
        y2w = vbf(AR2, 0, 64 * 256)
        pA = [vbf(AR2, 8192 + c * 128, 256) for c in range(64)]
        pB = [vbf(AR2, c * 128, 256) for c in range(64)]
        swdl = [s_wl0, s_wl1]
        DVE_CH = 64

        def ps8b(oc):
            if oc < 2:
                return g1, oc * 512
            if oc < 4:
                return g2, (oc - 2) * 512
            return (g3, g4, g5, g6)[oc - 4], 0

        y2w_free = None
        wd_free = [None, None]
        z_free = [0, 0]
        down_ev = 0
        x2b_free = None
        atm3_free = 0
        out_dma = 0
        dn_done = 0
        bt_done = [(0, 0), (0, 0)]

        def emit_butterfly(w):
            nonlocal y2w_free
            zsl = w % 2
            col0 = w * 256
            if y2w_free is not None:
                P.wait("act", *y2w_free)
            P.op("act", lambda e, col0=col0: e.dma_start(
                out=y2w.rearrange("p (c t) -> p c t", c=64),
                in_=y2d[:, col0:col0 + 256].rearrange(
                    "(c p) t -> p c t", p=128)), inc=s_y0, inck=16)
            yld = s_y0.n
            P.wait("dve", s_y0, yld)
            P.wait("pool", s_y0, yld)
            P.wait("dve", s_pe, z_free[zsl])
            P.wait("pool", s_pe, z_free[zsl])
            for st in range(6):
                step = 1 << st
                P.wait("dve", s_pl, s_pl.n)
                P.wait("pool", s_dve, s_dve.n)
                for c in range(64):
                    eng = "dve" if c < DVE_CH else "pool"
                    pa_, pb_ = c & ~step, c | step
                    if st == 0:
                        i0 = y2w[:, pa_ * 256:(pa_ + 1) * 256]
                        i1 = y2w[:, pb_ * 256:(pb_ + 1) * 256]
                    else:
                        src_ = pA if st % 2 == 1 else pB
                        i0 = src_[pa_][:, :]
                        i1 = src_[pb_][:, :]
                    if st == 5:
                        dst = z_sb[zsl][:, c * 256:(c + 1) * 256]
                    else:
                        dst_ = pA if st % 2 == 0 else pB
                        dst = dst_[c][:, :]
                    opk = ALU.subtract if (c & step) else ALU.add
                    lastd = (c == DVE_CH - 1)
                    lastp = (c == 63)
                    P.op(eng, lambda e, dst=dst, i0=i0, i1=i1, opk=opk: e.tensor_tensor(
                        out=dst, in0=i0, in1=i1, op=opk),
                        inc=(s_dve if (eng == "dve" and lastd) else
                             (s_pl if (eng == "pool" and lastp) else None)))
                P.drain("dve")
                P.drain("pool")
            y2w_free = (s_dve, s_dve.n)
            bt_done[zsl] = (s_dve.n, s_pl.n)
            bt_done[zsl] = (s_dve.n, s_pl.n)

        emit_butterfly(0)
        for w in range(NW3):
            zsl = w % 2
            P.wait("pe", s_dve, bt_done[zsl][0])
            P.wait("pe", s_pl, bt_done[zsl][1])
            for op_ in range(2):
                P.wait("pe", s_act, down_ev)
                if op_ == 0:
                    P.wait("pe", s_act, atm3_free)
                for f2g in range(32):
                    sl = f2g % 2
                    if wd_free[sl] is not None:
                        P.wait("sp", *wd_free[sl])
                    P.op("sp", lambda e, f2g=f2g, sl=sl, op_=op_: e.dma_start(
                        out=wdl[sl][:, 0:2048].rearrange("p (f n) -> p f n", f=2),
                        in_=wdTd[f2g * 256:(f2g + 1) * 256,
                                 op_ * 1024:(op_ + 1) * 1024].rearrange(
                            "(f p) n -> p f n", p=128)),
                        inc=swdl[sl], inck=16)
                    wld = swdl[sl].n
                    P.wait("pe", swdl[sl], wld)
                    for fi in range(2):
                        f2 = f2g * 2 + fi
                        for oc in range(8):
                            pst, col = ps8b(oc)
                            P.op("pe", lambda e, f2=f2, fi=fi, oc=oc, pst=pst,
                                 col=col, sl=sl, zsl=zsl: e.matmul(
                                pst[:, col:col + 256],
                                wdl[sl][:, fi * 1024 + oc * 128:fi * 1024 + (oc + 1) * 128],
                                z_sb[zsl][:, f2 * 256:(f2 + 1) * 256],
                                start=(f2 == 0), stop=(f2 == 63)),
                                inc=(s_pe if (oc == 7 and fi == 1) else None))
                    wd_free[sl] = (s_pe, s_pe.n)
                dn_done = s_pe.n
                P.wait("act", s_pe, dn_done)
                for oc in range(8):
                    ocg = op_ * 8 + oc
                    pst, col = ps8b(oc)
                    if oc > 0:
                        P.drain("act")
                    P.op("act", lambda e, ocg=ocg, pst=pst, col=col: e.activation(
                        out=ffo[:, ocg * 256:(ocg + 1) * 256],
                        in_=pst[:, col:col + 256],
                        func=AF.Copy, scale=wscdp_sb[:, ocg:ocg + 1]),
                        inc=(s_act if oc == 7 else None))
                down_ev = s_act.n
            z_free[zsl] = dn_done
            if w + 1 < NW3:
                emit_butterfly(w + 1)
            for tau in range(2):
                i_own = w * 2 + tau
                if x2b_free is not None:
                    P.wait("sp", *x2b_free)
                P.op("sp", lambda e, i_own=i_own: e.dma_start(
                    out=x2b[:, :], in_=x2d[i_own * 128:(i_own + 1) * 128, :]),
                    inc=s_x2r1, inck=16)
                x2ld = s_x2r1.n
                P.wait("pe", s_act, down_ev)
                P.wait("pe", s_act, atm3_free)
                for j in range(16):
                    P.op("pe", lambda e, j=j, tau=tau: e.transpose(
                        (psA if j < 8 else psB)[:, (j % 8) * 128:(j % 8) * 128 + 128],
                        ffo[:, j * 256 + tau * 128:j * 256 + (tau + 1) * 128],
                        identb[:, :]), inc=(s_pe if j == 15 else None))
                tp4 = s_pe.n
                P.wait("act", s_pe, tp4)
                P.wait("act", s_out, out_dma)
                P.op("act", lambda e: e.copy(out=atm3[:, 0:1024], in_=psA[:, :]),
                     inc=None)
                P.drain("act")
                P.op("act", lambda e: e.copy(out=atm3[:, 1024:2048], in_=psB[:, :]),
                     inc=s_act)
                atm3_ev = s_act.n
                atm3_free = s_act.n
                P.wait("dve", s_act, atm3_ev)
                for j in range(4):
                    P.op("dve", lambda e, j=j: e.bn_stats(
                        out=bnst[:, j * 6:(j + 1) * 6],
                        in_=atm3[:, j * 512:(j + 1) * 512]), inc=None)
                P.drain("dve")
                P.op("dve", lambda e: e.bn_aggr(out=mvt[:, 0:2], in_=bnst[:, :]),
                     inc=s_dve)
                bn4 = s_dve.n
                P.wait("act", s_dve, bn4)
                P.op("act", lambda e: e.activation(out=rstdt[:, 0:1], in_=mvt[:, 1:2],
                                                   func=AF.Ln, bias=fc[:, 0:1]), inc=None)
                P.drain("act")
                P.op("act", lambda e: e.activation(out=rstdt[:, 1:2], in_=rstdt[:, 0:1],
                                                   func=AF.Exp, scale=-0.5), inc=s_act)
                rstd4 = s_act.n
                P.wait("dve", s_act, rstd4)
                P.wait("dve", s_x2r1, x2ld)
                P.op("dve", lambda e: e.tensor_scalar(
                    out=atm3[:, :], in0=atm3[:, :], scalar1=mvt[:, 0:1],
                    scalar2=rstdt[:, 1:2], op0=ALU.subtract, op1=ALU.mult),
                    inc=None)
                P.drain("dve")
                P.op("dve", lambda e: e.tensor_tensor(
                    out=atm3[:, :], in0=atm3[:, :], in1=x2b[:, :], op=ALU.add),
                    inc=s_dve)
                x3_done = s_dve.n
                x2b_free = (s_dve, x3_done)
                P.wait("sp", s_dve, x3_done)
                P.op("sp", lambda e, i_own=i_own: e.dma_start(
                    out=out[i_own * 128:(i_own + 1) * 128, :], in_=atm3[:, :]),
                    inc=s_out, inck=16)
                out_dma = s_out.n

        P.wait("sp", s_out, s_out.n)
        P.wait("pool", s_po, s_po.n)

        import os
        _tr = float(os.environ.get("K_TRUNC", "1.0"))
        if _tr < 1.0:
            for k in P.q:
                P.q[k] = P.q[k][:int(len(P.q[k]) * _tr)]
        for k in P.q:
            _trk = float(os.environ.get("K_TRUNC_" + k.upper(), "1.0"))
            if _trk < 1.0:
                P.q[k] = P.q[k][:int(len(P.q[k]) * _trk)]
        with nc.Block() as block:
            @block.tensor
            def _(e):
                for f in P.q["pe"]:
                    f(e)

            @block.scalar
            def _(e):
                for f in P.q["act"]:
                    f(e)

            @block.vector
            def _(e):
                for f in P.q["dve"]:
                    f(e)

            @block.gpsimd
            def _(e):
                for f in P.q["pool"]:
                    f(e)

            @block.sync
            def _(e):
                for f in P.q["sp"]:
                    f(e)

    return nc


def _host_inputs(inputs, T=8192):
    OWN = T // 4
    x = np.ascontiguousarray(np.asarray(inputs["L0_attn_x"], dtype=np.float32)
                             .reshape(2 * T, D))
    q_a_w = np.ascontiguousarray(inputs["L0_attn_q_a_w"], dtype=np.float32)
    q_b_w = np.asarray(inputs["L0_attn_q_b_w"], dtype=np.float32)
    kv_a_w = np.ascontiguousarray(inputs["L0_attn_kv_a_w"], dtype=np.float32)
    kv_b_w = np.asarray(inputs["L0_attn_kv_b_w"], dtype=np.float32)
    o_w = np.ascontiguousarray(inputs["L0_attn_o_w"], dtype=np.float32)
    w_up = np.ascontiguousarray(inputs["L0_ffn_W_up"], dtype=np.float32)
    w_gate = np.ascontiguousarray(inputs["L0_ffn_W_gate"], dtype=np.float32)
    w_down = np.ascontiguousarray(inputs["L0_ffn_W_down"], dtype=np.float32)
    wq = np.asarray(inputs["L0_attn_q_a_ln_w"], dtype=np.float32)
    wkv = np.asarray(inputs["L0_attn_kv_a_ln_w"], dtype=np.float32)
    su_ug = np.asarray(inputs["L0_ffn_SU_ug"], dtype=np.float32)
    su_dp = np.asarray(inputs["L0_ffn_SU_dp"], dtype=np.float32)
    wsc_ug = np.asarray(inputs["L0_ffn_Wscale_ug"], dtype=np.float32)
    wsc_dp = np.asarray(inputs["L0_ffn_Wscale_dp"], dtype=np.float32)

    inv = (1.0 / THETA ** (np.arange(0, ROPE, 2) / ROPE)).astype(np.float64)
    tpos = np.arange(T, dtype=np.float64)
    fr = np.outer(tpos, inv)
    cos_t = np.ascontiguousarray(np.cos(fr).astype(np.float32))
    sin_t = np.ascontiguousarray(np.sin(fr).astype(np.float32))

    def hmat(m):
        Hm = np.array([[1.0]], dtype=np.float32)
        while Hm.shape[0] < m:
            Hm = np.block([[Hm, Hm], [Hm, -Hm]])
        return Hm
    H128m = np.ascontiguousarray(hmat(128).astype(np.float32))
    ones_b = np.ones((128, 128), dtype=ml_dtypes.bfloat16)
    ident_b = np.eye(128, dtype=ml_dtypes.bfloat16)
    ident_f = np.eye(128, dtype=np.float32)

    def cols(v, npart=128):
        return np.ascontiguousarray(v.reshape(-1, npart).T.astype(np.float32))

    NCH = OWN // 512
    LIMS = [16 * (s + 1) for s in range(NCH)]

    in_maps = []
    for r in range(NC_):
        b = r // 4
        j = r % 4
        chunks = [(s * 4 + j) if s % 2 == 0 else ((s + 1) * 4 - 1 - j)
                  for s in range(NCH)]
        own_rows = np.concatenate([np.arange(c * 512, (c + 1) * 512)
                                   for c in chunks])
        xb = x[b * T:(b + 1) * T]
        mk = np.zeros((128, NCH * 16 * 512), dtype=ml_dtypes.bfloat16)
        for s, c in enumerate(chunks):
            for ti in range(16):
                kt = LIMS[s] - 16 + ti
                krow = kt * 128 + np.arange(128)[:, None]
                qcol = c * 512 + np.arange(512)[None, :]
                mk[:, (s * 16 + ti) * 512:(s * 16 + ti + 1) * 512] = (
                    krow <= qcol).astype(ml_dtypes.bfloat16)
        cos16 = np.tile(cos_t[own_rows], (1, 16))
        sin16 = np.tile(sin_t[own_rows], (1, 16))
        mp = {
            "x_own": np.ascontiguousarray(xb[own_rows]),
            "x_batch": np.ascontiguousarray(xb),
            "q_a_w": q_a_w, "q_b_w": np.ascontiguousarray(q_b_w),
            "kv_a_w": kv_a_w, "kv_b_w": np.ascontiguousarray(kv_b_w),
            "o_w": o_w,
            "w_up": w_up, "w_gate": w_gate, "w_down": w_down,
            "wq_cols": cols(wq), "wkv_cols": cols(wkv),
            "su_ug_cols": cols(su_ug), "su_dp_cols": cols(su_dp),
            "wsc_ug_cols": cols(wsc_ug), "wsc_dp_cols": cols(wsc_dp),
            "cos_all": cos_t, "sin_all": sin_t,
            "cos16_own": np.ascontiguousarray(cos16),
            "sin16_own": np.ascontiguousarray(sin16),
            "h128": H128m, "ones_bf": ones_b, "ident_bf": ident_b,
            "ident_f32": ident_f, "masks_in": np.ascontiguousarray(mk),
            "fconst": np.ascontiguousarray(np.broadcast_to(np.array(
                [EPS_LN, EPS_RMS, math.log(ATT_SCALE), 0.0], dtype=np.float32),
                (128, 4))),
        }
        in_maps.append(mp)
    return in_maps


def _assemble(shards, T):
    OWN = T // 4
    NCH = OWN // 512
    full = np.empty((2, T, D), dtype=np.float32)
    for r in range(NC_):
        b = r // 4
        j = r % 4
        chunks = [(s * 4 + j) if s % 2 == 0 else ((s + 1) * 4 - 1 - j)
                  for s in range(NCH)]
        for s, c in enumerate(chunks):
            full[b, c * 512:(c + 1) * 512] = shards[r][s * 512:(s + 1) * 512]
    return full


def kernel(**inputs):
    T = 8192
    in_maps = _host_inputs(inputs, T=T)
    nc = build_program(T=T, debug=False)
    res = run_bass_kernel_spmd(nc, in_maps, core_ids=list(range(NC_)))
    shards = [res.results[r]["out"] for r in range(NC_)]
    return _assemble(shards, T)


# version 10
# speedup vs baseline: 1.1700x; 1.1700x over previous
import math
import numpy as np
import ml_dtypes
import concourse.bass as bass
import concourse.mybir as mybir
from concourse.bass_utils import run_bass_kernel_spmd

F32 = mybir.dt.float32
BF16 = mybir.dt.bfloat16
AF = mybir.ActivationFunctionType
ALU = mybir.AluOpType
AX = mybir.AxisListType

D = 2048
H = 16
NOPE = 128
ROPE = 64
QK = 192
VD = 128
KVL = 512
QL = 1536
INTER = 8192
EPS_LN = 1e-5
EPS_RMS = 1e-6
ATT_SCALE = QK ** -0.5
THETA = 5e5

NC_ = 8
GROUPS = [[0, 1, 2, 3], [4, 5, 6, 7]]
HPC = 4

BF = 2
FP = 4


class Sem:
    def __init__(self, h):
        self.h = h
        self.n = 0

    def bump(self, k=1):
        self.n += k
        return self.n


class Prog:

    ENGS = ("pe", "act", "dve", "pool", "sp")

    def __init__(self):
        self.q = {e: [] for e in self.ENGS}

    def wait(self, eng, sem, val):
        if val > 0:
            self.q[eng].append(lambda e, h=sem.h, v=val: e.wait_ge(h, v))

    def op(self, eng, build, inc=None, inck=1):
        if inc is None:
            self.q[eng].append(build)
            return None
        v = inc.bump(inck)
        def f(e, b=build, h=inc.h, k=inck):
            inst = b(e)
            inst.then_inc(h, k)
        self.q[eng].append(f)
        return v

    def drain(self, eng):
        self.q[eng].append(lambda e: e.drain())


def ceil_div(a, b):
    return (a + b - 1) // b


def build_program(T=8192, debug=False, taps=()):
    OWN = T // 4
    NT = OWN // 128
    NW = T // 512
    NKT = T // 128
    NB = OWN // 512
    QBLK = T // 2048
    assert OWN % 512 == 0 and T % 2048 == 0

    nc = bass.Bass("TRN2", target_bir_lowering=False, debug=debug)

    def din(name, shape, dt):
        return nc.dram_tensor(name, shape, dt, kind="ExternalInput")

    x_own = din("x_own", [OWN, D], F32)
    x_batch = din("x_batch", [T, D], F32)
    q_a_w = din("q_a_w", [D, QL], F32)
    q_b_w = din("q_b_w", [QL, H * QK], F32)
    kv_a_w = din("kv_a_w", [D, KVL + ROPE], F32)
    kv_b_w = din("kv_b_w", [KVL, H * 256], F32)
    o_w = din("o_w", [D, D], F32)
    w_up = din("w_up", [INTER, D], F32)
    w_gate = din("w_gate", [INTER, D], F32)
    w_down = din("w_down", [D, INTER], F32)
    wq_cols = din("wq_cols", [128, QL // 128], F32)
    wkv_cols = din("wkv_cols", [128, KVL // 128], F32)
    su_ug_cols = din("su_ug_cols", [128, 16], F32)
    su_dp_cols = din("su_dp_cols", [128, 64], F32)
    wsc_ug_cols = din("wsc_ug_cols", [128, 128], F32)
    wsc_dp_cols = din("wsc_dp_cols", [128, 16], F32)
    cos_all = din("cos_all", [T, 32], F32)
    sin_all = din("sin_all", [T, 32], F32)
    cos16_own = din("cos16_own", [OWN, 512], F32)
    sin16_own = din("sin16_own", [OWN, 512], F32)
    h128 = din("h128", [128, 128], F32)
    ones_bf = din("ones_bf", [128, 128], BF16)
    ident_bf = din("ident_bf", [128, 128], BF16)
    ident_f32 = din("ident_f32", [128, 128], F32)
    masks_in = din("masks_in", [128, (OWN // 512) * 16 * 512], BF16)
    fconst = din("fconst", [128, 4], F32)

    out = nc.dram_tensor("out", [OWN, D], F32, kind="ExternalOutput")

    qspill = nc.dram_tensor("qspill", [OWN, H * QK], BF16)
    ospill = nc.dram_tensor("ospill", [H * VD, OWN], BF16)
    x2d = nc.dram_tensor("x2d", [OWN, D], F32)
    y2d = nc.dram_tensor("y2d", [INTER, OWN], BF16)
    wdTd = nc.dram_tensor("wdTd", [INTER, D], BF16)

    tap_outs = {}
    for tname in taps:
        src = {"qspill": qspill, "ospill": ospill, "x2d": x2d,
               "y2d": y2d, "wdTd": wdTd}[tname]
        tap_outs[tname] = (nc.dram_tensor("dbg_" + tname, list(src.shape),
                                          src.dtype, kind="ExternalOutput"), src)

    P = Prog()

    import contextlib
    ctx = contextlib.ExitStack()
    with ctx:
        AR1 = ctx.enter_context(nc.sbuf_tensor("AR1", [128, 24576], F32))
        AR2 = ctx.enter_context(nc.sbuf_tensor("AR2", [128, 20480], F32))

        def v32(ar, off, n):
            return ar[:, off:off + n]

        def vbf(ar, off, n):
            return ar.ap().bitcast(BF16)[:, 2 * off:2 * off + n]

        onesb = ctx.enter_context(nc.sbuf_tensor("onesb", [128, 128], BF16))
        identb = ctx.enter_context(nc.sbuf_tensor("identb", [128, 128], BF16))
        identf = ctx.enter_context(nc.sbuf_tensor("identf", [128, 128], F32))
        h128f = ctx.enter_context(nc.sbuf_tensor("h128f", [128, 128], F32))
        hsu_sb = ctx.enter_context(nc.sbuf_tensor("hsu_sb", [128, 16 * 128], BF16))
        h128b_sb = ctx.enter_context(nc.sbuf_tensor("h128b_sb", [128, 128], BF16))
        colscr = ctx.enter_context(nc.sbuf_tensor("colscr", [128, 256], F32))
        rowscr = ctx.enter_context(nc.sbuf_tensor("rowscr", [16, 512], F32))
        upsc_cols = ctx.enter_context(nc.sbuf_tensor("upsc_cols", [128, 64], F32))
        wscg_cols = ctx.enter_context(nc.sbuf_tensor("wscg_cols", [128, 64], F32))
        wscdp_sb = ctx.enter_context(nc.sbuf_tensor("wscdp_sb", [128, 16], F32))
        wqc_sb = ctx.enter_context(nc.sbuf_tensor("wqc_sb", [128, 12], F32))
        wkvc_sb = ctx.enter_context(nc.sbuf_tensor("wkvc_sb", [128, 4], F32))
        suug_sb = ctx.enter_context(nc.sbuf_tensor("suug_sb", [128, 16], F32))
        bnst = ctx.enter_context(nc.sbuf_tensor("bnst", [128, 24], F32))
        mvt = ctx.enter_context(nc.sbuf_tensor("mvt", [128, 2], F32))
        rstdt = ctx.enter_context(nc.sbuf_tensor("rstdt", [128, 4], F32))
        sstat = ctx.enter_context(nc.sbuf_tensor("sstat", [128, 16], F32))
        fc = ctx.enter_context(nc.sbuf_tensor("fc", [128, 4], F32))
        nb_bf = ctx.enter_context(nc.sbuf_tensor("nb_bf", [1, 16], BF16))
        onesf = ctx.enter_context(nc.sbuf_tensor("onesf", [128, 1], F32))
        recb = ctx.enter_context(nc.sbuf_tensor("recb", [1, 512], BF16))

        g1 = ctx.enter_context(nc.psum_tensor("g1", [128, 1024], F32))
        g2 = ctx.enter_context(nc.psum_tensor("g2", [128, 1024], F32))
        g3 = ctx.enter_context(nc.psum_tensor("g3", [128, 512], F32))
        g4 = ctx.enter_context(nc.psum_tensor("g4", [128, 512], F32))
        g5 = ctx.enter_context(nc.psum_tensor("g5", [128, 512], F32))
        g6 = ctx.enter_context(nc.psum_tensor("g6", [128, 512], F32))
        psA = g3.ap().bitcast(BF16)
        psB = g4.ap().bitcast(BF16)
        ps1 = g5.ap()
        ps2 = g2.ap()[:, 0:512]
        ps3 = g6.ap()
        ps4 = g1.ap()[:, 0:512]

        def mksem(name):
            return Sem(ctx.enter_context(nc.semaphore(name)))
        s_c = mksem("s_c")
        s_x0, s_x1 = mksem("s_x0"), mksem("s_x1")
        s_wa = mksem("s_wa")
        s_mk = mksem("s_mk")
        s_kb = mksem("s_kb")
        s_pe, s_act, s_dve = mksem("s_pe"), mksem("s_act"), mksem("s_dve")
        s_out = mksem("s_out")
        s_po = mksem("s_po")
        s_bar = mksem("s_bar")
        sx = [s_x0, s_x1]
        s_qs0, s_qs1 = mksem("s_qs0"), mksem("s_qs1")
        s_cs0, s_cs1 = mksem("s_cs0"), mksem("s_cs1")
        s_kv = mksem("s_kv")
        s_qr0, s_qr1 = mksem("s_qr0"), mksem("s_qr1")
        s_c20 = mksem("s_c20")
        s_c21 = mksem("s_c21")
        s_oh0, s_oh1 = mksem("s_oh0"), mksem("s_oh1")
        s_ow = mksem("s_ow")
        s_wu0, s_wu1 = mksem("s_wu0"), mksem("s_wu1")
        s_y0, s_y1 = mksem("s_y0"), mksem("s_y1")
        s_wd0, s_wd1 = mksem("s_wd0"), mksem("s_wd1")
        s_x2r0, s_x2r1 = mksem("s_x2r0"), mksem("s_x2r1")
        s_wt0, s_wt1 = mksem("s_wt0"), mksem("s_wt1")
        s_y2w0, s_y2w1 = mksem("s_y2w0"), mksem("s_y2w1")
        s_wl0, s_wl1 = mksem("s_wl0"), mksem("s_wl1")
        s_pl = mksem("s_pl")

        ALL_SEMS = [s_c, s_x0, s_x1, s_wa, s_mk, s_kb, s_out, s_po,
                    s_qs0, s_qs1, s_cs0, s_cs1, s_kv, s_qr0, s_qr1, s_c20, s_c21,
                    s_oh0, s_oh1, s_ow, s_wu0, s_wu1, s_y0, s_y1,
                    s_wd0, s_wd1, s_x2r0, s_x2r1, s_wt0, s_wt1,
                    s_y2w0, s_y2w1, s_wl0, s_wl1]

        def barrier():
            for e in P.ENGS:
                P.op(e, lambda en: en.nop(), inc=s_bar)
            tgt = s_bar.n
            for e in P.ENGS:
                P.wait(e, s_bar, tgt)
                for s in ALL_SEMS + [s_pe, s_act, s_dve, s_pl]:
                    P.wait(e, s, s.n)

        LNS = math.log(ATT_SCALE)
        NCH = OWN // 512
        LIMS = [16 * (s + 1) for s in range(NCH)]

        P.op("sp", lambda e: e.dma_start(out=onesb[:, :], in_=ones_bf.ap()), inc=s_c, inck=16)
        P.op("sp", lambda e: e.dma_start(out=identb[:, :], in_=ident_bf.ap()), inc=s_c, inck=16)
        P.op("sp", lambda e: e.dma_start(out=identf[:, :], in_=ident_f32.ap()), inc=s_c, inck=16)
        P.op("sp", lambda e: e.dma_start(out=h128f[:, :], in_=h128.ap()), inc=s_c, inck=16)
        P.op("sp", lambda e: e.dma_start(out=wqc_sb[:, :], in_=wq_cols.ap()), inc=s_c, inck=16)
        P.op("sp", lambda e: e.dma_start(out=wkvc_sb[:, :], in_=wkv_cols.ap()), inc=s_c, inck=16)
        P.op("sp", lambda e: e.dma_start(out=suug_sb[:, :], in_=su_ug_cols.ap()), inc=s_c, inck=16)
        P.op("sp", lambda e: e.dma_start(out=upsc_cols[:, :], in_=wsc_ug_cols.ap()[:, 0:64]), inc=s_c, inck=16)
        P.op("sp", lambda e: e.dma_start(out=wscg_cols[:, :], in_=wsc_ug_cols.ap()[:, 64:128]), inc=s_c, inck=16)
        P.op("sp", lambda e: e.dma_start(out=wscdp_sb[:, :], in_=wsc_dp_cols.ap()), inc=s_c, inck=16)
        P.op("sp", lambda e: e.dma_start(out=colscr[:, 0:64], in_=su_dp_cols.ap()), inc=s_c, inck=16)
        P.op("sp", lambda e: e.dma_start(out=fc[:, :], in_=fconst.ap()), inc=s_c, inck=16)
        SC = s_c.n

        P.wait("dve", s_c, SC)
        P.op("dve", lambda e: e.memset(onesf[:, :], 1.0), inc=None)
        P.drain("dve")
        P.op("dve", lambda e: e.tensor_tensor(out=upsc_cols[:, :],
                                              in0=upsc_cols[:, :],
                                              in1=colscr[:, 0:64], op=ALU.mult),
             inc=s_dve)
        for c in range(16):
            P.op("dve", lambda e, c=c: e.tensor_scalar(
                out=hsu_sb[:, c * 128:(c + 1) * 128], in0=h128f[:, :],
                scalar1=suug_sb[:, c:c + 1], scalar2=1.0 / math.sqrt(2048.0),
                op0=ALU.mult, op1=ALU.mult), inc=s_dve)
        P.op("dve", lambda e: e.tensor_scalar(
            out=h128b_sb[:, :], in0=h128f[:, :], scalar1=1.0 / math.sqrt(8192.0),
            scalar2=None, op0=ALU.mult), inc=s_dve)
        consts_done = s_dve.n
        P.wait("pe", s_c, SC)
        P.wait("act", s_c, SC)

        qaw_bf = vbf(AR1, 0, 16 * 1536)
        qan_own = vbf(AR1, 12288, 12 * OWN)
        qatmp = v32(AR2, 4096, 1536)
        ln1tm = vbf(AR2, 5632, 2048)
        ln1fm = vbf(AR2, 6656, 2048)
        sqscr = v32(AR2, 7680, 512)

        P.op("pool", lambda e: e.dma_start(
            out=AR1.ap().bitcast(BF16)[:, 0:16 * 1536].rearrange(
                "p (c n) -> p c n", c=16),
            in_=q_a_w.ap().rearrange("(c p) n -> p c n", p=128)),
            inc=s_wa, inck=16)
        AW = s_wa.n

        xt_free = [None, None]
        fm_evac = 0
        qan_ev = 0
        for i in range(NT):
            sl = i % 2
            if xt_free[sl] is not None:
                P.wait("sp", *xt_free[sl])
            P.op("sp", lambda e, i=i, sl=sl: e.dma_start(
                out=AR2[:, sl * 2048:sl * 2048 + 2048],
                in_=x_own[i * 128:(i + 1) * 128, :]), inc=sx[sl], inck=16)
            x_ld = sx[sl].n
            P.wait("dve", sx[sl], x_ld)
            if i == 0:
                P.wait("dve", s_dve, consts_done)
            for j in range(4):
                P.op("dve", lambda e, sl=sl, j=j: e.bn_stats(
                    out=bnst[:, j * 6:(j + 1) * 6],
                    in_=AR2[:, sl * 2048 + j * 512:sl * 2048 + (j + 1) * 512]),
                    inc=None)
            P.drain("dve")
            P.op("dve", lambda e: e.bn_aggr(out=mvt[:, 0:2], in_=bnst[:, :]),
                 inc=s_dve)
            bn_done = s_dve.n
            P.wait("act", s_dve, bn_done)
            P.op("act", lambda e: e.activation(out=rstdt[:, 0:1], in_=mvt[:, 1:2],
                                               func=AF.Ln, bias=fc[:, 0:1]), inc=None)
            P.drain("act")
            P.op("act", lambda e: e.activation(out=rstdt[:, 1:2], in_=rstdt[:, 0:1],
                                               func=AF.Exp, scale=-0.5), inc=s_act)
            rstd_done = s_act.n
            P.wait("dve", s_act, rstd_done)
            P.op("dve", lambda e, sl=sl: e.tensor_scalar(
                out=ln1tm, in0=AR2[:, sl * 2048:sl * 2048 + 2048],
                scalar1=mvt[:, 0:1], scalar2=rstdt[:, 1:2],
                op0=ALU.subtract, op1=ALU.mult), inc=s_dve)
            ln1_done = s_dve.n
            xt_free[sl] = (s_dve, ln1_done)
            P.wait("pe", s_dve, ln1_done)
            if i > 0:
                P.wait("pe", s_act, fm_evac)
            for j in range(16):
                P.op("pe", lambda e, j=j: e.transpose(
                    (psA if j < 8 else psB)[:, (j % 8) * 128:(j % 8) * 128 + 128],
                    ln1tm[:, j * 128:(j + 1) * 128], identb[:, :]),
                    inc=(s_pe if j == 15 else None))
            tp_done = s_pe.n
            P.wait("act", s_pe, tp_done)
            P.op("act", lambda e: e.copy(out=ln1fm[:, 0:1024], in_=psA[:, :]), inc=None)
            P.drain("act")
            P.op("act", lambda e: e.copy(out=ln1fm[:, 1024:2048], in_=psB[:, :]),
                 inc=s_act)
            fm_evac = s_act.n
            P.wait("pe", s_act, fm_evac)
            if i == 0:
                P.wait("pe", s_wa, AW)
            for c in range(16):
                for sidx, pst in enumerate((ps1, ps3, ps4)):
                    P.op("pe", lambda e, c=c, sidx=sidx, pst=pst: e.matmul(
                        pst[:, :], ln1fm[:, c * 128:(c + 1) * 128],
                        qaw_bf[:, c * 1536 + sidx * 512:c * 1536 + (sidx + 1) * 512],
                        start=(c == 0), stop=(c == 15)),
                        inc=(s_pe if (c == 15 and sidx == 2) else None))
            qa_done = s_pe.n
            P.wait("act", s_pe, qa_done)
            P.op("act", lambda e: e.copy(out=qatmp[:, 0:512], in_=ps1[:, :]), inc=None)
            P.op("act", lambda e: e.copy(out=qatmp[:, 512:1024], in_=ps3[:, :]), inc=None)
            P.drain("act")
            P.op("act", lambda e: e.copy(out=qatmp[:, 1024:1536], in_=ps4[:, :]),
                 inc=s_act)
            qatmp_done = s_act.n
            P.wait("dve", s_act, qatmp_done)
            for k3 in range(3):
                P.op("dve", lambda e, k3=k3: e.scalar_tensor_tensor(
                    out=sqscr, in0=qatmp[:, k3 * 512:(k3 + 1) * 512], scalar=1.0,
                    in1=qatmp[:, k3 * 512:(k3 + 1) * 512],
                    op0=ALU.mult, op1=ALU.mult, accum_out=sstat[:, k3:k3 + 1]),
                    inc=None)
                P.drain("dve")
            P.op("dve", lambda e: e.tensor_tensor(
                out=sstat[:, 3:4], in0=sstat[:, 0:1], in1=sstat[:, 1:2],
                op=ALU.add), inc=None)
            P.drain("dve")
            P.op("dve", lambda e: e.tensor_tensor(
                out=sstat[:, 4:5], in0=sstat[:, 3:4], in1=sstat[:, 2:3],
                op=ALU.add), inc=s_dve)
            qssq_done = s_dve.n
            P.wait("act", s_dve, qssq_done)
            P.op("act", lambda e: e.activation(out=rstdt[:, 3:4], in_=sstat[:, 4:5],
                                               func=AF.Ln, scale=1.0 / QL,
                                               bias=fc[:, 1:2]), inc=None)
            P.drain("act")
            P.op("act", lambda e: e.activation(out=rstdt[:, 2:3], in_=rstdt[:, 3:4],
                                               func=AF.Exp, scale=-0.5, bias=fc[:, 2:3]),
                 inc=s_act)
            qrstd_done = s_act.n
            P.wait("dve", s_act, qrstd_done)
            P.op("dve", lambda e: e.tensor_scalar(
                out=ln1tm[:, 0:1536], in0=qatmp, scalar1=rstdt[:, 2:3],
                scalar2=None, op0=ALU.mult), inc=s_dve)
            qan_done = s_dve.n
            P.wait("pe", s_dve, qan_done)
            P.wait("pe", s_act, qan_ev)
            for j in range(12):
                P.op("pe", lambda e, j=j: e.transpose(
                    (psA if j < 8 else psB)[:, (j % 8) * 128:(j % 8) * 128 + 128],
                    ln1tm[:, j * 128:(j + 1) * 128], identb[:, :]),
                    inc=(s_pe if j == 11 else None))
            qtp_done = s_pe.n
            P.wait("act", s_pe, qtp_done)
            P.op("act", lambda e, i=i: e.copy(
                out=qan_own.rearrange("p (c t) -> p c t", c=12)[:, 0:8, i * 128:(i + 1) * 128],
                in_=psA[:, :].rearrange("p (c t) -> p c t", c=8)), inc=None)
            P.drain("act")
            P.op("act", lambda e, i=i: e.copy(
                out=qan_own.rearrange("p (c t) -> p c t", c=12)[:, 8:12, i * 128:(i + 1) * 128],
                in_=psB[:, 0:512].rearrange("p (c t) -> p c t", c=4)),
                inc=s_act)
            qan_ev = s_act.n
            fm_evac = s_act.n

        barrier()
        qbw_bf = vbf(AR2, 0, 12 * 3072)
        qtm = [vbf(AR1, s * 1536, 3072) for s in range(2)]
        ropescr = v32(AR1, 3072, 2048)
        ssq_all = v32(AR1, 5120, 256)
        cs_sl = [(v32(AR1, 5376 + s * 1024, 512), v32(AR1, 5888 + s * 1024, 512))
                 for s in range(2)]
        scs = [s_cs0, s_cs1]
        sqs = [s_qs0, s_qs1]

        P.op("pool", lambda e: e.dma_start(
            out=AR2.ap().bitcast(BF16)[:, 0:12 * 3072].rearrange(
                "p (c n) -> p c n", c=12),
            in_=q_b_w.ap().rearrange("(c p) n -> p c n", p=128)),
            inc=s_wa, inck=16)
        AW2 = s_wa.n
        P.wait("dve", s_wa, AW2)
        for c in range(12):
            P.op("dve", lambda e, c=c: e.tensor_scalar(
                out=qbw_bf[:, c * 3072:(c + 1) * 3072],
                in0=qbw_bf[:, c * 3072:(c + 1) * 3072],
                scalar1=wqc_sb[:, c:c + 1], scalar2=None, op0=ALU.mult),
                inc=(s_dve if c == 11 else None))
        qbw_ready = s_dve.n

        psq4 = [g3.ap()[:, 0:384], g4.ap()[:, 0:384],
                g5.ap()[:, 0:384], g6.ap()[:, 0:384]]
        qtm_free = [None, None]
        cs_free = [None, None]
        qev_done = 0
        for tq in range(NT):
            sl = tq % 2
            if cs_free[sl] is not None:
                P.wait("sp", *cs_free[sl])
            P.op("sp", lambda e, tq=tq, sl=sl: e.dma_start(
                out=cs_sl[sl][0][:, :],
                in_=cos16_own[tq * 128:(tq + 1) * 128, :]), inc=scs[sl], inck=16)
            P.op("sp", lambda e, tq=tq, sl=sl: e.dma_start(
                out=cs_sl[sl][1][:, :],
                in_=sin16_own[tq * 128:(tq + 1) * 128, :]), inc=scs[sl], inck=16)
            cs_ld = scs[sl].n
            if tq == 0:
                P.wait("pe", s_dve, qbw_ready)
            P.wait("pe", s_act, qev_done)
            if qtm_free[sl] is not None:
                P.wait("pe", *qtm_free[sl])
            for half in range(2):
                if half == 1:
                    P.wait("pe", s_act, s_act.n)
                for kv4 in range(4):
                    kv8 = half * 4 + kv4
                    for l in range(12):
                        P.op("pe", lambda e, kv4=kv4, kv8=kv8, l=l, tq=tq: e.matmul(
                            psq4[kv4],
                            qan_own[:, l * OWN + tq * 128:l * OWN + (tq + 1) * 128],
                            qbw_bf[:, l * 3072 + kv8 * 384:l * 3072 + (kv8 + 1) * 384],
                            start=(l == 0), stop=(l == 11)),
                            inc=(s_pe if (kv4 == 3 and l == 11) else None))
                qb_done = s_pe.n
                P.wait("act", s_pe, qb_done)
                if half == 0:
                    P.wait("act", s_act, qev_done)
                for kv4 in range(4):
                    kv8 = half * 4 + kv4
                    if kv4 == 3:
                        P.drain("act")
                    P.op("act", lambda e, kv4=kv4, kv8=kv8, sl=sl: e.copy(
                        out=qtm[sl][:, kv8 * 384:(kv8 + 1) * 384], in_=psq4[kv4]),
                        inc=(s_act if kv4 == 3 else None))
            qev_done = s_act.n
            P.wait("dve", s_act, qev_done)
            P.wait("dve", scs[sl], cs_ld)
            qa16 = qtm[sl].rearrange("p (h d) -> p h d", h=16)
            qe = qa16[:, :, 128:192:2]
            qo = qa16[:, :, 129:192:2]
            ct = cs_sl[sl][0].rearrange("p (h d) -> p h d", h=16)
            st_ = cs_sl[sl][1].rearrange("p (h d) -> p h d", h=16)
            P.op("dve", lambda e, qe=qe, ct=ct: e.tensor_tensor(
                out=AR1[:, 3072:3584].rearrange("p (h d) -> p h d", h=16),
                in0=qe, in1=ct, op=ALU.mult), inc=None)
            P.op("dve", lambda e, qo=qo, st_=st_: e.tensor_tensor(
                out=AR1[:, 3584:4096].rearrange("p (h d) -> p h d", h=16),
                in0=qo, in1=st_, op=ALU.mult), inc=None)
            P.op("dve", lambda e, qo=qo, ct=ct: e.tensor_tensor(
                out=AR1[:, 4096:4608].rearrange("p (h d) -> p h d", h=16),
                in0=qo, in1=ct, op=ALU.mult), inc=None)
            P.op("dve", lambda e, qe=qe, st_=st_: e.tensor_tensor(
                out=AR1[:, 4608:5120].rearrange("p (h d) -> p h d", h=16),
                in0=qe, in1=st_, op=ALU.mult), inc=None)
            P.drain("dve")
            P.op("dve", lambda e, qe=qe: e.tensor_tensor(
                out=qe,
                in0=AR1[:, 3072:3584].rearrange("p (h d) -> p h d", h=16),
                in1=AR1[:, 3584:4096].rearrange("p (h d) -> p h d", h=16),
                op=ALU.subtract), inc=None)
            P.op("dve", lambda e, qo=qo: e.tensor_tensor(
                out=qo,
                in0=AR1[:, 4096:4608].rearrange("p (h d) -> p h d", h=16),
                in1=AR1[:, 4608:5120].rearrange("p (h d) -> p h d", h=16),
                op=ALU.add), inc=None)
            P.drain("dve")
            for hh in range(16):
                P.op("dve", lambda e, hh=hh, sl=sl, tq=tq: e.scalar_tensor_tensor(
                    out=ropescr[:, (hh % 4) * 512:(hh % 4) * 512 + 192],
                    in0=qtm[sl][:, hh * 192:(hh + 1) * 192], scalar=1.0,
                    in1=qtm[sl][:, hh * 192:(hh + 1) * 192],
                    op0=ALU.mult, op1=ALU.mult,
                    accum_out=ssq_all[:, tq * 16 + hh:tq * 16 + hh + 1]),
                    inc=(s_dve if hh == 15 else None))
                if hh % 4 == 3:
                    P.drain("dve")
            rope_done = s_dve.n
            cs_free[sl] = (s_dve, rope_done)
            P.wait("sp", s_dve, rope_done)
            P.op("sp", lambda e, tq=tq, sl=sl: e.dma_start(
                out=qspill[tq * 128:(tq + 1) * 128, :], in_=qtm[sl][:, :]),
                inc=sqs[sl], inck=16)
            qtm_free[sl] = (sqs[sl], sqs[sl].n)

        a2d = s_dve.n
        P.wait("dve", s_dve, a2d)
        for hh in range(16):
            P.op("dve", lambda e, hh=hh: e.reduce_max(
                out=colscr[:, 64 + hh:65 + hh],
                in_=ssq_all[:, 0:NT * 16].rearrange(
                    "p (t h) -> p h t", h=16)[:, hh, :],
                axis=AX.X), inc=(s_dve if hh == 15 else None))
        mq1 = s_dve.n
        P.wait("pe", s_dve, mq1)
        P.op("pe", lambda e: e.transpose(g5[0:16, 0:128], colscr[:, 64:80],
                                         identf[:, :]), inc=s_pe)
        mq2 = s_pe.n
        P.wait("dve", s_pe, mq2)
        P.op("dve", lambda e: e.reduce_max(out=colscr[0:16, 80:81],
                                           in_=g5.ap()[0:16, 0:128],
                                           axis=AX.X), inc=s_dve)
        mq3 = s_dve.n
        P.wait("pe", s_dve, mq3)
        P.op("pe", lambda e: e.transpose(g6[0:1, 0:16], colscr[0:16, 80:81],
                                         identf[0:16, 0:16]), inc=s_pe)
        mq4 = s_pe.n
        P.wait("act", s_pe, mq4)
        P.op("act", lambda e: e.activation(out=rowscr[0:1, 0:16],
                                           in_=g6.ap()[0:1, 0:16],
                                           func=AF.Ln, bias=fc[0:1, 3:4]), inc=None)
        P.drain("act")
        P.op("act", lambda e: e.activation(out=rowscr[0:1, 16:32],
                                           in_=rowscr[0:1, 0:16],
                                           func=AF.Exp, scale=0.5), inc=s_act)
        maxq_done = s_act.n

        barrier()
        kvcfm = [vbf(AR2, c * (T // 2), T) for c in range(4)]
        kropeF = vbf(AR2, 2 * T, T)
        kvaw_bf = vbf(AR1, 0, 16 * 576)
        kvtmp = v32(AR1, 8704, 512)
        ln1tmK = vbf(AR1, 9216, 2048)
        ln1fmK = vbf(AR1, 10240, 2048)
        kvc_tm = vbf(AR1, 11264, 512)
        krope_tm = vbf(AR1, 11520, 64)
        sqscrK = v32(AR1, 11552, 512)
        ropescrK = v32(AR1, 12064, 128)
        cosb = v32(AR1, 12288, NKT * 32)
        sinb = v32(AR1, 12288 + NKT * 32, NKT * 32)

        P.op("pool", lambda e: e.dma_start(
            out=AR1.ap().bitcast(BF16)[:, 0:16 * 576].rearrange(
                "p (c n) -> p c n", c=16),
            in_=kv_a_w.ap().rearrange("(c p) n -> p c n", p=128)),
            inc=s_wa, inck=16)
        AW3 = s_wa.n
        P.op("sp", lambda e: e.dma_start(
            out=cosb.rearrange("p (n c) -> p n c", c=32),
            in_=cos_all.ap().rearrange("(n p) c -> p n c", p=128)),
            inc=s_kv, inck=16)
        P.op("sp", lambda e: e.dma_start(
            out=sinb.rearrange("p (n c) -> p n c", c=32),
            in_=sin_all.ap().rearrange("(n p) c -> p n c", p=128)),
            inc=s_kv, inck=16)
        CS_LD = s_kv.n

        xt_free = [None, None]
        fm_evacK = 0
        kvst_done = 0
        for i in range(NKT):
            sl = i % 2
            if xt_free[sl] is not None:
                P.wait("sp", *xt_free[sl])
            P.op("sp", lambda e, i=i, sl=sl: e.dma_start(
                out=AR1[:, 4608 + sl * 2048:4608 + sl * 2048 + 2048],
                in_=x_batch[i * 128:(i + 1) * 128, :]), inc=sx[sl], inck=16)
            x_ld = sx[sl].n
            P.wait("dve", sx[sl], x_ld)
            for j in range(4):
                P.op("dve", lambda e, sl=sl, j=j: e.bn_stats(
                    out=bnst[:, j * 6:(j + 1) * 6],
                    in_=AR1[:, 4608 + sl * 2048 + j * 512:4608 + sl * 2048 + (j + 1) * 512]),
                    inc=None)
            P.drain("dve")
            P.op("dve", lambda e: e.bn_aggr(out=mvt[:, 0:2], in_=bnst[:, :]),
                 inc=s_dve)
            bn_done = s_dve.n
            P.wait("act", s_dve, bn_done)
            P.op("act", lambda e: e.activation(out=rstdt[:, 0:1], in_=mvt[:, 1:2],
                                               func=AF.Ln, bias=fc[:, 0:1]), inc=None)
            P.drain("act")
            P.op("act", lambda e: e.activation(out=rstdt[:, 1:2], in_=rstdt[:, 0:1],
                                               func=AF.Exp, scale=-0.5), inc=s_act)
            rstd_done = s_act.n
            P.wait("dve", s_act, rstd_done)
            P.op("dve", lambda e, sl=sl: e.tensor_scalar(
                out=ln1tmK, in0=AR1[:, 4608 + sl * 2048:4608 + sl * 2048 + 2048],
                scalar1=mvt[:, 0:1], scalar2=rstdt[:, 1:2],
                op0=ALU.subtract, op1=ALU.mult), inc=s_dve)
            ln1_done = s_dve.n
            xt_free[sl] = (s_dve, ln1_done)
            P.wait("pe", s_dve, ln1_done)
            if i > 0:
                P.wait("pe", s_act, fm_evacK)
            for j in range(16):
                P.op("pe", lambda e, j=j: e.transpose(
                    (psA if j < 8 else psB)[:, (j % 8) * 128:(j % 8) * 128 + 128],
                    ln1tmK[:, j * 128:(j + 1) * 128], identb[:, :]),
                    inc=(s_pe if j == 15 else None))
            tp_done = s_pe.n
            P.wait("act", s_pe, tp_done)
            P.op("act", lambda e: e.copy(out=ln1fmK[:, 0:1024], in_=psA[:, :]), inc=None)
            P.drain("act")
            P.op("act", lambda e: e.copy(out=ln1fmK[:, 1024:2048], in_=psB[:, :]),
                 inc=s_act)
            fm_evacK = s_act.n
            P.wait("pe", s_act, fm_evacK)
            if i == 0:
                P.wait("pe", s_wa, AW3)
            for c in range(16):
                P.op("pe", lambda e, c=c: e.matmul(
                    ps1[:, :], ln1fmK[:, c * 128:(c + 1) * 128],
                    kvaw_bf[:, c * 576:c * 576 + 512],
                    start=(c == 0), stop=(c == 15)), inc=None)
            for c in range(16):
                P.op("pe", lambda e, c=c: e.matmul(
                    ps2[:, 0:64], ln1fmK[:, c * 128:(c + 1) * 128],
                    kvaw_bf[:, c * 576 + 512:c * 576 + 576],
                    start=(c == 0), stop=(c == 15)),
                    inc=(s_pe if c == 15 else None))
            kva_done = s_pe.n
            P.wait("act", s_pe, kva_done)
            P.op("act", lambda e: e.copy(out=kvtmp, in_=ps1[:, :]), inc=s_act)
            kvtmp_done = s_act.n
            P.wait("dve", s_act, kvtmp_done)
            P.op("dve", lambda e: e.scalar_tensor_tensor(
                out=sqscrK, in0=kvtmp, scalar=1.0, in1=kvtmp,
                op0=ALU.mult, op1=ALU.mult, accum_out=rstdt[:, 2:3]), inc=s_dve)
            kvssq_done = s_dve.n
            P.wait("act", s_dve, kvssq_done)
            P.op("act", lambda e: e.activation(out=rstdt[:, 3:4], in_=rstdt[:, 2:3],
                                               func=AF.Ln, scale=1.0 / KVL,
                                               bias=fc[:, 1:2]), inc=None)
            P.drain("act")
            P.op("act", lambda e: e.activation(out=rstdt[:, 2:3], in_=rstdt[:, 3:4],
                                               func=AF.Exp, scale=-0.5), inc=s_act)
            kvrstd_done = s_act.n
            P.wait("dve", s_act, kvrstd_done)
            P.op("dve", lambda e: e.tensor_scalar(
                out=kvc_tm, in0=kvtmp, scalar1=rstdt[:, 2:3], scalar2=None,
                op0=ALU.mult), inc=s_dve)
            kvc_done = s_dve.n
            if i == 0:
                P.wait("dve", s_kv, CS_LD)
            ctK = cosb[:, i * 32:(i + 1) * 32]
            stK = sinb[:, i * 32:(i + 1) * 32]
            ke = ps2[:, 0:64:2]
            ko = ps2[:, 1:64:2]
            P.op("dve", lambda e, ke=ke, ctK=ctK: e.tensor_tensor(
                out=ropescrK[:, 0:32], in0=ke, in1=ctK, op=ALU.mult), inc=None)
            P.op("dve", lambda e, ko=ko, stK=stK: e.tensor_tensor(
                out=ropescrK[:, 32:64], in0=ko, in1=stK, op=ALU.mult), inc=None)
            P.drain("dve")
            P.op("dve", lambda e: e.tensor_tensor(
                out=krope_tm[:, 0:64:2], in0=ropescrK[:, 0:32],
                in1=ropescrK[:, 32:64], op=ALU.subtract), inc=None)
            P.op("dve", lambda e, ko=ko, ctK=ctK: e.tensor_tensor(
                out=ropescrK[:, 64:96], in0=ko, in1=ctK, op=ALU.mult), inc=None)
            P.op("dve", lambda e, ke=ke, stK=stK: e.tensor_tensor(
                out=ropescrK[:, 96:128], in0=ke, in1=stK, op=ALU.mult), inc=None)
            P.drain("dve")
            P.op("dve", lambda e: e.tensor_tensor(
                out=krope_tm[:, 1:64:2], in0=ropescrK[:, 64:96],
                in1=ropescrK[:, 96:128], op=ALU.add), inc=s_dve)
            krope_done = s_dve.n
            P.wait("pe", s_dve, krope_done)
            if i > 0:
                P.wait("pe", s_act, kvst_done)
            for j in range(4):
                P.op("pe", lambda e, j=j: e.transpose(
                    psA[:, j * 128:(j + 1) * 128], kvc_tm[:, j * 128:(j + 1) * 128],
                    identb[:, :]), inc=None)
            P.op("pe", lambda e: e.transpose(
                psA[0:64, 512:640], krope_tm[:, 0:64], identb[:, :]),
                inc=s_pe)
            kvtp_done = s_pe.n
            P.wait("act", s_pe, kvtp_done)
            for cc in range(4):
                P.op("act", lambda e, cc=cc, i=i: e.copy(
                    out=kvcfm[cc][:, i * 128:(i + 1) * 128],
                    in_=psA[:, cc * 128:(cc + 1) * 128]), inc=None)
            P.drain("act")
            P.op("act", lambda e, i=i: e.copy(
                out=kropeF[0:64, i * 128:(i + 1) * 128],
                in_=psA[0:64, 512:640]), inc=s_act)
            kvst_done = s_act.n

        barrier()
        V2 = vbf(AR1, 0, NKT * 256)
        K_h = [vbf(AR1, 8192 + s * 4096, T) for s in range(2)]
        kvb2f = vbf(AR1, 16384, 4 * 512)
        masksl = vbf(AR1, 17408, 8 * 512)
        qtw = [vbf(AR1, 19456 + s * 384, 768) for s in range(2)]
        Qn_w = [vbf(AR1, 20224 + s * 256, 512) for s in range(2)]
        Qr_w = [vbf(AR1, 20736 + s * 256, 512) for s in range(2)]
        expS = [vbf(AR1, 21248 + s * 512, 1024) for s in range(2)]
        bcastf = v32(AR1, 22272, 512)
        recf = bcastf
        dacc = [v32(AR1, 22784 + s * 512, 512) for s in range(2)]
        outh_sl = [vbf(AR1, 23808 + s * 256, 512) for s in range(2)]
        sqr = [s_qr0, s_qr1]
        assert 23808 + 512 <= 24576

        SQRT_QK = math.sqrt(float(QK))
        exp_bank_free = [0, 0]
        v2_evac = 0
        kh_evac = 0
        g5_free = 0
        g12_expfree = [0, 0]
        expS_free = [0, 0]
        qtw_free = [None, None]
        qw_tp_ev = 0
        mask_used = 0
        kvb_fold_done = 0
        kvb_pe_used = 0
        outh_free = [None, None]
        psO_free = [0, 0]
        g3_ev = 0
        for hp in range(8):
            if hp == 0:
                P.op("pool", lambda e, hp=hp: e.dma_start(
                    out=kvb2f.rearrange("p (c n) -> p c n", c=4),
                    in_=kv_b_w.ap().rearrange("(c p) n -> p c n", p=128)[
                        :, :, hp * 512:(hp + 1) * 512]), inc=s_kb, inck=16)
            KB = s_kb.n
            P.wait("dve", s_kb, KB)
            for cc in range(4):
                P.op("dve", lambda e, cc=cc: e.tensor_scalar(
                    out=kvb2f[:, cc * 512:(cc + 1) * 512],
                    in0=kvb2f[:, cc * 512:(cc + 1) * 512],
                    scalar1=wkvc_sb[:, cc:cc + 1], scalar2=None, op0=ALU.mult),
                    inc=(s_dve if cc == 3 else None))
            kvb_fold_done = s_dve.n
            P.wait("pe", s_dve, kvb_fold_done)
            P.wait("pe", s_dve, g5_free)
            for kt in range(NKT):
                psV = (g3, g4)[kt % 2]
                P.wait("pe", s_act, exp_bank_free[kt % 2])
                for cc in range(4):
                    P.op("pe", lambda e, kt=kt, cc=cc, psV=psV: e.matmul(
                        psV[:, 0:256],
                        kvcfm[cc][:, kt * 128:(kt + 1) * 128],
                        kvb2f[:, cc * 512:(cc + 1) * 512].rearrange(
                            "p (h z) -> p h z", h=2)[:, :, 128:256],
                        start=(cc == 0), stop=(cc == 3)),
                        inc=(s_pe if cc == 3 else None))
                vkt = s_pe.n
                P.wait("act", s_pe, vkt)
                P.op("act", lambda e, kt=kt, psV=psV: e.copy(
                    out=V2[:, kt * 256:(kt + 1) * 256], in_=psV.ap()[:, 0:256]),
                    inc=s_act)
                v2_evac = s_act.n
                exp_bank_free[kt % 2] = s_act.n
            for hl in range(2):
                for win in range(T // 512):
                    psK = (g3, g4)[win % 2]
                    P.wait("pe", s_act, exp_bank_free[win % 2])
                    for cc in range(4):
                        P.op("pe", lambda e, win=win, cc=cc, psK=psK, hl=hl: e.matmul(
                            psK[:, :],
                            kvb2f[:, cc * 512 + hl * 256:cc * 512 + hl * 256 + 128],
                            kvcfm[cc][:, win * 512:(win + 1) * 512],
                            start=(cc == 0), stop=(cc == 3)),
                            inc=(s_pe if cc == 3 else None))
                    kkt = s_pe.n
                    P.wait("act", s_pe, kkt)
                    P.op("act", lambda e, win=win, psK=psK, hl=hl: e.copy(
                        out=K_h[hl][:, win * 512:(win + 1) * 512], in_=psK.ap()),
                        inc=s_act)
                    kh_evac = s_act.n
                    exp_bank_free[win % 2] = s_act.n
            kvb_pe_used = s_pe.n
            if hp + 1 < 8:
                P.wait("pool", s_pe, kvb_pe_used)
                P.wait("pool", s_dve, kvb_fold_done)
                P.op("pool", lambda e, hp=hp: e.dma_start(
                    out=kvb2f.rearrange("p (c n) -> p c n", c=4),
                    in_=kv_b_w.ap().rearrange("(c p) n -> p c n", p=128)[
                        :, :, (hp + 1) * 512:(hp + 2) * 512]), inc=s_kb, inck=16)
            P.wait("dve", s_act, kh_evac)
            if hp == 0:
                P.op("dve", lambda e: e.memset(colscr[:, 8:9], 0.0), inc=None)
                P.drain("dve")
                P.op("dve", lambda e: e.reduce_max(
                    out=colscr[0:64, 8:9], in_=kropeF[0:64, :], axis=AX.X,
                    apply_absolute_value=True), inc=None)
                P.drain("dve")
                P.wait("dve", s_act, maxq_done)
            for hl in range(2):
                h = hp * 2 + hl
                P.op("dve", lambda e, hl=hl: e.reduce_max(
                    out=colscr[:, 9:10], in_=K_h[hl][:, :], axis=AX.X,
                    apply_absolute_value=True), inc=None)
                P.drain("dve")
                P.op("dve", lambda e: e.tensor_tensor(
                    out=colscr[:, 10:11], in0=colscr[:, 9:10],
                    in1=colscr[:, 8:9], op=ALU.max), inc=s_dve)
                mk1 = s_dve.n
                P.wait("pe", s_dve, mk1)
                P.wait("pe", s_act, s_act.n)
                P.op("pe", lambda e: e.transpose(
                    g6[0:1, 0:128], colscr[:, 10:11], identf[:, :]), inc=s_pe)
                mk2 = s_pe.n
                P.wait("dve", s_pe, mk2)
                P.op("dve", lambda e: e.reduce_max(
                    out=rowscr[0:1, 40:41], in_=g6.ap()[0:1, 0:128], axis=AX.X),
                    inc=None)
                P.drain("dve")
                P.op("dve", lambda e, h=h: e.tensor_tensor(
                    out=rowscr[0:1, 41:42], in0=rowscr[0:1, 16 + h:17 + h],
                    in1=rowscr[0:1, 40:41], op=ALU.mult), inc=None)
                P.drain("dve")
                P.op("dve", lambda e, hl=hl: e.tensor_scalar(
                    out=nb_bf[0:1, hl:hl + 1], in0=rowscr[0:1, 41:42],
                    scalar1=-SQRT_QK, scalar2=None, op0=ALU.mult), inc=s_dve)
                negc_done = s_dve.n
                P.wait("pe", s_dve, negc_done)
                P.wait("pe", s_dve, psO_free[1])
                P.op("pe", lambda e, hl=hl: e.matmul(
                    g6[:, 0:1], onesb[0:1, 0:128], nb_bf[0:1, hl:hl + 1],
                    start=True, stop=True), inc=s_pe)
                negb_mm = s_pe.n
                P.wait("act", s_pe, negb_mm)
                P.op("act", lambda e, hl=hl: e.copy(out=colscr[:, 12 + hl:13 + hl],
                                                    in_=g6.ap()[:, 0:1]), inc=s_act)
            negc_bc = s_act.n

            for w in range(NCH):
                lim = LIMS[w]
                kmax2 = lim // 2
                mstart = lim - 16
                for hl in range(2):
                    h = hp * 2 + hl
                    sl = hl
                    if qtw_free[sl] is not None:
                        P.wait("sp", *qtw_free[sl])
                    P.op("sp", lambda e, h=h, w=w, sl=sl: e.dma_start(
                        out=qtw[sl].rearrange("p (tt d) -> p tt d", tt=4),
                        in_=qspill[w * 512:(w + 1) * 512,
                                   h * 192:(h + 1) * 192].rearrange(
                            "(tt p) d -> p tt d", p=128)), inc=sqr[sl], inck=16)
                    q_ld = sqr[sl].n
                    P.wait("pe", sqr[sl], q_ld)
                    P.wait("pe", s_act, qw_tp_ev)
                    if w == 0 and hl == 0:
                        P.wait("pe", s_act, max(kh_evac, v2_evac, negc_bc))
                        P.wait("pe", s_dve, g5_free)
                    for tt in range(4):
                        P.op("pe", lambda e, tt=tt, sl=sl: e.transpose(
                            psA[:, tt * 128:(tt + 1) * 128],
                            qtw[sl][:, tt * 192:tt * 192 + 128], identb[:, :]),
                            inc=None)
                    for tt in range(4):
                        P.op("pe", lambda e, tt=tt, sl=sl: e.transpose(
                            psB[0:64, tt * 128:(tt + 1) * 128],
                            qtw[sl][:, tt * 192 + 128:(tt + 1) * 192], identb[:, :]),
                            inc=(s_pe if tt == 3 else None))
                    qtp2 = s_pe.n
                    qtw_free[sl] = (s_pe, qtp2)
                    P.wait("act", s_pe, qtp2)
                    P.op("act", lambda e, hl=hl: e.copy(out=Qn_w[hl][:, :],
                                                        in_=psA[:, 0:512]),
                         inc=None)
                    P.drain("act")
                    P.op("act", lambda e, hl=hl: e.copy(out=Qr_w[hl][0:64, :],
                                                        in_=psB[0:64, 0:512]),
                         inc=s_act)
                    qw_tp_ev = s_act.n

                P.wait("pe", s_act, qw_tp_ev)
                for dw in range(16):
                    P.op("pe", lambda e: e.matmul(
                        g3[:, :], onesb[:, :], hsu_sb[:, 0:512],
                        start=True, stop=True),
                        inc=(s_pe if dw == 15 else None))
                warm_done = s_pe.n
                mk_ld = [0, 0]
                att_scores = {}
                att_expd = {}
                att_maskd = {}
                att_dacc = {}
                psS_h = (g1, g2)
                psO_h = (g5, g6)
                mhalf_loaded = [False, False]

                def load_mask_half(mh, w=w):
                    P.wait("sp", s_dve, s_dve.n)
                    P.op("sp", lambda e, w=w, mh=mh: e.dma_start(
                        out=masksl[:, :],
                        in_=masks_in.ap()[:, w * 8192 + mh * 4096:
                                          w * 8192 + (mh + 1) * 4096]),
                        inc=s_mk, inck=16)
                    mk_ld[mh] = s_mk.n

                def emit_attnv(hl, j, kmax2=kmax2):
                    first = (j == 0)
                    last = (j == kmax2 - 1)
                    P.wait("pe", s_act, att_expd[(hl, j)])
                    if first:
                        P.wait("pe", s_dve, psO_free[hl])
                    if (hl, j) in att_maskd:
                        P.wait("pe", s_dve, att_maskd[(hl, j)])
                    for sub in range(2):
                        kt = 2 * j + sub
                        P.op("pe", lambda e, kt=kt, hl=hl, sub=sub,
                             first=first, last=last: e.matmul(
                            psO_h[hl][:, :],
                            V2[:, kt * 256 + hl * 128:kt * 256 + (hl + 1) * 128],
                            expS[hl][:, sub * 512:(sub + 1) * 512],
                            start=(first and sub == 0),
                            stop=(last and sub == 1)),
                            inc=(s_pe if sub == 1 else None))
                    expS_free[hl] = s_pe.n

                for kg in range(kmax2):
                    need_mh0 = (2 * kg + 1 >= mstart) and not mhalf_loaded[0]
                    need_mh1 = (2 * kg + 1 >= mstart + 8) and not mhalf_loaded[1]
                    if need_mh0:
                        load_mask_half(0)
                        mhalf_loaded[0] = True
                    if need_mh1:
                        load_mask_half(1)
                        mhalf_loaded[1] = True
                    for hl in range(2):
                        psS = psS_h[hl]
                        if kg >= 1:
                            emit_attnv(hl, kg - 1)
                        P.wait("pe", s_act, g12_expfree[hl])
                        if kg == 0:
                            P.wait("pe", s_act, qw_tp_ev)
                        for sub in range(2):
                            kt = 2 * kg + sub
                            P.op("pe", lambda e, kt=kt, psS=psS, sub=sub, hl=hl: e.matmul(
                                psS[:, sub * 512:(sub + 1) * 512],
                                K_h[hl][:, kt * 128:(kt + 1) * 128], Qn_w[hl][:, :],
                                start=True, stop=False), inc=None)
                            P.op("pe", lambda e, kt=kt, psS=psS, sub=sub, hl=hl: e.matmul(
                                psS[:, sub * 512:(sub + 1) * 512],
                                kropeF[0:64, kt * 128:(kt + 1) * 128],
                                Qr_w[hl][0:64, :],
                                start=False, stop=True),
                                inc=(s_pe if sub == 1 else None))
                        att_scores[(hl, kg)] = s_pe.n
                        P.wait("act", s_pe, att_scores[(hl, kg)])
                        P.wait("act", s_pe, expS_free[hl])
                        if (hl, kg - 1) in att_dacc:
                            P.wait("act", s_dve, att_dacc[(hl, kg - 1)])
                        P.op("act", lambda e, hl=hl, psS=psS: e.activation(
                            out=expS[hl][:, :], in_=psS[:, :], func=AF.Exp,
                            bias=colscr[:, 12 + hl:13 + hl]), inc=s_act)
                        att_expd[(hl, kg)] = s_act.n
                        g12_expfree[hl] = s_act.n
                        diag_subs = [sub for sub in range(2)
                                     if 2 * kg + sub >= mstart]
                        P.wait("dve", s_act, att_expd[(hl, kg)])
                        if diag_subs:
                            for sub in diag_subs:
                                kt = 2 * kg + sub
                                pat = kt - mstart
                                P.wait("dve", s_mk, mk_ld[pat // 8])
                                P.op("dve", lambda e, hl=hl, sub=sub, pat=pat: e.tensor_tensor(
                                    out=expS[hl][:, sub * 512:(sub + 1) * 512],
                                    in0=expS[hl][:, sub * 512:(sub + 1) * 512],
                                    in1=masksl[:, (pat % 8) * 512:(pat % 8) * 512 + 512],
                                    op=ALU.mult),
                                    inc=s_dve)
                            att_maskd[(hl, kg)] = s_dve.n
                            P.drain("dve")
                        if kg == 0:
                            P.op("dve", lambda e, hl=hl: e.tensor_copy(
                                out=dacc[hl][:, :], in_=expS[hl][:, 0:512]),
                                inc=None)
                        else:
                            P.op("dve", lambda e, hl=hl: e.tensor_tensor(
                                out=dacc[hl][:, :], in0=dacc[hl][:, :],
                                in1=expS[hl][:, 0:512], op=ALU.add), inc=None)
                        P.drain("dve")
                        P.op("dve", lambda e, hl=hl: e.tensor_tensor(
                            out=dacc[hl][:, :], in0=dacc[hl][:, :],
                            in1=expS[hl][:, 512:1024], op=ALU.add), inc=s_dve)
                        P.drain("dve")
                        att_dacc[(hl, kg)] = s_dve.n
                mask_used = s_dve.n
                for hl in range(2):
                    emit_attnv(hl, kmax2 - 1)
                for hl in range(2):
                    h = hp * 2 + hl
                    attv_done = expS_free[hl]
                    P.wait("pe", s_dve, att_dacc[(hl, kmax2 - 1)])
                    P.wait("pe", s_act, qw_tp_ev)
                    P.wait("pe", s_act, g3_ev)
                    P.op("pe", lambda e, hl=hl: e.matmul(
                        g3[0:1, 0:512], onesf[:, 0:1], dacc[hl][:, :],
                        start=True, stop=True), inc=s_pe)
                    dn_red = s_pe.n
                    P.wait("dve", s_pe, dn_red)
                    P.drain("dve")
                    P.op("dve", lambda e: e.reciprocal(
                        out=recf[0:1, :], in_=g3.ap()[0:1, 0:512]), inc=s_dve)
                    P.drain("dve")
                    P.op("dve", lambda e: e.tensor_copy(
                        out=recb[0:1, :], in_=recf[0:1, :]), inc=s_dve)
                    recb_done = s_dve.n
                    P.wait("pe", s_dve, recb_done)
                    P.op("pe", lambda e: e.matmul(
                        g3[:, 0:512], onesb[0:1, 0:128],
                        recb[0:1, :], start=True, stop=True), inc=s_pe)
                    bc_mm = s_pe.n
                    P.wait("act", s_pe, bc_mm)
                    P.op("act", lambda e: e.copy(out=bcastf[:, :],
                                                 in_=g3.ap()[:, 0:512]), inc=s_act)
                    bc_ev = s_act.n
                    g3_ev = s_act.n
                    P.wait("dve", s_act, bc_ev)
                    P.wait("dve", s_pe, attv_done)
                    if outh_free[hl] is not None:
                        P.wait("dve", *outh_free[hl])
                    P.op("dve", lambda e, hl=hl: e.tensor_tensor(
                        out=outh_sl[hl][:, :], in0=psO_h[hl].ap()[:, :],
                        in1=bcastf[:, :], op=ALU.mult), inc=s_dve)
                    g5_free = s_dve.n
                    psO_free[hl] = s_dve.n
                    P.wait("sp", s_dve, g5_free)
                    sc2h = (s_c20, s_c21)[hl]
                    P.op("sp", lambda e, h=h, w=w, hl=hl: e.dma_start(
                        out=ospill[h * 128:(h + 1) * 128, w * 512:(w + 1) * 512],
                        in_=outh_sl[hl][:, :]), inc=sc2h, inck=16)
                    outh_free[hl] = (sc2h, sc2h.n)
                qw_tp_ev = s_act.n

        barrier()
        oww = vbf(AR1, 0, 16 * 2048)
        ohw = [vbf(AR1, 16384 + s * 2048, 16 * 256) for s in range(2)]
        aof32 = v32(AR2, 12288, 4096)
        xb2 = [v32(AR2, s * 2048, 2048) for s in range(2)]
        atm = v32(AR2, 4096, 2048)
        lnout = v32(AR2, 6144, 2048)
        x2t = [v32(AR2, 8192 + s * 2048, 2048) for s in range(2)]
        soh = [s_oh0, s_oh1]
        sx2 = [s_x2r0, s_x2r1]

        P.op("pool", lambda e: e.dma_start(
            out=oww.rearrange("p (c n) -> p c n", c=16),
            in_=o_w.ap().rearrange("(c p) n -> p c n", p=128)),
            inc=s_ow, inck=16)
        OW_LD = s_ow.n
        def psdc(dc):
            if dc < 4:
                return g1, (dc % 4) * 256
            if dc < 8:
                return g2, (dc % 4) * 256
            return (g3, g4, g5, g6)[(dc - 8) // 2], ((dc - 8) % 2) * 256

        ohw_free = [None, None]
        x2t_free = [None, None]
        xb2_free = [None, None]
        b2_ev = 0
        for w in range(OWN // 256):
            sl = w % 2
            if ohw_free[sl] is not None:
                P.wait("sp", *ohw_free[sl])
            P.op("sp", lambda e, w=w, sl=sl: e.dma_start(
                out=ohw[sl].rearrange("p (hh t) -> p hh t", hh=16),
                in_=ospill[:, :].rearrange("(hh p) t -> p hh t", p=128)[
                    :, :, w * 256:(w + 1) * 256]), inc=soh[sl], inck=16)
            oh_ld = soh[sl].n
            if w > 0:
                P.wait("pe", s_act, b2_ev)
                for dw in range(16):
                    P.op("pe", lambda e: e.matmul(
                        g1[:, 0:512], onesb[:, :], hsu_sb[:, 0:512],
                        start=True, stop=True),
                        inc=(s_pe if dw == 15 else None))
            P.wait("pe", soh[sl], oh_ld)
            if w == 0:
                P.wait("pe", s_ow, OW_LD)
            for dc in range(16):
                pst, col = psdc(dc)
                for hh in range(16):
                    P.op("pe", lambda e, dc=dc, hh=hh, pst=pst, col=col, sl=sl: e.matmul(
                        pst[:, col:col + 256],
                        oww[:, hh * 2048 + dc * 128:hh * 2048 + (dc + 1) * 128],
                        ohw[sl][:, hh * 256:(hh + 1) * 256],
                        start=(hh == 0), stop=(hh == 15)),
                        inc=(s_pe if (dc == 15 and hh == 15) else None))
            ow_done = s_pe.n
            ohw_free[sl] = (s_pe, ow_done)
            P.wait("act", s_pe, ow_done)
            P.op("act", lambda e: e.copy(out=aof32[:, 0:1024], in_=g1[:, :]), inc=None)
            P.op("act", lambda e: e.copy(out=aof32[:, 1024:2048], in_=g2[:, :]), inc=None)
            P.op("act", lambda e: e.copy(out=aof32[:, 2048:2560], in_=g3[:, :]), inc=None)
            P.op("act", lambda e: e.copy(out=aof32[:, 2560:3072], in_=g4[:, :]), inc=None)
            P.op("act", lambda e: e.copy(out=aof32[:, 3072:3584], in_=g5[:, :]), inc=None)
            P.drain("act")
            P.op("act", lambda e: e.copy(out=aof32[:, 3584:4096], in_=g6[:, :]),
                 inc=s_act)
            b2_ev = s_act.n
            for tau in range(2):
                i_own = w * 2 + tau
                if xb2_free[sl] is not None and tau == 0:
                    P.wait("sp", *xb2_free[sl])
                if tau == 0:
                    P.op("sp", lambda e, w=w, sl=sl: e.dma_start(
                        out=AR2[:, sl * 2048:sl * 2048 + 2048],
                        in_=x_own[w * 256:w * 256 + 128, :]), inc=sx[0] if False else sx2[sl], inck=16)
                else:
                    P.op("sp", lambda e, w=w, sl=sl: e.dma_start(
                        out=AR2[:, sl * 2048:sl * 2048 + 2048],
                        in_=x_own[w * 256 + 128:w * 256 + 256, :]), inc=sx2[sl], inck=16)
                xld = sx2[sl].n
                P.wait("pe", s_act, b2_ev)
                if tau == 1 or w > 0:
                    P.wait("pe", s_dve, s_dve.n)
                for dc in range(16):
                    P.op("pe", lambda e, dc=dc, tau=tau: e.transpose(
                        (g1 if dc < 8 else g2)[:, (dc % 8) * 128:(dc % 8) * 128 + 128],
                        aof32[:, dc * 256 + tau * 128:dc * 256 + (tau + 1) * 128],
                        identf[:, :]), inc=(s_pe if dc == 15 else None))
                tp2 = s_pe.n
                P.wait("act", s_pe, tp2)
                P.op("act", lambda e: e.copy(out=atm[:, 0:1024], in_=g1[:, :]), inc=None)
                P.drain("act")
                P.op("act", lambda e: e.copy(out=atm[:, 1024:2048], in_=g2[:, :]),
                     inc=s_act)
                atm_ev = s_act.n
                P.wait("dve", s_act, atm_ev)
                for j in range(4):
                    P.op("dve", lambda e, j=j: e.bn_stats(
                        out=bnst[:, j * 6:(j + 1) * 6],
                        in_=atm[:, j * 512:(j + 1) * 512]), inc=None)
                P.drain("dve")
                P.op("dve", lambda e: e.bn_aggr(out=mvt[:, 0:2], in_=bnst[:, :]),
                     inc=s_dve)
                bn2 = s_dve.n
                P.wait("act", s_dve, bn2)
                P.op("act", lambda e: e.activation(out=rstdt[:, 0:1], in_=mvt[:, 1:2],
                                                   func=AF.Ln, bias=fc[:, 0:1]), inc=None)
                P.drain("act")
                P.op("act", lambda e: e.activation(out=rstdt[:, 1:2], in_=rstdt[:, 0:1],
                                                   func=AF.Exp, scale=-0.5), inc=s_act)
                rstd2 = s_act.n
                P.wait("dve", s_act, rstd2)
                P.op("dve", lambda e: e.tensor_scalar(
                    out=lnout[:, :], in0=atm[:, :], scalar1=mvt[:, 0:1],
                    scalar2=rstdt[:, 1:2], op0=ALU.subtract, op1=ALU.mult),
                    inc=None)
                P.drain("dve")
                P.wait("dve", sx2[sl], xld)
                if x2t_free[sl] is not None:
                    P.wait("dve", *x2t_free[sl])
                P.op("dve", lambda e, sl=sl: e.tensor_tensor(
                    out=x2t[sl][:, :], in0=lnout[:, :],
                    in1=AR2[:, sl * 2048:sl * 2048 + 2048], op=ALU.add),
                    inc=s_dve)
                x2_done = s_dve.n
                xb2_free[sl] = (s_dve, x2_done)
                P.wait("sp", s_dve, x2_done)
                P.op("sp", lambda e, i_own=i_own, sl=sl: e.dma_start(
                    out=x2d[i_own * 128:(i_own + 1) * 128, :], in_=x2t[sl][:, :]),
                    inc=s_out, inck=16)
                x2t_free[sl] = (s_out, s_out.n)
            b2_ev = s_act.n

        barrier()
        wdnat = [vbf(AR1, s * 4096, 8192) for s in range(2)]
        wdTs = [vbf(AR1, 8192 + s * 4096, 8192) for s in range(2)]
        swd = [s_wd0, s_wd1]
        wdn_free = [None, None]
        wdts_free = [None, None]
        tpd_bank = [0, 0]
        for ob in range(16):
            sl = ob % 2
            if wdn_free[sl] is not None:
                P.wait("pool", *wdn_free[sl])
            P.op("pool", lambda e, ob=ob, sl=sl: e.dma_start(
                out=wdnat[sl][:, :], in_=w_down[ob * 128:(ob + 1) * 128, :]),
                inc=swd[sl], inck=16)
            wd_ld = swd[sl].n
            P.wait("pe", swd[sl], wd_ld)
            if wdts_free[sl] is not None:
                P.wait("pe", *wdts_free[sl])
            tpd = 0
            for grp in range(8):
                pst = (psA, psB)[grp % 2]
                P.wait("pe", s_act, tpd_bank[grp % 2])
                for j in range(8):
                    fc_ = grp * 8 + j
                    P.op("pe", lambda e, fc_=fc_, j=j, pst=pst, sl=sl: e.transpose(
                        pst[:, j * 128:(j + 1) * 128],
                        wdnat[sl][:, fc_ * 128:(fc_ + 1) * 128], identb[:, :]),
                        inc=(s_pe if j == 7 else None))
                tg = s_pe.n
                P.wait("act", s_pe, tg)
                P.op("act", lambda e, grp=grp, pst=pst, sl=sl: e.copy(
                    out=wdTs[sl][:, grp * 1024:(grp + 1) * 1024], in_=pst[:, :]),
                    inc=s_act)
                tpd = s_act.n
                tpd_bank[grp % 2] = s_act.n
            wdn_free[sl] = (s_pe, s_pe.n)
            swt = [s_wt0, s_wt1]
            P.wait("sp", s_act, tpd)
            P.op("sp", lambda e, ob=ob, sl=sl: e.dma_start(
                out=wdTd[:, ob * 128:(ob + 1) * 128].rearrange(
                    "(fc p) o -> p fc o", p=128),
                in_=wdTs[sl].rearrange("p (fc o) -> p fc o", fc=64)),
                inc=swt[sl], inck=16)
            wdts_free[sl] = (swt[sl], swt[sl].n)

        barrier()
        NWC = OWN // 512
        h_sb = vbf(AR1, 0, 16 * OWN)
        ln3fm = vbf(AR1, 16384, 16 * 512)
        ln3tm = vbf(AR1, 20480, 2048)
        xb3 = v32(AR1, 21504, 2048)
        fA = [v32(AR2, c * 512, 512) for c in range(16)]
        fB = [v32(AR2, 8192 + c * 512, 512) for c in range(16)]

        xb3_free = None
        ln3_ev = 0
        fw_ev_bank = [0, 0]
        for w in range(NWC):
            for tau in range(4):
                i_own = w * 4 + tau
                if xb3_free is not None:
                    P.wait("sp", *xb3_free)
                P.op("sp", lambda e, i_own=i_own: e.dma_start(
                    out=xb3[:, :], in_=x2d[i_own * 128:(i_own + 1) * 128, :]),
                    inc=s_x2r0, inck=16)
                x2ld = s_x2r0.n
                P.wait("dve", s_x2r0, x2ld)
                for j in range(4):
                    P.op("dve", lambda e, j=j: e.bn_stats(
                        out=bnst[:, j * 6:(j + 1) * 6],
                        in_=xb3[:, j * 512:(j + 1) * 512]), inc=None)
                P.drain("dve")
                P.op("dve", lambda e: e.bn_aggr(out=mvt[:, 0:2], in_=bnst[:, :]),
                     inc=s_dve)
                bn3 = s_dve.n
                P.wait("act", s_dve, bn3)
                P.op("act", lambda e: e.activation(out=rstdt[:, 0:1], in_=mvt[:, 1:2],
                                                   func=AF.Ln, bias=fc[:, 0:1]), inc=None)
                P.drain("act")
                P.op("act", lambda e: e.activation(out=rstdt[:, 1:2], in_=rstdt[:, 0:1],
                                                   func=AF.Exp, scale=-0.5), inc=s_act)
                rstd3 = s_act.n
                P.wait("dve", s_act, rstd3)
                P.op("dve", lambda e: e.tensor_scalar(
                    out=ln3tm[:, :], in0=xb3[:, :], scalar1=mvt[:, 0:1],
                    scalar2=rstdt[:, 1:2], op0=ALU.subtract, op1=ALU.mult),
                    inc=s_dve)
                ln3_done = s_dve.n
                xb3_free = (s_dve, ln3_done)
                P.wait("pe", s_dve, ln3_done)
                P.wait("pe", s_act, ln3_ev)
                for j in range(16):
                    P.op("pe", lambda e, j=j: e.transpose(
                        (psA if j < 8 else psB)[:, (j % 8) * 128:(j % 8) * 128 + 128],
                        ln3tm[:, j * 128:(j + 1) * 128], identb[:, :]),
                        inc=(s_pe if j == 15 else None))
                tp3 = s_pe.n
                P.wait("act", s_pe, tp3)
                P.op("act", lambda e, tau=tau: e.copy(
                    out=ln3fm.rearrange("p (c t) -> p c t", c=16)[
                        :, 0:8, tau * 128:(tau + 1) * 128],
                    in_=psA[:, :].rearrange("p (c t) -> p c t", c=8)), inc=None)
                P.drain("act")
                P.op("act", lambda e, tau=tau: e.copy(
                    out=ln3fm.rearrange("p (c t) -> p c t", c=16)[
                        :, 8:16, tau * 128:(tau + 1) * 128],
                    in_=psB[:, :].rearrange("p (c t) -> p c t", c=8)), inc=s_act)
                ln3_ev = s_act.n
            P.wait("pe", s_act, ln3_ev)
            if w > 0:
                P.wait("act", s_dve, h_done)
            for c in range(16):
                pst = (g5, g6)[c % 2]
                P.wait("pe", s_act, fw_ev_bank[c % 2])
                P.op("pe", lambda e, c=c, pst=pst: e.matmul(
                    pst[:, :], hsu_sb[:, c * 128:(c + 1) * 128],
                    ln3fm[:, c * 512:(c + 1) * 512], start=True, stop=True),
                    inc=s_pe)
                fmm = s_pe.n
                P.wait("act", s_pe, fmm)
                P.op("act", lambda e, c=c, pst=pst: e.copy(
                    out=fA[c][:, :], in_=pst.ap()), inc=s_act)
                fw_ev_bank[c % 2] = s_act.n
            P.wait("dve", s_act, s_act.n)
            cur, nxt = fA, fB
            for st in range(4):
                step = 1 << st
                for c in range(16):
                    pa, pb = c & ~step, c | step
                    if st == 3:
                        dst = h_sb[:, c * OWN + w * 512:c * OWN + w * 512 + 512]
                    else:
                        dst = nxt[c][:, :]
                    opk = ALU.subtract if (c & step) else ALU.add
                    last_bt = (st == 3 and c == 15)
                    P.op("dve", lambda e, dst=dst, pa=pa, pb=pb, cur=cur, opk=opk: e.tensor_tensor(
                        out=dst, in0=cur[pa][:, :], in1=cur[pb][:, :],
                        op=opk), inc=(s_dve if last_bt else None))
                P.drain("dve")
                cur, nxt = nxt, cur
            h_done = s_dve.n

        barrier()
        wnat_up = [vbf(AR2, s * 1024, 2048) for s in range(2)]
        wnat_gt = [vbf(AR2, 2048 + s * 1024, 2048) for s in range(2)]
        uplhsT = [vbf(AR2, 4096 + s * 1024, 2048) for s in range(2)]
        gtlhsT = [vbf(AR2, 6144 + s * 1024, 2048) for s in range(2)]
        sgate = v32(AR2, 8192, 512)
        upsc = v32(AR2, 8704, 512)
        midb = vbf(AR2, 9216, 512)
        y2st = [vbf(AR2, 9472 + s * 1024, 2048) for s in range(2)]
        swu = [s_wu0, s_wu1]
        sy2w = [s_y2w0, s_y2w1]
        wn_free = [None, None]
        lh_free = [None, None]
        y2_free = [None, None]
        c2_tp_bank = [0, 0]
        mid_ev = 0
        y2_ev = 0
        for c2 in range(64):
            sl = c2 % 2
            if wn_free[sl] is not None:
                P.wait("pool", *wn_free[sl])
            P.op("pool", lambda e, c2=c2, sl=sl: e.dma_start(
                out=wnat_up[sl][:, :], in_=w_up[c2 * 128:(c2 + 1) * 128, :]),
                inc=swu[sl], inck=16)
            P.op("pool", lambda e, c2=c2, sl=sl: e.dma_start(
                out=wnat_gt[sl][:, :], in_=w_gate[c2 * 128:(c2 + 1) * 128, :]),
                inc=swu[sl], inck=16)
            wu_ld = swu[sl].n
            P.wait("pe", swu[sl], wu_ld)
            if lh_free[sl] is not None:
                P.wait("pe", *lh_free[sl])
            for j in range(16):
                P.wait("pe", s_act, c2_tp_bank[j % 2])
                P.op("pe", lambda e, j=j, sl=sl: e.transpose(
                    (psA, psB)[j % 2][:, (j // 2) * 128:(j // 2) * 128 + 128],
                    wnat_up[sl][:, j * 128:(j + 1) * 128], identb[:, :]),
                    inc=(s_pe if j >= 14 else None))
            tpu = s_pe.n
            P.wait("act", s_pe, tpu)
            P.op("act", lambda e, sl=sl: e.copy(
                out=uplhsT[sl].rearrange("p (c t) -> p c t", c=16)[:, 0:16:2, :],
                in_=psA[:, :].rearrange("p (c t) -> p c t", c=8)), inc=None)
            P.drain("act")
            P.op("act", lambda e, sl=sl: e.copy(
                out=uplhsT[sl].rearrange("p (c t) -> p c t", c=16)[:, 1:16:2, :],
                in_=psB[:, :].rearrange("p (c t) -> p c t", c=8)), inc=s_act)
            c2_tp_bank[0] = s_act.n
            c2_tp_bank[1] = s_act.n
            for j in range(16):
                P.wait("pe", s_act, c2_tp_bank[j % 2])
                P.op("pe", lambda e, j=j, sl=sl: e.transpose(
                    (psA, psB)[j % 2][:, (j // 2) * 128:(j // 2) * 128 + 128],
                    wnat_gt[sl][:, j * 128:(j + 1) * 128], identb[:, :]),
                    inc=(s_pe if j >= 14 else None))
            tpg = s_pe.n
            wn_free[sl] = (s_pe, tpg)
            P.wait("act", s_pe, tpg)
            P.op("act", lambda e, sl=sl: e.copy(
                out=gtlhsT[sl].rearrange("p (c t) -> p c t", c=16)[:, 0:16:2, :],
                in_=psA[:, :].rearrange("p (c t) -> p c t", c=8)), inc=None)
            P.drain("act")
            P.op("act", lambda e, sl=sl: e.copy(
                out=gtlhsT[sl].rearrange("p (c t) -> p c t", c=16)[:, 1:16:2, :],
                in_=psB[:, :].rearrange("p (c t) -> p c t", c=8)), inc=s_act)
            c2_tp_bank[0] = s_act.n
            c2_tp_bank[1] = s_act.n
            lh_ready = s_act.n
            ysl = c2 % 2
            if y2_free[ysl] is not None:
                P.wait("act", *y2_free[ysl])
            for w in range(NWC):
                P.wait("pe", s_act, lh_ready)
                if c2 == 0 and w == 0:
                    P.wait("pe", s_dve, h_done)
                P.wait("pe", s_dve, mid_ev)
                P.wait("pe", s_act, y2_ev)
                for c in range(16):
                    P.op("pe", lambda e, c=c, sl=sl, w=w: e.matmul(
                        g5[:, :], uplhsT[sl][:, c * 128:(c + 1) * 128],
                        h_sb[:, c * OWN + w * 512:c * OWN + (w + 1) * 512],
                        start=(c == 0), stop=(c == 15)), inc=None)
                for c in range(16):
                    P.op("pe", lambda e, c=c, sl=sl, w=w: e.matmul(
                        g6[:, :], gtlhsT[sl][:, c * 128:(c + 1) * 128],
                        h_sb[:, c * OWN + w * 512:c * OWN + (w + 1) * 512],
                        start=(c == 0), stop=(c == 15)),
                        inc=(s_pe if c == 15 else None))
                ug_done = s_pe.n
                P.wait("act", s_pe, ug_done)
                P.op("act", lambda e, c2=c2: e.activation(
                    out=sgate[:, :], in_=g6[:, :], func=AF.Silu,
                    scale=wscg_cols[:, c2:c2 + 1]), inc=s_act)
                silu_done = s_act.n
                P.wait("dve", s_pe, ug_done)
                P.op("dve", lambda e, c2=c2: e.tensor_scalar(
                    out=upsc[:, :], in0=g5[:, :], scalar1=upsc_cols[:, c2:c2 + 1],
                    scalar2=None, op0=ALU.mult), inc=None)
                P.drain("dve")
                P.wait("dve", s_act, silu_done)
                P.op("dve", lambda e: e.tensor_tensor(
                    out=midb[:, :], in0=upsc[:, :], in1=sgate[:, :],
                    op=ALU.mult), inc=s_dve)
                mid_ev = s_dve.n
                P.wait("pe", s_dve, mid_ev)
                P.op("pe", lambda e: e.matmul(
                    g1[:, 0:512], h128b_sb[:, :], midb[:, :],
                    start=True, stop=True), inc=s_pe)
                ymm = s_pe.n
                P.wait("act", s_pe, ymm)
                P.op("act", lambda e, w=w, ysl=ysl: e.copy(
                    out=y2st[ysl][:, w * 512:(w + 1) * 512],
                    in_=g1.ap()[:, 0:512]), inc=s_act)
                y2_ev = s_act.n
            P.wait("sp", s_act, y2_ev)
            P.op("sp", lambda e, c2=c2, ysl=ysl: e.dma_start(
                out=y2d[c2 * 128:(c2 + 1) * 128, 0:OWN],
                in_=y2st[ysl][:, 0:OWN]), inc=sy2w[ysl], inck=16)
            y2_free[ysl] = (sy2w[ysl], sy2w[ysl].n)

        barrier()
        NW3 = OWN // 256
        z_sb = [vbf(AR1, s * 8192, 64 * 256) for s in range(2)]
        ffo = vbf(AR1, 16384, 16 * 256)
        atm3 = v32(AR1, 18432, 2048)
        x2b = v32(AR1, 20480, 2048)
        wdl = [vbf(AR1, 22528 + s * 1024, 2048) for s in range(2)]
        y2w = vbf(AR2, 0, 64 * 256)
        pA = [vbf(AR2, 8192 + c * 128, 256) for c in range(64)]
        pB = [vbf(AR2, c * 128, 256) for c in range(64)]
        swdl = [s_wl0, s_wl1]
        DVE_CH = 64

        def ps8b(oc):
            if oc < 2:
                return g1, oc * 512
            if oc < 4:
                return g2, (oc - 2) * 512
            return (g3, g4, g5, g6)[oc - 4], 0

        y2w_free = None
        wd_free = [None, None]
        z_free = [0, 0]
        down_ev = 0
        x2b_free = None
        atm3_free = 0
        out_dma = 0
        dn_done = 0
        bt_done = [(0, 0), (0, 0)]

        def emit_butterfly(w):
            nonlocal y2w_free
            zsl = w % 2
            col0 = w * 256
            if y2w_free is not None:
                P.wait("act", *y2w_free)
            P.op("act", lambda e, col0=col0: e.dma_start(
                out=y2w.rearrange("p (c t) -> p c t", c=64),
                in_=y2d[:, col0:col0 + 256].rearrange(
                    "(c p) t -> p c t", p=128)), inc=s_y0, inck=16)
            yld = s_y0.n
            P.wait("dve", s_y0, yld)
            P.wait("pool", s_y0, yld)
            P.wait("dve", s_pe, z_free[zsl])
            P.wait("pool", s_pe, z_free[zsl])
            for st in range(6):
                step = 1 << st
                P.wait("dve", s_pl, s_pl.n)
                P.wait("pool", s_dve, s_dve.n)
                for c in range(64):
                    eng = "dve" if c < DVE_CH else "pool"
                    pa_, pb_ = c & ~step, c | step
                    if st == 0:
                        i0 = y2w[:, pa_ * 256:(pa_ + 1) * 256]
                        i1 = y2w[:, pb_ * 256:(pb_ + 1) * 256]
                    else:
                        src_ = pA if st % 2 == 1 else pB
                        i0 = src_[pa_][:, :]
                        i1 = src_[pb_][:, :]
                    if st == 5:
                        dst = z_sb[zsl][:, c * 256:(c + 1) * 256]
                    else:
                        dst_ = pA if st % 2 == 0 else pB
                        dst = dst_[c][:, :]
                    opk = ALU.subtract if (c & step) else ALU.add
                    lastd = (c == DVE_CH - 1)
                    lastp = (c == 63)
                    P.op(eng, lambda e, dst=dst, i0=i0, i1=i1, opk=opk: e.tensor_tensor(
                        out=dst, in0=i0, in1=i1, op=opk),
                        inc=(s_dve if (eng == "dve" and lastd) else
                             (s_pl if (eng == "pool" and lastp) else None)))
                P.drain("dve")
                P.drain("pool")
            y2w_free = (s_dve, s_dve.n)
            bt_done[zsl] = (s_dve.n, s_pl.n)
            bt_done[zsl] = (s_dve.n, s_pl.n)

        emit_butterfly(0)
        for w in range(NW3):
            zsl = w % 2
            P.wait("pe", s_dve, bt_done[zsl][0])
            P.wait("pe", s_pl, bt_done[zsl][1])
            for op_ in range(2):
                P.wait("pe", s_act, down_ev)
                if op_ == 0:
                    P.wait("pe", s_act, atm3_free)
                for f2g in range(32):
                    sl = f2g % 2
                    if wd_free[sl] is not None:
                        P.wait("sp", *wd_free[sl])
                    P.op("sp", lambda e, f2g=f2g, sl=sl, op_=op_: e.dma_start(
                        out=wdl[sl][:, 0:2048].rearrange("p (f n) -> p f n", f=2),
                        in_=wdTd[f2g * 256:(f2g + 1) * 256,
                                 op_ * 1024:(op_ + 1) * 1024].rearrange(
                            "(f p) n -> p f n", p=128)),
                        inc=swdl[sl], inck=16)
                    wld = swdl[sl].n
                    P.wait("pe", swdl[sl], wld)
                    for fi in range(2):
                        f2 = f2g * 2 + fi
                        for oc in range(8):
                            pst, col = ps8b(oc)
                            P.op("pe", lambda e, f2=f2, fi=fi, oc=oc, pst=pst,
                                 col=col, sl=sl, zsl=zsl: e.matmul(
                                pst[:, col:col + 256],
                                wdl[sl][:, fi * 1024 + oc * 128:fi * 1024 + (oc + 1) * 128],
                                z_sb[zsl][:, f2 * 256:(f2 + 1) * 256],
                                start=(f2 == 0), stop=(f2 == 63)),
                                inc=(s_pe if (oc == 7 and fi == 1) else None))
                    wd_free[sl] = (s_pe, s_pe.n)
                dn_done = s_pe.n
                P.wait("act", s_pe, dn_done)
                for oc in range(8):
                    ocg = op_ * 8 + oc
                    pst, col = ps8b(oc)
                    if oc > 0:
                        P.drain("act")
                    P.op("act", lambda e, ocg=ocg, pst=pst, col=col: e.activation(
                        out=ffo[:, ocg * 256:(ocg + 1) * 256],
                        in_=pst[:, col:col + 256],
                        func=AF.Copy, scale=wscdp_sb[:, ocg:ocg + 1]),
                        inc=(s_act if oc == 7 else None))
                down_ev = s_act.n
            z_free[zsl] = dn_done
            if w + 1 < NW3:
                emit_butterfly(w + 1)
            for tau in range(2):
                i_own = w * 2 + tau
                if x2b_free is not None:
                    P.wait("sp", *x2b_free)
                P.op("sp", lambda e, i_own=i_own: e.dma_start(
                    out=x2b[:, :], in_=x2d[i_own * 128:(i_own + 1) * 128, :]),
                    inc=s_x2r1, inck=16)
                x2ld = s_x2r1.n
                P.wait("pe", s_act, down_ev)
                P.wait("pe", s_act, atm3_free)
                for j in range(16):
                    P.op("pe", lambda e, j=j, tau=tau: e.transpose(
                        (psA if j < 8 else psB)[:, (j % 8) * 128:(j % 8) * 128 + 128],
                        ffo[:, j * 256 + tau * 128:j * 256 + (tau + 1) * 128],
                        identb[:, :]), inc=(s_pe if j == 15 else None))
                tp4 = s_pe.n
                P.wait("act", s_pe, tp4)
                P.wait("act", s_out, out_dma)
                P.op("act", lambda e: e.copy(out=atm3[:, 0:1024], in_=psA[:, :]),
                     inc=None)
                P.drain("act")
                P.op("act", lambda e: e.copy(out=atm3[:, 1024:2048], in_=psB[:, :]),
                     inc=s_act)
                atm3_ev = s_act.n
                atm3_free = s_act.n
                P.wait("dve", s_act, atm3_ev)
                for j in range(4):
                    P.op("dve", lambda e, j=j: e.bn_stats(
                        out=bnst[:, j * 6:(j + 1) * 6],
                        in_=atm3[:, j * 512:(j + 1) * 512]), inc=None)
                P.drain("dve")
                P.op("dve", lambda e: e.bn_aggr(out=mvt[:, 0:2], in_=bnst[:, :]),
                     inc=s_dve)
                bn4 = s_dve.n
                P.wait("act", s_dve, bn4)
                P.op("act", lambda e: e.activation(out=rstdt[:, 0:1], in_=mvt[:, 1:2],
                                                   func=AF.Ln, bias=fc[:, 0:1]), inc=None)
                P.drain("act")
                P.op("act", lambda e: e.activation(out=rstdt[:, 1:2], in_=rstdt[:, 0:1],
                                                   func=AF.Exp, scale=-0.5), inc=s_act)
                rstd4 = s_act.n
                P.wait("dve", s_act, rstd4)
                P.wait("dve", s_x2r1, x2ld)
                P.op("dve", lambda e: e.tensor_scalar(
                    out=atm3[:, :], in0=atm3[:, :], scalar1=mvt[:, 0:1],
                    scalar2=rstdt[:, 1:2], op0=ALU.subtract, op1=ALU.mult),
                    inc=None)
                P.drain("dve")
                P.op("dve", lambda e: e.tensor_tensor(
                    out=atm3[:, :], in0=atm3[:, :], in1=x2b[:, :], op=ALU.add),
                    inc=s_dve)
                x3_done = s_dve.n
                x2b_free = (s_dve, x3_done)
                P.wait("sp", s_dve, x3_done)
                P.op("sp", lambda e, i_own=i_own: e.dma_start(
                    out=out[i_own * 128:(i_own + 1) * 128, :], in_=atm3[:, :]),
                    inc=s_out, inck=16)
                out_dma = s_out.n

        P.wait("sp", s_out, s_out.n)
        P.wait("pool", s_po, s_po.n)

        import os
        _tr = float(os.environ.get("K_TRUNC", "1.0"))
        if _tr < 1.0:
            for k in P.q:
                P.q[k] = P.q[k][:int(len(P.q[k]) * _tr)]
        for k in P.q:
            _trk = float(os.environ.get("K_TRUNC_" + k.upper(), "1.0"))
            if _trk < 1.0:
                P.q[k] = P.q[k][:int(len(P.q[k]) * _trk)]
        with nc.Block() as block:
            @block.tensor
            def _(e):
                for f in P.q["pe"]:
                    f(e)

            @block.scalar
            def _(e):
                for f in P.q["act"]:
                    f(e)

            @block.vector
            def _(e):
                for f in P.q["dve"]:
                    f(e)

            @block.gpsimd
            def _(e):
                for f in P.q["pool"]:
                    f(e)

            @block.sync
            def _(e):
                for f in P.q["sp"]:
                    f(e)

    return nc


def _host_inputs(inputs, T=8192):
    OWN = T // 4
    x = np.ascontiguousarray(np.asarray(inputs["L0_attn_x"], dtype=np.float32)
                             .reshape(2 * T, D))
    q_a_w = np.ascontiguousarray(inputs["L0_attn_q_a_w"], dtype=np.float32)
    q_b_w = np.asarray(inputs["L0_attn_q_b_w"], dtype=np.float32)
    kv_a_w = np.ascontiguousarray(inputs["L0_attn_kv_a_w"], dtype=np.float32)
    kv_b_w = np.asarray(inputs["L0_attn_kv_b_w"], dtype=np.float32)
    o_w = np.ascontiguousarray(inputs["L0_attn_o_w"], dtype=np.float32)
    w_up = np.ascontiguousarray(inputs["L0_ffn_W_up"], dtype=np.float32)
    w_gate = np.ascontiguousarray(inputs["L0_ffn_W_gate"], dtype=np.float32)
    w_down = np.ascontiguousarray(inputs["L0_ffn_W_down"], dtype=np.float32)
    wq = np.asarray(inputs["L0_attn_q_a_ln_w"], dtype=np.float32)
    wkv = np.asarray(inputs["L0_attn_kv_a_ln_w"], dtype=np.float32)
    su_ug = np.asarray(inputs["L0_ffn_SU_ug"], dtype=np.float32)
    su_dp = np.asarray(inputs["L0_ffn_SU_dp"], dtype=np.float32)
    wsc_ug = np.asarray(inputs["L0_ffn_Wscale_ug"], dtype=np.float32)
    wsc_dp = np.asarray(inputs["L0_ffn_Wscale_dp"], dtype=np.float32)

    inv = (1.0 / THETA ** (np.arange(0, ROPE, 2) / ROPE)).astype(np.float64)
    tpos = np.arange(T, dtype=np.float64)
    fr = np.outer(tpos, inv)
    cos_t = np.ascontiguousarray(np.cos(fr).astype(np.float32))
    sin_t = np.ascontiguousarray(np.sin(fr).astype(np.float32))

    def hmat(m):
        Hm = np.array([[1.0]], dtype=np.float32)
        while Hm.shape[0] < m:
            Hm = np.block([[Hm, Hm], [Hm, -Hm]])
        return Hm
    H128m = np.ascontiguousarray(hmat(128).astype(np.float32))
    ones_b = np.ones((128, 128), dtype=ml_dtypes.bfloat16)
    ident_b = np.eye(128, dtype=ml_dtypes.bfloat16)
    ident_f = np.eye(128, dtype=np.float32)

    def cols(v, npart=128):
        return np.ascontiguousarray(v.reshape(-1, npart).T.astype(np.float32))

    NCH = OWN // 512
    LIMS = [16 * (s + 1) for s in range(NCH)]

    in_maps = []
    for r in range(NC_):
        b = r // 4
        j = r % 4
        chunks = [(s * 4 + j) if s % 2 == 0 else ((s + 1) * 4 - 1 - j)
                  for s in range(NCH)]
        own_rows = np.concatenate([np.arange(c * 512, (c + 1) * 512)
                                   for c in chunks])
        xb = x[b * T:(b + 1) * T]
        mk = np.zeros((128, NCH * 16 * 512), dtype=ml_dtypes.bfloat16)
        for s, c in enumerate(chunks):
            for ti in range(16):
                kt = LIMS[s] - 16 + ti
                krow = kt * 128 + np.arange(128)[:, None]
                qcol = c * 512 + np.arange(512)[None, :]
                mk[:, (s * 16 + ti) * 512:(s * 16 + ti + 1) * 512] = (
                    krow <= qcol).astype(ml_dtypes.bfloat16)
        cos16 = np.tile(cos_t[own_rows], (1, 16))
        sin16 = np.tile(sin_t[own_rows], (1, 16))
        mp = {
            "x_own": np.ascontiguousarray(xb[own_rows]),
            "x_batch": np.ascontiguousarray(xb),
            "q_a_w": q_a_w, "q_b_w": np.ascontiguousarray(q_b_w),
            "kv_a_w": kv_a_w, "kv_b_w": np.ascontiguousarray(kv_b_w),
            "o_w": o_w,
            "w_up": w_up, "w_gate": w_gate, "w_down": w_down,
            "wq_cols": cols(wq), "wkv_cols": cols(wkv),
            "su_ug_cols": cols(su_ug), "su_dp_cols": cols(su_dp),
            "wsc_ug_cols": cols(wsc_ug), "wsc_dp_cols": cols(wsc_dp),
            "cos_all": cos_t, "sin_all": sin_t,
            "cos16_own": np.ascontiguousarray(cos16),
            "sin16_own": np.ascontiguousarray(sin16),
            "h128": H128m, "ones_bf": ones_b, "ident_bf": ident_b,
            "ident_f32": ident_f, "masks_in": np.ascontiguousarray(mk),
            "fconst": np.ascontiguousarray(np.broadcast_to(np.array(
                [EPS_LN, EPS_RMS, math.log(ATT_SCALE), 0.0], dtype=np.float32),
                (128, 4))),
        }
        in_maps.append(mp)
    return in_maps


def _assemble(shards, T):
    OWN = T // 4
    NCH = OWN // 512
    full = np.empty((2, T, D), dtype=np.float32)
    for r in range(NC_):
        b = r // 4
        j = r % 4
        chunks = [(s * 4 + j) if s % 2 == 0 else ((s + 1) * 4 - 1 - j)
                  for s in range(NCH)]
        for s, c in enumerate(chunks):
            full[b, c * 512:(c + 1) * 512] = shards[r][s * 512:(s + 1) * 512]
    return full


def kernel(**inputs):
    T = 8192
    in_maps = _host_inputs(inputs, T=T)
    nc = build_program(T=T, debug=False)
    res = run_bass_kernel_spmd(nc, in_maps, core_ids=list(range(NC_)))
    shards = [res.results[r]["out"] for r in range(NC_)]
    return _assemble(shards, T)


# version 11
# speedup vs baseline: 1.2260x; 1.0479x over previous
import math
import numpy as np
import ml_dtypes
import concourse.bass as bass
import concourse.mybir as mybir
from concourse.bass_utils import run_bass_kernel_spmd

F32 = mybir.dt.float32
BF16 = mybir.dt.bfloat16
AF = mybir.ActivationFunctionType
ALU = mybir.AluOpType
AX = mybir.AxisListType

D = 2048
H = 16
NOPE = 128
ROPE = 64
QK = 192
VD = 128
KVL = 512
QL = 1536
INTER = 8192
EPS_LN = 1e-5
EPS_RMS = 1e-6
ATT_SCALE = QK ** -0.5
THETA = 5e5

NC_ = 8
GROUPS = [[0, 1, 2, 3], [4, 5, 6, 7]]
HPC = 4

BF = 2
FP = 4


class Sem:
    def __init__(self, h):
        self.h = h
        self.n = 0

    def bump(self, k=1):
        self.n += k
        return self.n


class Prog:

    ENGS = ("pe", "act", "dve", "pool", "sp")

    def __init__(self):
        self.q = {e: [] for e in self.ENGS}

    def wait(self, eng, sem, val):
        if val > 0:
            self.q[eng].append(lambda e, h=sem.h, v=val: e.wait_ge(h, v))

    def op(self, eng, build, inc=None, inck=1):
        if inc is None:
            self.q[eng].append(build)
            return None
        v = inc.bump(inck)
        def f(e, b=build, h=inc.h, k=inck):
            inst = b(e)
            inst.then_inc(h, k)
        self.q[eng].append(f)
        return v

    def drain(self, eng):
        self.q[eng].append(lambda e: e.drain())


def ceil_div(a, b):
    return (a + b - 1) // b


def build_program(T=8192, debug=False, taps=()):
    OWN = T // 4
    NT = OWN // 128
    NW = T // 512
    NKT = T // 128
    NB = OWN // 512
    QBLK = T // 2048
    assert OWN % 512 == 0 and T % 2048 == 0

    nc = bass.Bass("TRN2", target_bir_lowering=False, debug=debug)

    def din(name, shape, dt):
        return nc.dram_tensor(name, shape, dt, kind="ExternalInput")

    x_own = din("x_own", [OWN, D], F32)
    x_batch = din("x_batch", [T, D], F32)
    q_a_w = din("q_a_w", [D, QL], F32)
    q_b_w = din("q_b_w", [QL, H * QK], F32)
    kv_a_w = din("kv_a_w", [D, KVL + ROPE], F32)
    kv_b_w = din("kv_b_w", [KVL, H * 256], F32)
    o_w = din("o_w", [D, D], F32)
    w_up = din("w_up", [INTER, D], F32)
    w_gate = din("w_gate", [INTER, D], F32)
    w_down = din("w_down", [D, INTER], F32)
    wq_cols = din("wq_cols", [128, QL // 128], F32)
    wkv_cols = din("wkv_cols", [128, KVL // 128], F32)
    su_ug_cols = din("su_ug_cols", [128, 16], F32)
    su_dp_cols = din("su_dp_cols", [128, 64], F32)
    wsc_ug_cols = din("wsc_ug_cols", [128, 128], F32)
    wsc_dp_cols = din("wsc_dp_cols", [128, 16], F32)
    cos_all = din("cos_all", [T, 32], F32)
    sin_all = din("sin_all", [T, 32], F32)
    cos16_own = din("cos16_own", [OWN, 512], F32)
    sin16_own = din("sin16_own", [OWN, 512], F32)
    h128 = din("h128", [128, 128], F32)
    ones_bf = din("ones_bf", [128, 128], BF16)
    ident_bf = din("ident_bf", [128, 128], BF16)
    ident_f32 = din("ident_f32", [128, 128], F32)
    masks_in = din("masks_in", [128, (OWN // 512) * 16 * 512], BF16)
    fconst = din("fconst", [128, 4], F32)

    out = nc.dram_tensor("out", [OWN, D], F32, kind="ExternalOutput")

    qspill = nc.dram_tensor("qspill", [OWN, H * QK], BF16)
    ospill = nc.dram_tensor("ospill", [H * VD, OWN], BF16)
    x2d = nc.dram_tensor("x2d", [OWN, D], F32)
    y2d = nc.dram_tensor("y2d", [INTER, OWN], BF16)
    wdTd = nc.dram_tensor("wdTd", [INTER, D], BF16)

    tap_outs = {}
    for tname in taps:
        src = {"qspill": qspill, "ospill": ospill, "x2d": x2d,
               "y2d": y2d, "wdTd": wdTd}[tname]
        tap_outs[tname] = (nc.dram_tensor("dbg_" + tname, list(src.shape),
                                          src.dtype, kind="ExternalOutput"), src)

    P = Prog()

    import contextlib
    ctx = contextlib.ExitStack()
    with ctx:
        AR1 = ctx.enter_context(nc.sbuf_tensor("AR1", [128, 24576], F32))
        AR2 = ctx.enter_context(nc.sbuf_tensor("AR2", [128, 20480], F32))

        def v32(ar, off, n):
            return ar[:, off:off + n]

        def vbf(ar, off, n):
            return ar.ap().bitcast(BF16)[:, 2 * off:2 * off + n]

        onesb = ctx.enter_context(nc.sbuf_tensor("onesb", [128, 128], BF16))
        identb = ctx.enter_context(nc.sbuf_tensor("identb", [128, 128], BF16))
        identf = ctx.enter_context(nc.sbuf_tensor("identf", [128, 128], F32))
        h128f = ctx.enter_context(nc.sbuf_tensor("h128f", [128, 128], F32))
        hsu_sb = ctx.enter_context(nc.sbuf_tensor("hsu_sb", [128, 16 * 128], BF16))
        h128b_sb = ctx.enter_context(nc.sbuf_tensor("h128b_sb", [128, 128], BF16))
        colscr = ctx.enter_context(nc.sbuf_tensor("colscr", [128, 256], F32))
        rowscr = ctx.enter_context(nc.sbuf_tensor("rowscr", [16, 512], F32))
        upsc_cols = ctx.enter_context(nc.sbuf_tensor("upsc_cols", [128, 64], F32))
        wscg_cols = ctx.enter_context(nc.sbuf_tensor("wscg_cols", [128, 64], F32))
        wscdp_sb = ctx.enter_context(nc.sbuf_tensor("wscdp_sb", [128, 16], F32))
        wqc_sb = ctx.enter_context(nc.sbuf_tensor("wqc_sb", [128, 12], F32))
        wkvc_sb = ctx.enter_context(nc.sbuf_tensor("wkvc_sb", [128, 4], F32))
        suug_sb = ctx.enter_context(nc.sbuf_tensor("suug_sb", [128, 16], F32))
        bnst = ctx.enter_context(nc.sbuf_tensor("bnst", [128, 24], F32))
        mvt = ctx.enter_context(nc.sbuf_tensor("mvt", [128, 2], F32))
        rstdt = ctx.enter_context(nc.sbuf_tensor("rstdt", [128, 4], F32))
        sstat = ctx.enter_context(nc.sbuf_tensor("sstat", [128, 16], F32))
        fc = ctx.enter_context(nc.sbuf_tensor("fc", [128, 4], F32))
        nb_bf = ctx.enter_context(nc.sbuf_tensor("nb_bf", [1, 16], BF16))
        onesf = ctx.enter_context(nc.sbuf_tensor("onesf", [128, 1], F32))
        recb = ctx.enter_context(nc.sbuf_tensor("recb", [1, 512], BF16))
        recb2 = ctx.enter_context(nc.sbuf_tensor("recb2", [1, 512], BF16))
        bcastf2 = ctx.enter_context(nc.sbuf_tensor("bcastf2", [128, 512], F32))

        g1 = ctx.enter_context(nc.psum_tensor("g1", [128, 1024], F32))
        g2 = ctx.enter_context(nc.psum_tensor("g2", [128, 1024], F32))
        g3 = ctx.enter_context(nc.psum_tensor("g3", [128, 512], F32))
        g4 = ctx.enter_context(nc.psum_tensor("g4", [128, 512], F32))
        g5 = ctx.enter_context(nc.psum_tensor("g5", [128, 512], F32))
        g6 = ctx.enter_context(nc.psum_tensor("g6", [128, 512], F32))
        psA = g3.ap().bitcast(BF16)
        psB = g4.ap().bitcast(BF16)
        ps1 = g5.ap()
        ps2 = g2.ap()[:, 0:512]
        ps3 = g6.ap()
        ps4 = g1.ap()[:, 0:512]

        def mksem(name):
            return Sem(ctx.enter_context(nc.semaphore(name)))
        s_c = mksem("s_c")
        s_x0, s_x1 = mksem("s_x0"), mksem("s_x1")
        s_wa = mksem("s_wa")
        s_mk = mksem("s_mk")
        s_kb = mksem("s_kb")
        s_pe, s_act, s_dve = mksem("s_pe"), mksem("s_act"), mksem("s_dve")
        s_out = mksem("s_out")
        s_po = mksem("s_po")
        s_bar = mksem("s_bar")
        sx = [s_x0, s_x1]
        s_qs0, s_qs1 = mksem("s_qs0"), mksem("s_qs1")
        s_cs0, s_cs1 = mksem("s_cs0"), mksem("s_cs1")
        s_kv = mksem("s_kv")
        s_qr0, s_qr1 = mksem("s_qr0"), mksem("s_qr1")
        s_c20 = mksem("s_c20")
        s_c21 = mksem("s_c21")
        s_oh0, s_oh1 = mksem("s_oh0"), mksem("s_oh1")
        s_ow = mksem("s_ow")
        s_wu0, s_wu1 = mksem("s_wu0"), mksem("s_wu1")
        s_y0, s_y1 = mksem("s_y0"), mksem("s_y1")
        s_wd0, s_wd1 = mksem("s_wd0"), mksem("s_wd1")
        s_x2r0, s_x2r1 = mksem("s_x2r0"), mksem("s_x2r1")
        s_wt0, s_wt1 = mksem("s_wt0"), mksem("s_wt1")
        s_y2w0, s_y2w1 = mksem("s_y2w0"), mksem("s_y2w1")
        s_wl0, s_wl1 = mksem("s_wl0"), mksem("s_wl1")
        s_pl = mksem("s_pl")

        ALL_SEMS = [s_c, s_x0, s_x1, s_wa, s_mk, s_kb, s_out, s_po,
                    s_qs0, s_qs1, s_cs0, s_cs1, s_kv, s_qr0, s_qr1, s_c20, s_c21,
                    s_oh0, s_oh1, s_ow, s_wu0, s_wu1, s_y0, s_y1,
                    s_wd0, s_wd1, s_x2r0, s_x2r1, s_wt0, s_wt1,
                    s_y2w0, s_y2w1, s_wl0, s_wl1]

        def barrier():
            for e in P.ENGS:
                P.op(e, lambda en: en.nop(), inc=s_bar)
            tgt = s_bar.n
            for e in P.ENGS:
                P.wait(e, s_bar, tgt)
                for s in ALL_SEMS + [s_pe, s_act, s_dve, s_pl]:
                    P.wait(e, s, s.n)

        LNS = math.log(ATT_SCALE)
        NCH = OWN // 512
        LIMS = [16 * (s + 1) for s in range(NCH)]

        P.op("sp", lambda e: e.dma_start(out=onesb[:, :], in_=ones_bf.ap()), inc=s_c, inck=16)
        P.op("sp", lambda e: e.dma_start(out=identb[:, :], in_=ident_bf.ap()), inc=s_c, inck=16)
        P.op("sp", lambda e: e.dma_start(out=identf[:, :], in_=ident_f32.ap()), inc=s_c, inck=16)
        P.op("sp", lambda e: e.dma_start(out=h128f[:, :], in_=h128.ap()), inc=s_c, inck=16)
        P.op("sp", lambda e: e.dma_start(out=wqc_sb[:, :], in_=wq_cols.ap()), inc=s_c, inck=16)
        P.op("sp", lambda e: e.dma_start(out=wkvc_sb[:, :], in_=wkv_cols.ap()), inc=s_c, inck=16)
        P.op("sp", lambda e: e.dma_start(out=suug_sb[:, :], in_=su_ug_cols.ap()), inc=s_c, inck=16)
        P.op("sp", lambda e: e.dma_start(out=upsc_cols[:, :], in_=wsc_ug_cols.ap()[:, 0:64]), inc=s_c, inck=16)
        P.op("sp", lambda e: e.dma_start(out=wscg_cols[:, :], in_=wsc_ug_cols.ap()[:, 64:128]), inc=s_c, inck=16)
        P.op("sp", lambda e: e.dma_start(out=wscdp_sb[:, :], in_=wsc_dp_cols.ap()), inc=s_c, inck=16)
        P.op("sp", lambda e: e.dma_start(out=colscr[:, 0:64], in_=su_dp_cols.ap()), inc=s_c, inck=16)
        P.op("sp", lambda e: e.dma_start(out=fc[:, :], in_=fconst.ap()), inc=s_c, inck=16)
        SC = s_c.n

        P.wait("dve", s_c, SC)
        P.op("dve", lambda e: e.memset(onesf[:, :], 1.0), inc=None)
        P.drain("dve")
        P.op("dve", lambda e: e.tensor_tensor(out=upsc_cols[:, :],
                                              in0=upsc_cols[:, :],
                                              in1=colscr[:, 0:64], op=ALU.mult),
             inc=s_dve)
        for c in range(16):
            P.op("dve", lambda e, c=c: e.tensor_scalar(
                out=hsu_sb[:, c * 128:(c + 1) * 128], in0=h128f[:, :],
                scalar1=suug_sb[:, c:c + 1], scalar2=1.0 / math.sqrt(2048.0),
                op0=ALU.mult, op1=ALU.mult), inc=s_dve)
        P.op("dve", lambda e: e.tensor_scalar(
            out=h128b_sb[:, :], in0=h128f[:, :], scalar1=1.0 / math.sqrt(8192.0),
            scalar2=None, op0=ALU.mult), inc=s_dve)
        consts_done = s_dve.n
        P.wait("pe", s_c, SC)
        P.wait("act", s_c, SC)

        qaw_bf = vbf(AR1, 0, 16 * 1536)
        qan_own = vbf(AR1, 12288, 12 * OWN)
        qatmp = v32(AR2, 4096, 1536)
        ln1tm = vbf(AR2, 5632, 2048)
        ln1fm = vbf(AR2, 6656, 2048)
        sqscr = v32(AR2, 7680, 512)

        P.op("pool", lambda e: e.dma_start(
            out=AR1.ap().bitcast(BF16)[:, 0:16 * 1536].rearrange(
                "p (c n) -> p c n", c=16),
            in_=q_a_w.ap().rearrange("(c p) n -> p c n", p=128)),
            inc=s_wa, inck=16)
        AW = s_wa.n

        xt_free = [None, None]
        fm_evac = 0
        qan_ev = 0
        for i in range(NT):
            sl = i % 2
            if xt_free[sl] is not None:
                P.wait("sp", *xt_free[sl])
            P.op("sp", lambda e, i=i, sl=sl: e.dma_start(
                out=AR2[:, sl * 2048:sl * 2048 + 2048],
                in_=x_own[i * 128:(i + 1) * 128, :]), inc=sx[sl], inck=16)
            x_ld = sx[sl].n
            P.wait("dve", sx[sl], x_ld)
            if i == 0:
                P.wait("dve", s_dve, consts_done)
            for j in range(4):
                P.op("dve", lambda e, sl=sl, j=j: e.bn_stats(
                    out=bnst[:, j * 6:(j + 1) * 6],
                    in_=AR2[:, sl * 2048 + j * 512:sl * 2048 + (j + 1) * 512]),
                    inc=None)
            P.drain("dve")
            P.op("dve", lambda e: e.bn_aggr(out=mvt[:, 0:2], in_=bnst[:, :]),
                 inc=s_dve)
            bn_done = s_dve.n
            P.wait("act", s_dve, bn_done)
            P.op("act", lambda e: e.activation(out=rstdt[:, 0:1], in_=mvt[:, 1:2],
                                               func=AF.Ln, bias=fc[:, 0:1]), inc=None)
            P.drain("act")
            P.op("act", lambda e: e.activation(out=rstdt[:, 1:2], in_=rstdt[:, 0:1],
                                               func=AF.Exp, scale=-0.5), inc=s_act)
            rstd_done = s_act.n
            P.wait("dve", s_act, rstd_done)
            P.op("dve", lambda e, sl=sl: e.tensor_scalar(
                out=ln1tm, in0=AR2[:, sl * 2048:sl * 2048 + 2048],
                scalar1=mvt[:, 0:1], scalar2=rstdt[:, 1:2],
                op0=ALU.subtract, op1=ALU.mult), inc=s_dve)
            ln1_done = s_dve.n
            xt_free[sl] = (s_dve, ln1_done)
            P.wait("pe", s_dve, ln1_done)
            if i > 0:
                P.wait("pe", s_act, fm_evac)
            for j in range(16):
                P.op("pe", lambda e, j=j: e.transpose(
                    (psA if j < 8 else psB)[:, (j % 8) * 128:(j % 8) * 128 + 128],
                    ln1tm[:, j * 128:(j + 1) * 128], identb[:, :]),
                    inc=(s_pe if j == 15 else None))
            tp_done = s_pe.n
            P.wait("act", s_pe, tp_done)
            P.op("act", lambda e: e.copy(out=ln1fm[:, 0:1024], in_=psA[:, :]), inc=None)
            P.drain("act")
            P.op("act", lambda e: e.copy(out=ln1fm[:, 1024:2048], in_=psB[:, :]),
                 inc=s_act)
            fm_evac = s_act.n
            P.wait("pe", s_act, fm_evac)
            if i == 0:
                P.wait("pe", s_wa, AW)
            for c in range(16):
                for sidx, pst in enumerate((ps1, ps3, ps4)):
                    P.op("pe", lambda e, c=c, sidx=sidx, pst=pst: e.matmul(
                        pst[:, :], ln1fm[:, c * 128:(c + 1) * 128],
                        qaw_bf[:, c * 1536 + sidx * 512:c * 1536 + (sidx + 1) * 512],
                        start=(c == 0), stop=(c == 15)),
                        inc=(s_pe if (c == 15 and sidx == 2) else None))
            qa_done = s_pe.n
            P.wait("act", s_pe, qa_done)
            P.op("act", lambda e: e.copy(out=qatmp[:, 0:512], in_=ps1[:, :]), inc=None)
            P.op("act", lambda e: e.copy(out=qatmp[:, 512:1024], in_=ps3[:, :]), inc=None)
            P.drain("act")
            P.op("act", lambda e: e.copy(out=qatmp[:, 1024:1536], in_=ps4[:, :]),
                 inc=s_act)
            qatmp_done = s_act.n
            P.wait("dve", s_act, qatmp_done)
            for k3 in range(3):
                P.op("dve", lambda e, k3=k3: e.scalar_tensor_tensor(
                    out=sqscr, in0=qatmp[:, k3 * 512:(k3 + 1) * 512], scalar=1.0,
                    in1=qatmp[:, k3 * 512:(k3 + 1) * 512],
                    op0=ALU.mult, op1=ALU.mult, accum_out=sstat[:, k3:k3 + 1]),
                    inc=None)
                P.drain("dve")
            P.op("dve", lambda e: e.tensor_tensor(
                out=sstat[:, 3:4], in0=sstat[:, 0:1], in1=sstat[:, 1:2],
                op=ALU.add), inc=None)
            P.drain("dve")
            P.op("dve", lambda e: e.tensor_tensor(
                out=sstat[:, 4:5], in0=sstat[:, 3:4], in1=sstat[:, 2:3],
                op=ALU.add), inc=s_dve)
            qssq_done = s_dve.n
            P.wait("act", s_dve, qssq_done)
            P.op("act", lambda e: e.activation(out=rstdt[:, 3:4], in_=sstat[:, 4:5],
                                               func=AF.Ln, scale=1.0 / QL,
                                               bias=fc[:, 1:2]), inc=None)
            P.drain("act")
            P.op("act", lambda e: e.activation(out=rstdt[:, 2:3], in_=rstdt[:, 3:4],
                                               func=AF.Exp, scale=-0.5, bias=fc[:, 2:3]),
                 inc=s_act)
            qrstd_done = s_act.n
            P.wait("dve", s_act, qrstd_done)
            P.op("dve", lambda e: e.tensor_scalar(
                out=ln1tm[:, 0:1536], in0=qatmp, scalar1=rstdt[:, 2:3],
                scalar2=None, op0=ALU.mult), inc=s_dve)
            qan_done = s_dve.n
            P.wait("pe", s_dve, qan_done)
            P.wait("pe", s_act, qan_ev)
            for j in range(12):
                P.op("pe", lambda e, j=j: e.transpose(
                    (psA if j < 8 else psB)[:, (j % 8) * 128:(j % 8) * 128 + 128],
                    ln1tm[:, j * 128:(j + 1) * 128], identb[:, :]),
                    inc=(s_pe if j == 11 else None))
            qtp_done = s_pe.n
            P.wait("act", s_pe, qtp_done)
            P.op("act", lambda e, i=i: e.copy(
                out=qan_own.rearrange("p (c t) -> p c t", c=12)[:, 0:8, i * 128:(i + 1) * 128],
                in_=psA[:, :].rearrange("p (c t) -> p c t", c=8)), inc=None)
            P.drain("act")
            P.op("act", lambda e, i=i: e.copy(
                out=qan_own.rearrange("p (c t) -> p c t", c=12)[:, 8:12, i * 128:(i + 1) * 128],
                in_=psB[:, 0:512].rearrange("p (c t) -> p c t", c=4)),
                inc=s_act)
            qan_ev = s_act.n
            fm_evac = s_act.n

        barrier()
        qbw_bf = vbf(AR2, 0, 12 * 3072)
        qtm = [vbf(AR1, s * 1536, 3072) for s in range(2)]
        ropescr = v32(AR1, 3072, 2048)
        ssq_all = v32(AR1, 5120, 256)
        cs_sl = [(v32(AR1, 5376 + s * 1024, 512), v32(AR1, 5888 + s * 1024, 512))
                 for s in range(2)]
        scs = [s_cs0, s_cs1]
        sqs = [s_qs0, s_qs1]

        P.op("pool", lambda e: e.dma_start(
            out=AR2.ap().bitcast(BF16)[:, 0:12 * 3072].rearrange(
                "p (c n) -> p c n", c=12),
            in_=q_b_w.ap().rearrange("(c p) n -> p c n", p=128)),
            inc=s_wa, inck=16)
        AW2 = s_wa.n
        P.wait("dve", s_wa, AW2)
        for c in range(12):
            P.op("dve", lambda e, c=c: e.tensor_scalar(
                out=qbw_bf[:, c * 3072:(c + 1) * 3072],
                in0=qbw_bf[:, c * 3072:(c + 1) * 3072],
                scalar1=wqc_sb[:, c:c + 1], scalar2=None, op0=ALU.mult),
                inc=(s_dve if c == 11 else None))
        qbw_ready = s_dve.n

        psq4 = [g3.ap()[:, 0:384], g4.ap()[:, 0:384],
                g5.ap()[:, 0:384], g6.ap()[:, 0:384]]
        qtm_free = [None, None]
        cs_free = [None, None]
        qev_done = 0
        for tq in range(NT):
            sl = tq % 2
            if cs_free[sl] is not None:
                P.wait("sp", *cs_free[sl])
            P.op("sp", lambda e, tq=tq, sl=sl: e.dma_start(
                out=cs_sl[sl][0][:, :],
                in_=cos16_own[tq * 128:(tq + 1) * 128, :]), inc=scs[sl], inck=16)
            P.op("sp", lambda e, tq=tq, sl=sl: e.dma_start(
                out=cs_sl[sl][1][:, :],
                in_=sin16_own[tq * 128:(tq + 1) * 128, :]), inc=scs[sl], inck=16)
            cs_ld = scs[sl].n
            if tq == 0:
                P.wait("pe", s_dve, qbw_ready)
            P.wait("pe", s_act, qev_done)
            if qtm_free[sl] is not None:
                P.wait("pe", *qtm_free[sl])
            for half in range(2):
                if half == 1:
                    P.wait("pe", s_act, s_act.n)
                for kv4 in range(4):
                    kv8 = half * 4 + kv4
                    for l in range(12):
                        P.op("pe", lambda e, kv4=kv4, kv8=kv8, l=l, tq=tq: e.matmul(
                            psq4[kv4],
                            qan_own[:, l * OWN + tq * 128:l * OWN + (tq + 1) * 128],
                            qbw_bf[:, l * 3072 + kv8 * 384:l * 3072 + (kv8 + 1) * 384],
                            start=(l == 0), stop=(l == 11)),
                            inc=(s_pe if (kv4 == 3 and l == 11) else None))
                qb_done = s_pe.n
                P.wait("act", s_pe, qb_done)
                if half == 0:
                    P.wait("act", s_act, qev_done)
                for kv4 in range(4):
                    kv8 = half * 4 + kv4
                    if kv4 == 3:
                        P.drain("act")
                    P.op("act", lambda e, kv4=kv4, kv8=kv8, sl=sl: e.copy(
                        out=qtm[sl][:, kv8 * 384:(kv8 + 1) * 384], in_=psq4[kv4]),
                        inc=(s_act if kv4 == 3 else None))
            qev_done = s_act.n
            P.wait("dve", s_act, qev_done)
            P.wait("dve", scs[sl], cs_ld)
            qa16 = qtm[sl].rearrange("p (h d) -> p h d", h=16)
            qe = qa16[:, :, 128:192:2]
            qo = qa16[:, :, 129:192:2]
            ct = cs_sl[sl][0].rearrange("p (h d) -> p h d", h=16)
            st_ = cs_sl[sl][1].rearrange("p (h d) -> p h d", h=16)
            P.op("dve", lambda e, qe=qe, ct=ct: e.tensor_tensor(
                out=AR1[:, 3072:3584].rearrange("p (h d) -> p h d", h=16),
                in0=qe, in1=ct, op=ALU.mult), inc=None)
            P.op("dve", lambda e, qo=qo, st_=st_: e.tensor_tensor(
                out=AR1[:, 3584:4096].rearrange("p (h d) -> p h d", h=16),
                in0=qo, in1=st_, op=ALU.mult), inc=None)
            P.op("dve", lambda e, qo=qo, ct=ct: e.tensor_tensor(
                out=AR1[:, 4096:4608].rearrange("p (h d) -> p h d", h=16),
                in0=qo, in1=ct, op=ALU.mult), inc=None)
            P.op("dve", lambda e, qe=qe, st_=st_: e.tensor_tensor(
                out=AR1[:, 4608:5120].rearrange("p (h d) -> p h d", h=16),
                in0=qe, in1=st_, op=ALU.mult), inc=None)
            P.drain("dve")
            P.op("dve", lambda e, qe=qe: e.tensor_tensor(
                out=qe,
                in0=AR1[:, 3072:3584].rearrange("p (h d) -> p h d", h=16),
                in1=AR1[:, 3584:4096].rearrange("p (h d) -> p h d", h=16),
                op=ALU.subtract), inc=None)
            P.op("dve", lambda e, qo=qo: e.tensor_tensor(
                out=qo,
                in0=AR1[:, 4096:4608].rearrange("p (h d) -> p h d", h=16),
                in1=AR1[:, 4608:5120].rearrange("p (h d) -> p h d", h=16),
                op=ALU.add), inc=None)
            P.drain("dve")
            for hh in range(16):
                P.op("dve", lambda e, hh=hh, sl=sl, tq=tq: e.scalar_tensor_tensor(
                    out=ropescr[:, (hh % 4) * 512:(hh % 4) * 512 + 192],
                    in0=qtm[sl][:, hh * 192:(hh + 1) * 192], scalar=1.0,
                    in1=qtm[sl][:, hh * 192:(hh + 1) * 192],
                    op0=ALU.mult, op1=ALU.mult,
                    accum_out=ssq_all[:, tq * 16 + hh:tq * 16 + hh + 1]),
                    inc=(s_dve if hh == 15 else None))
                if hh % 4 == 3:
                    P.drain("dve")
            rope_done = s_dve.n
            cs_free[sl] = (s_dve, rope_done)
            P.wait("sp", s_dve, rope_done)
            P.op("sp", lambda e, tq=tq, sl=sl: e.dma_start(
                out=qspill[tq * 128:(tq + 1) * 128, :], in_=qtm[sl][:, :]),
                inc=sqs[sl], inck=16)
            qtm_free[sl] = (sqs[sl], sqs[sl].n)

        a2d = s_dve.n
        P.wait("dve", s_dve, a2d)
        for hh in range(16):
            P.op("dve", lambda e, hh=hh: e.reduce_max(
                out=colscr[:, 64 + hh:65 + hh],
                in_=ssq_all[:, 0:NT * 16].rearrange(
                    "p (t h) -> p h t", h=16)[:, hh, :],
                axis=AX.X), inc=(s_dve if hh == 15 else None))
        mq1 = s_dve.n
        P.wait("pe", s_dve, mq1)
        P.op("pe", lambda e: e.transpose(g5[0:16, 0:128], colscr[:, 64:80],
                                         identf[:, :]), inc=s_pe)
        mq2 = s_pe.n
        P.wait("dve", s_pe, mq2)
        P.op("dve", lambda e: e.reduce_max(out=colscr[0:16, 80:81],
                                           in_=g5.ap()[0:16, 0:128],
                                           axis=AX.X), inc=s_dve)
        mq3 = s_dve.n
        P.wait("pe", s_dve, mq3)
        P.op("pe", lambda e: e.transpose(g6[0:1, 0:16], colscr[0:16, 80:81],
                                         identf[0:16, 0:16]), inc=s_pe)
        mq4 = s_pe.n
        P.wait("act", s_pe, mq4)
        P.op("act", lambda e: e.activation(out=rowscr[0:1, 0:16],
                                           in_=g6.ap()[0:1, 0:16],
                                           func=AF.Ln, bias=fc[0:1, 3:4]), inc=None)
        P.drain("act")
        P.op("act", lambda e: e.activation(out=rowscr[0:1, 16:32],
                                           in_=rowscr[0:1, 0:16],
                                           func=AF.Exp, scale=0.5), inc=s_act)
        maxq_done = s_act.n

        barrier()
        kvcfm = [vbf(AR2, c * (T // 2), T) for c in range(4)]
        kropeF = vbf(AR2, 2 * T, T)
        kvaw_bf = vbf(AR1, 0, 16 * 576)
        kvtmp = v32(AR1, 8704, 512)
        ln1tmK = vbf(AR1, 9216, 2048)
        ln1fmK = vbf(AR1, 10240, 2048)
        kvc_tm = vbf(AR1, 11264, 512)
        krope_tm = vbf(AR1, 11520, 64)
        sqscrK = v32(AR1, 11552, 512)
        ropescrK = v32(AR1, 12064, 128)
        cosb = v32(AR1, 12288, NKT * 32)
        sinb = v32(AR1, 12288 + NKT * 32, NKT * 32)

        P.op("pool", lambda e: e.dma_start(
            out=AR1.ap().bitcast(BF16)[:, 0:16 * 576].rearrange(
                "p (c n) -> p c n", c=16),
            in_=kv_a_w.ap().rearrange("(c p) n -> p c n", p=128)),
            inc=s_wa, inck=16)
        AW3 = s_wa.n
        P.op("sp", lambda e: e.dma_start(
            out=cosb.rearrange("p (n c) -> p n c", c=32),
            in_=cos_all.ap().rearrange("(n p) c -> p n c", p=128)),
            inc=s_kv, inck=16)
        P.op("sp", lambda e: e.dma_start(
            out=sinb.rearrange("p (n c) -> p n c", c=32),
            in_=sin_all.ap().rearrange("(n p) c -> p n c", p=128)),
            inc=s_kv, inck=16)
        CS_LD = s_kv.n

        xt_free = [None, None]
        fm_evacK = 0
        kvst_done = 0
        for i in range(NKT):
            sl = i % 2
            if xt_free[sl] is not None:
                P.wait("sp", *xt_free[sl])
            P.op("sp", lambda e, i=i, sl=sl: e.dma_start(
                out=AR1[:, 4608 + sl * 2048:4608 + sl * 2048 + 2048],
                in_=x_batch[i * 128:(i + 1) * 128, :]), inc=sx[sl], inck=16)
            x_ld = sx[sl].n
            P.wait("dve", sx[sl], x_ld)
            for j in range(4):
                P.op("dve", lambda e, sl=sl, j=j: e.bn_stats(
                    out=bnst[:, j * 6:(j + 1) * 6],
                    in_=AR1[:, 4608 + sl * 2048 + j * 512:4608 + sl * 2048 + (j + 1) * 512]),
                    inc=None)
            P.drain("dve")
            P.op("dve", lambda e: e.bn_aggr(out=mvt[:, 0:2], in_=bnst[:, :]),
                 inc=s_dve)
            bn_done = s_dve.n
            P.wait("act", s_dve, bn_done)
            P.op("act", lambda e: e.activation(out=rstdt[:, 0:1], in_=mvt[:, 1:2],
                                               func=AF.Ln, bias=fc[:, 0:1]), inc=None)
            P.drain("act")
            P.op("act", lambda e: e.activation(out=rstdt[:, 1:2], in_=rstdt[:, 0:1],
                                               func=AF.Exp, scale=-0.5), inc=s_act)
            rstd_done = s_act.n
            P.wait("dve", s_act, rstd_done)
            P.op("dve", lambda e, sl=sl: e.tensor_scalar(
                out=ln1tmK, in0=AR1[:, 4608 + sl * 2048:4608 + sl * 2048 + 2048],
                scalar1=mvt[:, 0:1], scalar2=rstdt[:, 1:2],
                op0=ALU.subtract, op1=ALU.mult), inc=s_dve)
            ln1_done = s_dve.n
            xt_free[sl] = (s_dve, ln1_done)
            P.wait("pe", s_dve, ln1_done)
            if i > 0:
                P.wait("pe", s_act, fm_evacK)
            for j in range(16):
                P.op("pe", lambda e, j=j: e.transpose(
                    (psA if j < 8 else psB)[:, (j % 8) * 128:(j % 8) * 128 + 128],
                    ln1tmK[:, j * 128:(j + 1) * 128], identb[:, :]),
                    inc=(s_pe if j == 15 else None))
            tp_done = s_pe.n
            P.wait("act", s_pe, tp_done)
            P.op("act", lambda e: e.copy(out=ln1fmK[:, 0:1024], in_=psA[:, :]), inc=None)
            P.drain("act")
            P.op("act", lambda e: e.copy(out=ln1fmK[:, 1024:2048], in_=psB[:, :]),
                 inc=s_act)
            fm_evacK = s_act.n
            P.wait("pe", s_act, fm_evacK)
            if i == 0:
                P.wait("pe", s_wa, AW3)
            for c in range(16):
                P.op("pe", lambda e, c=c: e.matmul(
                    ps1[:, :], ln1fmK[:, c * 128:(c + 1) * 128],
                    kvaw_bf[:, c * 576:c * 576 + 512],
                    start=(c == 0), stop=(c == 15)), inc=None)
            for c in range(16):
                P.op("pe", lambda e, c=c: e.matmul(
                    ps2[:, 0:64], ln1fmK[:, c * 128:(c + 1) * 128],
                    kvaw_bf[:, c * 576 + 512:c * 576 + 576],
                    start=(c == 0), stop=(c == 15)),
                    inc=(s_pe if c == 15 else None))
            kva_done = s_pe.n
            P.wait("act", s_pe, kva_done)
            P.op("act", lambda e: e.copy(out=kvtmp, in_=ps1[:, :]), inc=s_act)
            kvtmp_done = s_act.n
            P.wait("dve", s_act, kvtmp_done)
            P.op("dve", lambda e: e.scalar_tensor_tensor(
                out=sqscrK, in0=kvtmp, scalar=1.0, in1=kvtmp,
                op0=ALU.mult, op1=ALU.mult, accum_out=rstdt[:, 2:3]), inc=s_dve)
            kvssq_done = s_dve.n
            P.wait("act", s_dve, kvssq_done)
            P.op("act", lambda e: e.activation(out=rstdt[:, 3:4], in_=rstdt[:, 2:3],
                                               func=AF.Ln, scale=1.0 / KVL,
                                               bias=fc[:, 1:2]), inc=None)
            P.drain("act")
            P.op("act", lambda e: e.activation(out=rstdt[:, 2:3], in_=rstdt[:, 3:4],
                                               func=AF.Exp, scale=-0.5), inc=s_act)
            kvrstd_done = s_act.n
            P.wait("dve", s_act, kvrstd_done)
            P.op("dve", lambda e: e.tensor_scalar(
                out=kvc_tm, in0=kvtmp, scalar1=rstdt[:, 2:3], scalar2=None,
                op0=ALU.mult), inc=s_dve)
            kvc_done = s_dve.n
            if i == 0:
                P.wait("dve", s_kv, CS_LD)
            ctK = cosb[:, i * 32:(i + 1) * 32]
            stK = sinb[:, i * 32:(i + 1) * 32]
            ke = ps2[:, 0:64:2]
            ko = ps2[:, 1:64:2]
            P.op("dve", lambda e, ke=ke, ctK=ctK: e.tensor_tensor(
                out=ropescrK[:, 0:32], in0=ke, in1=ctK, op=ALU.mult), inc=None)
            P.op("dve", lambda e, ko=ko, stK=stK: e.tensor_tensor(
                out=ropescrK[:, 32:64], in0=ko, in1=stK, op=ALU.mult), inc=None)
            P.drain("dve")
            P.op("dve", lambda e: e.tensor_tensor(
                out=krope_tm[:, 0:64:2], in0=ropescrK[:, 0:32],
                in1=ropescrK[:, 32:64], op=ALU.subtract), inc=None)
            P.op("dve", lambda e, ko=ko, ctK=ctK: e.tensor_tensor(
                out=ropescrK[:, 64:96], in0=ko, in1=ctK, op=ALU.mult), inc=None)
            P.op("dve", lambda e, ke=ke, stK=stK: e.tensor_tensor(
                out=ropescrK[:, 96:128], in0=ke, in1=stK, op=ALU.mult), inc=None)
            P.drain("dve")
            P.op("dve", lambda e: e.tensor_tensor(
                out=krope_tm[:, 1:64:2], in0=ropescrK[:, 64:96],
                in1=ropescrK[:, 96:128], op=ALU.add), inc=s_dve)
            krope_done = s_dve.n
            P.wait("pe", s_dve, krope_done)
            if i > 0:
                P.wait("pe", s_act, kvst_done)
            for j in range(4):
                P.op("pe", lambda e, j=j: e.transpose(
                    psA[:, j * 128:(j + 1) * 128], kvc_tm[:, j * 128:(j + 1) * 128],
                    identb[:, :]), inc=None)
            P.op("pe", lambda e: e.transpose(
                psA[0:64, 512:640], krope_tm[:, 0:64], identb[:, :]),
                inc=s_pe)
            kvtp_done = s_pe.n
            P.wait("act", s_pe, kvtp_done)
            for cc in range(4):
                P.op("act", lambda e, cc=cc, i=i: e.copy(
                    out=kvcfm[cc][:, i * 128:(i + 1) * 128],
                    in_=psA[:, cc * 128:(cc + 1) * 128]), inc=None)
            P.drain("act")
            P.op("act", lambda e, i=i: e.copy(
                out=kropeF[0:64, i * 128:(i + 1) * 128],
                in_=psA[0:64, 512:640]), inc=s_act)
            kvst_done = s_act.n

        barrier()
        V2 = vbf(AR1, 0, NKT * 256)
        K_h = [vbf(AR1, 8192 + s * 4096, T) for s in range(2)]
        kvb2f = vbf(AR1, 16384, 4 * 512)
        masksl = vbf(AR1, 17408, 8 * 512)
        qtw = [vbf(AR1, 19456 + s * 384, 768) for s in range(2)]
        Qn_w = [vbf(AR1, 20224 + s * 256, 512) for s in range(2)]
        Qr_w = [vbf(AR1, 20736 + s * 256, 512) for s in range(2)]
        expS = [vbf(AR1, 21248 + s * 512, 1024) for s in range(2)]
        bcastf = v32(AR1, 22272, 512)
        recf = bcastf
        recf2 = bcastf2
        dacc = [v32(AR1, 22784 + s * 512, 512) for s in range(2)]
        outh_sl = [vbf(AR1, 23808 + s * 256, 512) for s in range(2)]
        sqr = [s_qr0, s_qr1]
        assert 23808 + 512 <= 24576

        SQRT_QK = math.sqrt(float(QK))
        exp_bank_free = [0, 0]
        v2_evac = 0
        kh_evac = 0
        g5_free = 0
        g12_expfree = [0, 0]
        expS_free = [0, 0]
        qtw_free = [None, None]
        qw_tp_ev = 0
        mask_used = 0
        kvb_fold_done = 0
        kvb_pe_used = 0
        outh_free = [None, None]
        psO_free = [0, 0]
        g3_ev = 0
        for hp in range(8):
            if hp == 0:
                P.op("pool", lambda e, hp=hp: e.dma_start(
                    out=kvb2f.rearrange("p (c n) -> p c n", c=4),
                    in_=kv_b_w.ap().rearrange("(c p) n -> p c n", p=128)[
                        :, :, hp * 512:(hp + 1) * 512]), inc=s_kb, inck=16)
            KB = s_kb.n
            P.wait("dve", s_kb, KB)
            for cc in range(4):
                P.op("dve", lambda e, cc=cc: e.tensor_scalar(
                    out=kvb2f[:, cc * 512:(cc + 1) * 512],
                    in0=kvb2f[:, cc * 512:(cc + 1) * 512],
                    scalar1=wkvc_sb[:, cc:cc + 1], scalar2=None, op0=ALU.mult),
                    inc=(s_dve if cc == 3 else None))
            kvb_fold_done = s_dve.n
            P.wait("pe", s_dve, kvb_fold_done)
            P.wait("pe", s_dve, g5_free)
            for kt in range(NKT):
                psV = (g3, g4)[kt % 2]
                P.wait("pe", s_act, exp_bank_free[kt % 2])
                for cc in range(4):
                    P.op("pe", lambda e, kt=kt, cc=cc, psV=psV: e.matmul(
                        psV[:, 0:256],
                        kvcfm[cc][:, kt * 128:(kt + 1) * 128],
                        kvb2f[:, cc * 512:(cc + 1) * 512].rearrange(
                            "p (h z) -> p h z", h=2)[:, :, 128:256],
                        start=(cc == 0), stop=(cc == 3)),
                        inc=(s_pe if cc == 3 else None))
                vkt = s_pe.n
                P.wait("act", s_pe, vkt)
                P.op("act", lambda e, kt=kt, psV=psV: e.copy(
                    out=V2[:, kt * 256:(kt + 1) * 256], in_=psV.ap()[:, 0:256]),
                    inc=s_act)
                v2_evac = s_act.n
                exp_bank_free[kt % 2] = s_act.n
            for hl in range(2):
                for win in range(T // 512):
                    psK = (g3, g4)[win % 2]
                    P.wait("pe", s_act, exp_bank_free[win % 2])
                    for cc in range(4):
                        P.op("pe", lambda e, win=win, cc=cc, psK=psK, hl=hl: e.matmul(
                            psK[:, :],
                            kvb2f[:, cc * 512 + hl * 256:cc * 512 + hl * 256 + 128],
                            kvcfm[cc][:, win * 512:(win + 1) * 512],
                            start=(cc == 0), stop=(cc == 3)),
                            inc=(s_pe if cc == 3 else None))
                    kkt = s_pe.n
                    P.wait("act", s_pe, kkt)
                    P.op("act", lambda e, win=win, psK=psK, hl=hl: e.copy(
                        out=K_h[hl][:, win * 512:(win + 1) * 512], in_=psK.ap()),
                        inc=s_act)
                    kh_evac = s_act.n
                    exp_bank_free[win % 2] = s_act.n
            kvb_pe_used = s_pe.n
            if hp + 1 < 8:
                P.wait("pool", s_pe, kvb_pe_used)
                P.wait("pool", s_dve, kvb_fold_done)
                P.op("pool", lambda e, hp=hp: e.dma_start(
                    out=kvb2f.rearrange("p (c n) -> p c n", c=4),
                    in_=kv_b_w.ap().rearrange("(c p) n -> p c n", p=128)[
                        :, :, (hp + 1) * 512:(hp + 2) * 512]), inc=s_kb, inck=16)
            P.wait("dve", s_act, kh_evac)
            if hp == 0:
                P.op("dve", lambda e: e.memset(colscr[:, 8:9], 0.0), inc=None)
                P.drain("dve")
                P.op("dve", lambda e: e.reduce_max(
                    out=colscr[0:64, 8:9], in_=kropeF[0:64, :], axis=AX.X,
                    apply_absolute_value=True), inc=None)
                P.drain("dve")
                P.wait("dve", s_act, maxq_done)
            for hl in range(2):
                h = hp * 2 + hl
                P.op("dve", lambda e, hl=hl: e.reduce_max(
                    out=colscr[:, 9:10], in_=K_h[hl][:, :], axis=AX.X,
                    apply_absolute_value=True), inc=None)
                P.drain("dve")
                P.op("dve", lambda e: e.tensor_tensor(
                    out=colscr[:, 10:11], in0=colscr[:, 9:10],
                    in1=colscr[:, 8:9], op=ALU.max), inc=s_dve)
                mk1 = s_dve.n
                P.wait("pe", s_dve, mk1)
                P.wait("pe", s_act, s_act.n)
                P.op("pe", lambda e: e.transpose(
                    g6[0:1, 0:128], colscr[:, 10:11], identf[:, :]), inc=s_pe)
                mk2 = s_pe.n
                P.wait("dve", s_pe, mk2)
                P.op("dve", lambda e: e.reduce_max(
                    out=rowscr[0:1, 40:41], in_=g6.ap()[0:1, 0:128], axis=AX.X),
                    inc=None)
                P.drain("dve")
                P.op("dve", lambda e, h=h: e.tensor_tensor(
                    out=rowscr[0:1, 41:42], in0=rowscr[0:1, 16 + h:17 + h],
                    in1=rowscr[0:1, 40:41], op=ALU.mult), inc=None)
                P.drain("dve")
                P.op("dve", lambda e, hl=hl: e.tensor_scalar(
                    out=nb_bf[0:1, hl:hl + 1], in0=rowscr[0:1, 41:42],
                    scalar1=-SQRT_QK, scalar2=None, op0=ALU.mult), inc=s_dve)
                negc_done = s_dve.n
                P.wait("pe", s_dve, negc_done)
                P.wait("pe", s_dve, psO_free[1])
                P.op("pe", lambda e, hl=hl: e.matmul(
                    g6[:, 0:1], onesb[0:1, 0:128], nb_bf[0:1, hl:hl + 1],
                    start=True, stop=True), inc=s_pe)
                negb_mm = s_pe.n
                P.wait("act", s_pe, negb_mm)
                P.op("act", lambda e, hl=hl: e.copy(out=colscr[:, 12 + hl:13 + hl],
                                                    in_=g6.ap()[:, 0:1]), inc=s_act)
            negc_bc = s_act.n

            for w in range(NCH):
                lim = LIMS[w]
                kmax2 = lim // 2
                mstart = lim - 16
                for hl in range(2):
                    h = hp * 2 + hl
                    sl = hl
                    if qtw_free[sl] is not None:
                        P.wait("sp", *qtw_free[sl])
                    P.op("sp", lambda e, h=h, w=w, sl=sl: e.dma_start(
                        out=qtw[sl].rearrange("p (tt d) -> p tt d", tt=4),
                        in_=qspill[w * 512:(w + 1) * 512,
                                   h * 192:(h + 1) * 192].rearrange(
                            "(tt p) d -> p tt d", p=128)), inc=sqr[sl], inck=16)
                    q_ld = sqr[sl].n
                    P.wait("pe", sqr[sl], q_ld)
                    P.wait("pe", s_act, qw_tp_ev)
                    if w == 0 and hl == 0:
                        P.wait("pe", s_act, max(kh_evac, v2_evac, negc_bc))
                        P.wait("pe", s_dve, g5_free)
                    for tt in range(4):
                        P.op("pe", lambda e, tt=tt, sl=sl: e.transpose(
                            psA[:, tt * 128:(tt + 1) * 128],
                            qtw[sl][:, tt * 192:tt * 192 + 128], identb[:, :]),
                            inc=None)
                    for tt in range(4):
                        P.op("pe", lambda e, tt=tt, sl=sl: e.transpose(
                            psB[0:64, tt * 128:(tt + 1) * 128],
                            qtw[sl][:, tt * 192 + 128:(tt + 1) * 192], identb[:, :]),
                            inc=(s_pe if tt == 3 else None))
                    qtp2 = s_pe.n
                    qtw_free[sl] = (s_pe, qtp2)
                    P.wait("act", s_pe, qtp2)
                    P.op("act", lambda e, hl=hl: e.copy(out=Qn_w[hl][:, :],
                                                        in_=psA[:, 0:512]),
                         inc=None)
                    P.drain("act")
                    P.op("act", lambda e, hl=hl: e.copy(out=Qr_w[hl][0:64, :],
                                                        in_=psB[0:64, 0:512]),
                         inc=s_act)
                    qw_tp_ev = s_act.n

                P.wait("pe", s_act, qw_tp_ev)
                for dw in range(16):
                    P.op("pe", lambda e: e.matmul(
                        g3[:, :], onesb[:, :], hsu_sb[:, 0:512],
                        start=True, stop=True),
                        inc=(s_pe if dw == 15 else None))
                warm_done = s_pe.n
                mk_ld = [0, 0]
                att_scores = {}
                att_expd = {}
                att_maskd = {}
                att_dacc = {}
                psS_h = (g1, g2)
                psO_h = (g5, g6)
                mhalf_loaded = [False, False]

                def load_mask_half(mh, w=w):
                    P.wait("sp", s_dve, s_dve.n)
                    P.op("sp", lambda e, w=w, mh=mh: e.dma_start(
                        out=masksl[:, :],
                        in_=masks_in.ap()[:, w * 8192 + mh * 4096:
                                          w * 8192 + (mh + 1) * 4096]),
                        inc=s_mk, inck=16)
                    mk_ld[mh] = s_mk.n

                def emit_attnv(hl, j, kmax2=kmax2):
                    first = (j == 0)
                    last = (j == kmax2 - 1)
                    P.wait("pe", s_act, att_expd[(hl, j)])
                    if first:
                        P.wait("pe", s_dve, psO_free[hl])
                    if (hl, j) in att_maskd:
                        P.wait("pe", s_dve, att_maskd[(hl, j)])
                    for sub in range(2):
                        kt = 2 * j + sub
                        P.op("pe", lambda e, kt=kt, hl=hl, sub=sub,
                             first=first, last=last: e.matmul(
                            psO_h[hl][:, :],
                            V2[:, kt * 256 + hl * 128:kt * 256 + (hl + 1) * 128],
                            expS[hl][:, sub * 512:(sub + 1) * 512],
                            start=(first and sub == 0),
                            stop=(last and sub == 1)),
                            inc=(s_pe if sub == 1 else None))
                    expS_free[hl] = s_pe.n

                for kg in range(kmax2):
                    need_mh0 = (2 * kg + 1 >= mstart) and not mhalf_loaded[0]
                    need_mh1 = (2 * kg + 1 >= mstart + 8) and not mhalf_loaded[1]
                    if need_mh0:
                        load_mask_half(0)
                        mhalf_loaded[0] = True
                    if need_mh1:
                        load_mask_half(1)
                        mhalf_loaded[1] = True
                    for hl in range(2):
                        psS = psS_h[hl]
                        if kg >= 1:
                            emit_attnv(hl, kg - 1)
                        P.wait("pe", s_act, g12_expfree[hl])
                        if kg == 0:
                            P.wait("pe", s_act, qw_tp_ev)
                        for sub in range(2):
                            kt = 2 * kg + sub
                            P.op("pe", lambda e, kt=kt, psS=psS, sub=sub, hl=hl: e.matmul(
                                psS[:, sub * 512:(sub + 1) * 512],
                                K_h[hl][:, kt * 128:(kt + 1) * 128], Qn_w[hl][:, :],
                                start=True, stop=False), inc=None)
                            P.op("pe", lambda e, kt=kt, psS=psS, sub=sub, hl=hl: e.matmul(
                                psS[:, sub * 512:(sub + 1) * 512],
                                kropeF[0:64, kt * 128:(kt + 1) * 128],
                                Qr_w[hl][0:64, :],
                                start=False, stop=True),
                                inc=(s_pe if sub == 1 else None))
                        att_scores[(hl, kg)] = s_pe.n
                        P.wait("act", s_pe, att_scores[(hl, kg)])
                        P.wait("act", s_pe, expS_free[hl])
                        if (hl, kg - 1) in att_dacc:
                            P.wait("act", s_dve, att_dacc[(hl, kg - 1)])
                        P.op("act", lambda e, hl=hl, psS=psS: e.activation(
                            out=expS[hl][:, :], in_=psS[:, :], func=AF.Exp,
                            bias=colscr[:, 12 + hl:13 + hl]), inc=s_act)
                        att_expd[(hl, kg)] = s_act.n
                        g12_expfree[hl] = s_act.n
                        diag_subs = [sub for sub in range(2)
                                     if 2 * kg + sub >= mstart]
                        P.wait("dve", s_act, att_expd[(hl, kg)])
                        if diag_subs:
                            for sub in diag_subs:
                                kt = 2 * kg + sub
                                pat = kt - mstart
                                P.wait("dve", s_mk, mk_ld[pat // 8])
                                P.op("dve", lambda e, hl=hl, sub=sub, pat=pat: e.tensor_tensor(
                                    out=expS[hl][:, sub * 512:(sub + 1) * 512],
                                    in0=expS[hl][:, sub * 512:(sub + 1) * 512],
                                    in1=masksl[:, (pat % 8) * 512:(pat % 8) * 512 + 512],
                                    op=ALU.mult),
                                    inc=s_dve)
                            att_maskd[(hl, kg)] = s_dve.n
                            P.drain("dve")
                        if kg == 0:
                            P.op("dve", lambda e, hl=hl: e.tensor_copy(
                                out=dacc[hl][:, :], in_=expS[hl][:, 0:512]),
                                inc=None)
                        else:
                            P.op("dve", lambda e, hl=hl: e.tensor_tensor(
                                out=dacc[hl][:, :], in0=dacc[hl][:, :],
                                in1=expS[hl][:, 0:512], op=ALU.add), inc=None)
                        P.drain("dve")
                        P.op("dve", lambda e, hl=hl: e.tensor_tensor(
                            out=dacc[hl][:, :], in0=dacc[hl][:, :],
                            in1=expS[hl][:, 512:1024], op=ALU.add), inc=s_dve)
                        P.drain("dve")
                        att_dacc[(hl, kg)] = s_dve.n
                mask_used = s_dve.n
                for hl in range(2):
                    emit_attnv(hl, kmax2 - 1)
                dnb = (g3, g4)
                recfb = (recf, recf2)
                recbb = (recb, recb2)
                bcfb = (bcastf, bcastf2)
                dn_red = [0, 0]
                for hl in range(2):
                    P.wait("pe", s_dve, att_dacc[(hl, kmax2 - 1)])
                    if hl == 0:
                        P.wait("pe", s_act, qw_tp_ev)
                        P.wait("pe", s_act, g3_ev)
                    P.op("pe", lambda e, hl=hl: e.matmul(
                        dnb[hl][0:1, 0:512], onesf[:, 0:1], dacc[hl][:, :],
                        start=True, stop=True), inc=s_pe)
                    dn_red[hl] = s_pe.n
                recb_d = [0, 0]
                for hl in range(2):
                    P.wait("dve", s_pe, dn_red[hl])
                    P.drain("dve")
                    P.op("dve", lambda e, hl=hl: e.reciprocal(
                        out=recfb[hl][0:1, :], in_=dnb[hl].ap()[0:1, 0:512]),
                        inc=s_dve)
                    P.drain("dve")
                    P.op("dve", lambda e, hl=hl: e.tensor_copy(
                        out=recbb[hl][0:1, :], in_=recfb[hl][0:1, :]),
                        inc=s_dve)
                    recb_d[hl] = s_dve.n
                bc_ev = [0, 0]
                for hl in range(2):
                    P.wait("pe", s_dve, recb_d[hl])
                    P.op("pe", lambda e, hl=hl: e.matmul(
                        dnb[hl][:, 0:512], onesb[0:1, 0:128],
                        recbb[hl][0:1, :], start=True, stop=True), inc=s_pe)
                    bc_mm = s_pe.n
                    P.wait("act", s_pe, bc_mm)
                    P.op("act", lambda e, hl=hl: e.copy(
                        out=bcfb[hl][:, :], in_=dnb[hl].ap()[:, 0:512]),
                        inc=s_act)
                    bc_ev[hl] = s_act.n
                g3_ev = bc_ev[1]
                for hl in range(2):
                    h = hp * 2 + hl
                    attv_done = expS_free[hl]
                    P.wait("dve", s_act, bc_ev[hl])
                    P.wait("dve", s_pe, attv_done)
                    if outh_free[hl] is not None:
                        P.wait("dve", *outh_free[hl])
                    P.op("dve", lambda e, hl=hl: e.tensor_tensor(
                        out=outh_sl[hl][:, :], in0=psO_h[hl].ap()[:, :],
                        in1=bcfb[hl][:, :], op=ALU.mult), inc=s_dve)
                    g5_free = s_dve.n
                    psO_free[hl] = s_dve.n
                    P.wait("sp", s_dve, g5_free)
                    sc2h = (s_c20, s_c21)[hl]
                    P.op("sp", lambda e, h=h, w=w, hl=hl: e.dma_start(
                        out=ospill[h * 128:(h + 1) * 128, w * 512:(w + 1) * 512],
                        in_=outh_sl[hl][:, :]), inc=sc2h, inck=16)
                    outh_free[hl] = (sc2h, sc2h.n)
                qw_tp_ev = bc_ev[1]

        barrier()
        oww = vbf(AR1, 0, 16 * 2048)
        ohw = [vbf(AR1, 16384 + s * 2048, 16 * 256) for s in range(2)]
        aof32 = v32(AR2, 12288, 4096)
        xb2 = [v32(AR2, s * 2048, 2048) for s in range(2)]
        atm = v32(AR2, 4096, 2048)
        lnout = v32(AR2, 6144, 2048)
        x2t = [v32(AR2, 8192 + s * 2048, 2048) for s in range(2)]
        soh = [s_oh0, s_oh1]
        sx2 = [s_x2r0, s_x2r1]

        P.op("pool", lambda e: e.dma_start(
            out=oww.rearrange("p (c n) -> p c n", c=16),
            in_=o_w.ap().rearrange("(c p) n -> p c n", p=128)),
            inc=s_ow, inck=16)
        OW_LD = s_ow.n
        def psdc(dc):
            if dc < 4:
                return g1, (dc % 4) * 256
            if dc < 8:
                return g2, (dc % 4) * 256
            return (g3, g4, g5, g6)[(dc - 8) // 2], ((dc - 8) % 2) * 256

        ohw_free = [None, None]
        x2t_free = [None, None]
        xb2_free = [None, None]
        b2_ev = 0
        for w in range(OWN // 256):
            sl = w % 2
            if ohw_free[sl] is not None:
                P.wait("sp", *ohw_free[sl])
            P.op("sp", lambda e, w=w, sl=sl: e.dma_start(
                out=ohw[sl].rearrange("p (hh t) -> p hh t", hh=16),
                in_=ospill[:, :].rearrange("(hh p) t -> p hh t", p=128)[
                    :, :, w * 256:(w + 1) * 256]), inc=soh[sl], inck=16)
            oh_ld = soh[sl].n
            if w > 0:
                P.wait("pe", s_act, b2_ev)
                for dw in range(16):
                    P.op("pe", lambda e: e.matmul(
                        g1[:, 0:512], onesb[:, :], hsu_sb[:, 0:512],
                        start=True, stop=True),
                        inc=(s_pe if dw == 15 else None))
            P.wait("pe", soh[sl], oh_ld)
            if w == 0:
                P.wait("pe", s_ow, OW_LD)
            for dc in range(16):
                pst, col = psdc(dc)
                for hh in range(16):
                    P.op("pe", lambda e, dc=dc, hh=hh, pst=pst, col=col, sl=sl: e.matmul(
                        pst[:, col:col + 256],
                        oww[:, hh * 2048 + dc * 128:hh * 2048 + (dc + 1) * 128],
                        ohw[sl][:, hh * 256:(hh + 1) * 256],
                        start=(hh == 0), stop=(hh == 15)),
                        inc=(s_pe if (dc == 15 and hh == 15) else None))
            ow_done = s_pe.n
            ohw_free[sl] = (s_pe, ow_done)
            P.wait("act", s_pe, ow_done)
            P.op("act", lambda e: e.copy(out=aof32[:, 0:1024], in_=g1[:, :]), inc=None)
            P.op("act", lambda e: e.copy(out=aof32[:, 1024:2048], in_=g2[:, :]), inc=None)
            P.op("act", lambda e: e.copy(out=aof32[:, 2048:2560], in_=g3[:, :]), inc=None)
            P.op("act", lambda e: e.copy(out=aof32[:, 2560:3072], in_=g4[:, :]), inc=None)
            P.op("act", lambda e: e.copy(out=aof32[:, 3072:3584], in_=g5[:, :]), inc=None)
            P.drain("act")
            P.op("act", lambda e: e.copy(out=aof32[:, 3584:4096], in_=g6[:, :]),
                 inc=s_act)
            b2_ev = s_act.n
            for tau in range(2):
                i_own = w * 2 + tau
                if xb2_free[sl] is not None and tau == 0:
                    P.wait("sp", *xb2_free[sl])
                if tau == 0:
                    P.op("sp", lambda e, w=w, sl=sl: e.dma_start(
                        out=AR2[:, sl * 2048:sl * 2048 + 2048],
                        in_=x_own[w * 256:w * 256 + 128, :]), inc=sx[0] if False else sx2[sl], inck=16)
                else:
                    P.op("sp", lambda e, w=w, sl=sl: e.dma_start(
                        out=AR2[:, sl * 2048:sl * 2048 + 2048],
                        in_=x_own[w * 256 + 128:w * 256 + 256, :]), inc=sx2[sl], inck=16)
                xld = sx2[sl].n
                P.wait("pe", s_act, b2_ev)
                if tau == 1 or w > 0:
                    P.wait("pe", s_dve, s_dve.n)
                for dc in range(16):
                    P.op("pe", lambda e, dc=dc, tau=tau: e.transpose(
                        (g1 if dc < 8 else g2)[:, (dc % 8) * 128:(dc % 8) * 128 + 128],
                        aof32[:, dc * 256 + tau * 128:dc * 256 + (tau + 1) * 128],
                        identf[:, :]), inc=(s_pe if dc == 15 else None))
                tp2 = s_pe.n
                P.wait("act", s_pe, tp2)
                P.op("act", lambda e: e.copy(out=atm[:, 0:1024], in_=g1[:, :]), inc=None)
                P.drain("act")
                P.op("act", lambda e: e.copy(out=atm[:, 1024:2048], in_=g2[:, :]),
                     inc=s_act)
                atm_ev = s_act.n
                P.wait("dve", s_act, atm_ev)
                for j in range(4):
                    P.op("dve", lambda e, j=j: e.bn_stats(
                        out=bnst[:, j * 6:(j + 1) * 6],
                        in_=atm[:, j * 512:(j + 1) * 512]), inc=None)
                P.drain("dve")
                P.op("dve", lambda e: e.bn_aggr(out=mvt[:, 0:2], in_=bnst[:, :]),
                     inc=s_dve)
                bn2 = s_dve.n
                P.wait("act", s_dve, bn2)
                P.op("act", lambda e: e.activation(out=rstdt[:, 0:1], in_=mvt[:, 1:2],
                                                   func=AF.Ln, bias=fc[:, 0:1]), inc=None)
                P.drain("act")
                P.op("act", lambda e: e.activation(out=rstdt[:, 1:2], in_=rstdt[:, 0:1],
                                                   func=AF.Exp, scale=-0.5), inc=s_act)
                rstd2 = s_act.n
                P.wait("dve", s_act, rstd2)
                P.op("dve", lambda e: e.tensor_scalar(
                    out=lnout[:, :], in0=atm[:, :], scalar1=mvt[:, 0:1],
                    scalar2=rstdt[:, 1:2], op0=ALU.subtract, op1=ALU.mult),
                    inc=None)
                P.drain("dve")
                P.wait("dve", sx2[sl], xld)
                if x2t_free[sl] is not None:
                    P.wait("dve", *x2t_free[sl])
                P.op("dve", lambda e, sl=sl: e.tensor_tensor(
                    out=x2t[sl][:, :], in0=lnout[:, :],
                    in1=AR2[:, sl * 2048:sl * 2048 + 2048], op=ALU.add),
                    inc=s_dve)
                x2_done = s_dve.n
                xb2_free[sl] = (s_dve, x2_done)
                P.wait("sp", s_dve, x2_done)
                P.op("sp", lambda e, i_own=i_own, sl=sl: e.dma_start(
                    out=x2d[i_own * 128:(i_own + 1) * 128, :], in_=x2t[sl][:, :]),
                    inc=s_out, inck=16)
                x2t_free[sl] = (s_out, s_out.n)
            b2_ev = s_act.n

        barrier()
        wdnat = [vbf(AR1, s * 4096, 8192) for s in range(2)]
        wdTs = [vbf(AR1, 8192 + s * 4096, 8192) for s in range(2)]
        swd = [s_wd0, s_wd1]
        wdn_free = [None, None]
        wdts_free = [None, None]
        tpd_bank = [0, 0]
        for ob in range(16):
            sl = ob % 2
            if wdn_free[sl] is not None:
                P.wait("pool", *wdn_free[sl])
            P.op("pool", lambda e, ob=ob, sl=sl: e.dma_start(
                out=wdnat[sl][:, :], in_=w_down[ob * 128:(ob + 1) * 128, :]),
                inc=swd[sl], inck=16)
            wd_ld = swd[sl].n
            P.wait("pe", swd[sl], wd_ld)
            if wdts_free[sl] is not None:
                P.wait("pe", *wdts_free[sl])
            tpd = 0
            for grp in range(8):
                pst = (psA, psB)[grp % 2]
                P.wait("pe", s_act, tpd_bank[grp % 2])
                for j in range(8):
                    fc_ = grp * 8 + j
                    P.op("pe", lambda e, fc_=fc_, j=j, pst=pst, sl=sl: e.transpose(
                        pst[:, j * 128:(j + 1) * 128],
                        wdnat[sl][:, fc_ * 128:(fc_ + 1) * 128], identb[:, :]),
                        inc=(s_pe if j == 7 else None))
                tg = s_pe.n
                P.wait("act", s_pe, tg)
                P.op("act", lambda e, grp=grp, pst=pst, sl=sl: e.copy(
                    out=wdTs[sl][:, grp * 1024:(grp + 1) * 1024], in_=pst[:, :]),
                    inc=s_act)
                tpd = s_act.n
                tpd_bank[grp % 2] = s_act.n
            wdn_free[sl] = (s_pe, s_pe.n)
            swt = [s_wt0, s_wt1]
            P.wait("sp", s_act, tpd)
            P.op("sp", lambda e, ob=ob, sl=sl: e.dma_start(
                out=wdTd[:, ob * 128:(ob + 1) * 128].rearrange(
                    "(fc p) o -> p fc o", p=128),
                in_=wdTs[sl].rearrange("p (fc o) -> p fc o", fc=64)),
                inc=swt[sl], inck=16)
            wdts_free[sl] = (swt[sl], swt[sl].n)

        barrier()
        NWC = OWN // 512
        h_sb = vbf(AR1, 0, 16 * OWN)
        ln3fm = vbf(AR1, 16384, 16 * 512)
        ln3tm = vbf(AR1, 20480, 2048)
        xb3 = v32(AR1, 21504, 2048)
        fA = [v32(AR2, c * 512, 512) for c in range(16)]
        fB = [v32(AR2, 8192 + c * 512, 512) for c in range(16)]

        xb3_free = None
        ln3_ev = 0
        fw_ev_bank = [0, 0]
        for w in range(NWC):
            for tau in range(4):
                i_own = w * 4 + tau
                if xb3_free is not None:
                    P.wait("sp", *xb3_free)
                P.op("sp", lambda e, i_own=i_own: e.dma_start(
                    out=xb3[:, :], in_=x2d[i_own * 128:(i_own + 1) * 128, :]),
                    inc=s_x2r0, inck=16)
                x2ld = s_x2r0.n
                P.wait("dve", s_x2r0, x2ld)
                for j in range(4):
                    P.op("dve", lambda e, j=j: e.bn_stats(
                        out=bnst[:, j * 6:(j + 1) * 6],
                        in_=xb3[:, j * 512:(j + 1) * 512]), inc=None)
                P.drain("dve")
                P.op("dve", lambda e: e.bn_aggr(out=mvt[:, 0:2], in_=bnst[:, :]),
                     inc=s_dve)
                bn3 = s_dve.n
                P.wait("act", s_dve, bn3)
                P.op("act", lambda e: e.activation(out=rstdt[:, 0:1], in_=mvt[:, 1:2],
                                                   func=AF.Ln, bias=fc[:, 0:1]), inc=None)
                P.drain("act")
                P.op("act", lambda e: e.activation(out=rstdt[:, 1:2], in_=rstdt[:, 0:1],
                                                   func=AF.Exp, scale=-0.5), inc=s_act)
                rstd3 = s_act.n
                P.wait("dve", s_act, rstd3)
                P.op("dve", lambda e: e.tensor_scalar(
                    out=ln3tm[:, :], in0=xb3[:, :], scalar1=mvt[:, 0:1],
                    scalar2=rstdt[:, 1:2], op0=ALU.subtract, op1=ALU.mult),
                    inc=s_dve)
                ln3_done = s_dve.n
                xb3_free = (s_dve, ln3_done)
                P.wait("pe", s_dve, ln3_done)
                P.wait("pe", s_act, ln3_ev)
                for j in range(16):
                    P.op("pe", lambda e, j=j: e.transpose(
                        (psA if j < 8 else psB)[:, (j % 8) * 128:(j % 8) * 128 + 128],
                        ln3tm[:, j * 128:(j + 1) * 128], identb[:, :]),
                        inc=(s_pe if j == 15 else None))
                tp3 = s_pe.n
                P.wait("act", s_pe, tp3)
                P.op("act", lambda e, tau=tau: e.copy(
                    out=ln3fm.rearrange("p (c t) -> p c t", c=16)[
                        :, 0:8, tau * 128:(tau + 1) * 128],
                    in_=psA[:, :].rearrange("p (c t) -> p c t", c=8)), inc=None)
                P.drain("act")
                P.op("act", lambda e, tau=tau: e.copy(
                    out=ln3fm.rearrange("p (c t) -> p c t", c=16)[
                        :, 8:16, tau * 128:(tau + 1) * 128],
                    in_=psB[:, :].rearrange("p (c t) -> p c t", c=8)), inc=s_act)
                ln3_ev = s_act.n
            P.wait("pe", s_act, ln3_ev)
            if w > 0:
                P.wait("act", s_dve, h_done)
            for c in range(16):
                pst = (g5, g6)[c % 2]
                P.wait("pe", s_act, fw_ev_bank[c % 2])
                P.op("pe", lambda e, c=c, pst=pst: e.matmul(
                    pst[:, :], hsu_sb[:, c * 128:(c + 1) * 128],
                    ln3fm[:, c * 512:(c + 1) * 512], start=True, stop=True),
                    inc=s_pe)
                fmm = s_pe.n
                P.wait("act", s_pe, fmm)
                P.op("act", lambda e, c=c, pst=pst: e.copy(
                    out=fA[c][:, :], in_=pst.ap()), inc=s_act)
                fw_ev_bank[c % 2] = s_act.n
            P.wait("dve", s_act, s_act.n)
            cur, nxt = fA, fB
            for st in range(4):
                step = 1 << st
                for c in range(16):
                    pa, pb = c & ~step, c | step
                    if st == 3:
                        dst = h_sb[:, c * OWN + w * 512:c * OWN + w * 512 + 512]
                    else:
                        dst = nxt[c][:, :]
                    opk = ALU.subtract if (c & step) else ALU.add
                    last_bt = (st == 3 and c == 15)
                    P.op("dve", lambda e, dst=dst, pa=pa, pb=pb, cur=cur, opk=opk: e.tensor_tensor(
                        out=dst, in0=cur[pa][:, :], in1=cur[pb][:, :],
                        op=opk), inc=(s_dve if last_bt else None))
                P.drain("dve")
                cur, nxt = nxt, cur
            h_done = s_dve.n

        barrier()
        wnat_up = [vbf(AR2, s * 1024, 2048) for s in range(2)]
        wnat_gt = [vbf(AR2, 2048 + s * 1024, 2048) for s in range(2)]
        uplhsT = [vbf(AR2, 4096 + s * 1024, 2048) for s in range(2)]
        gtlhsT = [vbf(AR2, 6144 + s * 1024, 2048) for s in range(2)]
        sgate = [v32(AR2, 8192, 512), v32(AR2, 11520, 512)]
        upsb = [v32(AR2, 8704, 512), v32(AR2, 12032, 512)]
        midb = [vbf(AR2, 9216, 512), vbf(AR2, 12544, 512)]
        y2st = [vbf(AR2, 9472 + s * 1024, 2048) for s in range(2)]
        swu = [s_wu0, s_wu1]
        sy2w = [s_y2w0, s_y2w1]
        wn_free = [None, None]
        lh_free = [None, None]
        y2_free = [None, None]
        c2_tp_bank = [0, 0]
        ug_done = [0, 0]
        silu_d = [0, 0]
        mid_done = [0, 0]
        ymm_done = [0, 0]
        y2e = [0]
        pend = [None]

        def emit_ymm(w, s2, ysl):
            P.wait("pe", s_dve, mid_done[s2])
            P.wait("pe", s_act, y2e[0])
            P.op("pe", lambda e, s2=s2: e.matmul(
                g1[:, 0:512], h128b_sb[:, :], midb[s2][:, :],
                start=True, stop=True), inc=s_pe)
            ymm_done[s2] = s_pe.n
            P.wait("act", s_pe, ymm_done[s2])
            P.op("act", lambda e, w=w, ysl=ysl: e.copy(
                out=y2st[ysl][:, w * 512:(w + 1) * 512],
                in_=g1.ap()[:, 0:512]), inc=s_act)
            y2e[0] = s_act.n
        for c2 in range(64):
            sl = c2 % 2
            if wn_free[sl] is not None:
                P.wait("pool", *wn_free[sl])
            P.op("pool", lambda e, c2=c2, sl=sl: e.dma_start(
                out=wnat_up[sl][:, :], in_=w_up[c2 * 128:(c2 + 1) * 128, :]),
                inc=swu[sl], inck=16)
            P.op("pool", lambda e, c2=c2, sl=sl: e.dma_start(
                out=wnat_gt[sl][:, :], in_=w_gate[c2 * 128:(c2 + 1) * 128, :]),
                inc=swu[sl], inck=16)
            wu_ld = swu[sl].n
            P.wait("pe", swu[sl], wu_ld)
            if lh_free[sl] is not None:
                P.wait("pe", *lh_free[sl])
            for j in range(16):
                P.wait("pe", s_act, c2_tp_bank[j % 2])
                P.op("pe", lambda e, j=j, sl=sl: e.transpose(
                    (psA, psB)[j % 2][:, (j // 2) * 128:(j // 2) * 128 + 128],
                    wnat_up[sl][:, j * 128:(j + 1) * 128], identb[:, :]),
                    inc=(s_pe if j >= 14 else None))
            tpu = s_pe.n
            P.wait("act", s_pe, tpu)
            P.op("act", lambda e, sl=sl: e.copy(
                out=uplhsT[sl].rearrange("p (c t) -> p c t", c=16)[:, 0:16:2, :],
                in_=psA[:, :].rearrange("p (c t) -> p c t", c=8)), inc=None)
            P.drain("act")
            P.op("act", lambda e, sl=sl: e.copy(
                out=uplhsT[sl].rearrange("p (c t) -> p c t", c=16)[:, 1:16:2, :],
                in_=psB[:, :].rearrange("p (c t) -> p c t", c=8)), inc=s_act)
            c2_tp_bank[0] = s_act.n
            c2_tp_bank[1] = s_act.n
            for j in range(16):
                P.wait("pe", s_act, c2_tp_bank[j % 2])
                P.op("pe", lambda e, j=j, sl=sl: e.transpose(
                    (psA, psB)[j % 2][:, (j // 2) * 128:(j // 2) * 128 + 128],
                    wnat_gt[sl][:, j * 128:(j + 1) * 128], identb[:, :]),
                    inc=(s_pe if j >= 14 else None))
            tpg = s_pe.n
            wn_free[sl] = (s_pe, tpg)
            P.wait("act", s_pe, tpg)
            P.op("act", lambda e, sl=sl: e.copy(
                out=gtlhsT[sl].rearrange("p (c t) -> p c t", c=16)[:, 0:16:2, :],
                in_=psA[:, :].rearrange("p (c t) -> p c t", c=8)), inc=None)
            P.drain("act")
            P.op("act", lambda e, sl=sl: e.copy(
                out=gtlhsT[sl].rearrange("p (c t) -> p c t", c=16)[:, 1:16:2, :],
                in_=psB[:, :].rearrange("p (c t) -> p c t", c=8)), inc=s_act)
            c2_tp_bank[0] = s_act.n
            c2_tp_bank[1] = s_act.n
            lh_ready = s_act.n
            ysl = c2 % 2
            if y2_free[ysl] is not None:
                P.wait("act", *y2_free[ysl])
            for w in range(NWC):
                s2 = w % 2
                pu, pg = ((g5.ap(), g6.ap()),
                          (g2.ap()[:, 0:512], g2.ap()[:, 512:1024]))[s2]
                P.wait("pe", s_act, lh_ready)
                if c2 == 0 and w == 0:
                    P.wait("pe", s_dve, h_done)
                P.wait("pe", s_dve, mid_done[s2])
                P.wait("pe", s_act, silu_d[s2])
                for c in range(16):
                    P.op("pe", lambda e, c=c, sl=sl, w=w, pu=pu: e.matmul(
                        pu[:, :], uplhsT[sl][:, c * 128:(c + 1) * 128],
                        h_sb[:, c * OWN + w * 512:c * OWN + (w + 1) * 512],
                        start=(c == 0), stop=(c == 15)), inc=None)
                for c in range(16):
                    P.op("pe", lambda e, c=c, sl=sl, w=w, pg=pg: e.matmul(
                        pg[:, :], gtlhsT[sl][:, c * 128:(c + 1) * 128],
                        h_sb[:, c * OWN + w * 512:c * OWN + (w + 1) * 512],
                        start=(c == 0), stop=(c == 15)),
                        inc=(s_pe if c == 15 else None))
                ug_done[s2] = s_pe.n
                if pend[0] is not None:
                    emit_ymm(*pend[0])
                P.wait("act", s_pe, ug_done[s2])
                P.op("act", lambda e, c2=c2, pg=pg, s2=s2: e.activation(
                    out=sgate[s2][:, :], in_=pg[:, :], func=AF.Silu,
                    scale=wscg_cols[:, c2:c2 + 1]), inc=s_act)
                silu_d[s2] = s_act.n
                P.wait("dve", s_pe, ug_done[s2])
                P.wait("dve", s_pe, ymm_done[s2])
                P.op("dve", lambda e, c2=c2, pu=pu, s2=s2: e.tensor_scalar(
                    out=upsb[s2][:, :], in0=pu[:, :],
                    scalar1=upsc_cols[:, c2:c2 + 1],
                    scalar2=None, op0=ALU.mult), inc=None)
                P.drain("dve")
                P.wait("dve", s_act, silu_d[s2])
                P.op("dve", lambda e, s2=s2: e.tensor_tensor(
                    out=midb[s2][:, :], in0=upsb[s2][:, :], in1=sgate[s2][:, :],
                    op=ALU.mult), inc=s_dve)
                mid_done[s2] = s_dve.n
                pend[0] = (w, s2, ysl)
            emit_ymm(*pend[0])
            pend[0] = None
            P.wait("sp", s_act, y2e[0])
            P.op("sp", lambda e, c2=c2, ysl=ysl: e.dma_start(
                out=y2d[c2 * 128:(c2 + 1) * 128, 0:OWN],
                in_=y2st[ysl][:, 0:OWN]), inc=sy2w[ysl], inck=16)
            y2_free[ysl] = (sy2w[ysl], sy2w[ysl].n)

        barrier()
        NW3 = OWN // 256
        z_sb = [vbf(AR1, s * 8192, 64 * 256) for s in range(2)]
        ffo = vbf(AR1, 16384, 16 * 256)
        atm3 = v32(AR1, 18432, 2048)
        x2b = v32(AR1, 20480, 2048)
        wdl = [vbf(AR1, 22528 + s * 1024, 2048) for s in range(2)]
        y2w = vbf(AR2, 0, 64 * 256)
        pA = [vbf(AR2, 8192 + c * 128, 256) for c in range(64)]
        pB = [vbf(AR2, c * 128, 256) for c in range(64)]
        swdl = [s_wl0, s_wl1]
        DVE_CH = 64

        def ps8b(oc):
            if oc < 2:
                return g1, oc * 512
            if oc < 4:
                return g2, (oc - 2) * 512
            return (g3, g4, g5, g6)[oc - 4], 0

        y2w_free = None
        wd_free = [None, None]
        z_free = [0, 0]
        down_ev = 0
        x2b_free = None
        atm3_free = 0
        out_dma = 0
        dn_done = 0
        bt_done = [(0, 0), (0, 0)]

        def emit_butterfly(w):
            nonlocal y2w_free
            zsl = w % 2
            col0 = w * 256
            if y2w_free is not None:
                P.wait("act", *y2w_free)
            P.op("act", lambda e, col0=col0: e.dma_start(
                out=y2w.rearrange("p (c t) -> p c t", c=64),
                in_=y2d[:, col0:col0 + 256].rearrange(
                    "(c p) t -> p c t", p=128)), inc=s_y0, inck=16)
            yld = s_y0.n
            P.wait("dve", s_y0, yld)
            P.wait("pool", s_y0, yld)
            P.wait("dve", s_pe, z_free[zsl])
            P.wait("pool", s_pe, z_free[zsl])
            for st in range(6):
                step = 1 << st
                P.wait("dve", s_pl, s_pl.n)
                P.wait("pool", s_dve, s_dve.n)
                for c in range(64):
                    eng = "dve" if c < DVE_CH else "pool"
                    pa_, pb_ = c & ~step, c | step
                    if st == 0:
                        i0 = y2w[:, pa_ * 256:(pa_ + 1) * 256]
                        i1 = y2w[:, pb_ * 256:(pb_ + 1) * 256]
                    else:
                        src_ = pA if st % 2 == 1 else pB
                        i0 = src_[pa_][:, :]
                        i1 = src_[pb_][:, :]
                    if st == 5:
                        dst = z_sb[zsl][:, c * 256:(c + 1) * 256]
                    else:
                        dst_ = pA if st % 2 == 0 else pB
                        dst = dst_[c][:, :]
                    opk = ALU.subtract if (c & step) else ALU.add
                    lastd = (c == DVE_CH - 1)
                    lastp = (c == 63)
                    P.op(eng, lambda e, dst=dst, i0=i0, i1=i1, opk=opk: e.tensor_tensor(
                        out=dst, in0=i0, in1=i1, op=opk),
                        inc=(s_dve if (eng == "dve" and lastd) else
                             (s_pl if (eng == "pool" and lastp) else None)))
                P.drain("dve")
                P.drain("pool")
            y2w_free = (s_dve, s_dve.n)
            bt_done[zsl] = (s_dve.n, s_pl.n)
            bt_done[zsl] = (s_dve.n, s_pl.n)

        emit_butterfly(0)
        for w in range(NW3):
            zsl = w % 2
            P.wait("pe", s_dve, bt_done[zsl][0])
            P.wait("pe", s_pl, bt_done[zsl][1])
            for op_ in range(2):
                P.wait("pe", s_act, down_ev)
                if op_ == 0:
                    P.wait("pe", s_act, atm3_free)
                for f2g in range(32):
                    sl = f2g % 2
                    if wd_free[sl] is not None:
                        P.wait("sp", *wd_free[sl])
                    P.op("sp", lambda e, f2g=f2g, sl=sl, op_=op_: e.dma_start(
                        out=wdl[sl][:, 0:2048].rearrange("p (f n) -> p f n", f=2),
                        in_=wdTd[f2g * 256:(f2g + 1) * 256,
                                 op_ * 1024:(op_ + 1) * 1024].rearrange(
                            "(f p) n -> p f n", p=128)),
                        inc=swdl[sl], inck=16)
                    wld = swdl[sl].n
                    P.wait("pe", swdl[sl], wld)
                    for fi in range(2):
                        f2 = f2g * 2 + fi
                        for oc in range(8):
                            pst, col = ps8b(oc)
                            P.op("pe", lambda e, f2=f2, fi=fi, oc=oc, pst=pst,
                                 col=col, sl=sl, zsl=zsl: e.matmul(
                                pst[:, col:col + 256],
                                wdl[sl][:, fi * 1024 + oc * 128:fi * 1024 + (oc + 1) * 128],
                                z_sb[zsl][:, f2 * 256:(f2 + 1) * 256],
                                start=(f2 == 0), stop=(f2 == 63)),
                                inc=(s_pe if (oc == 7 and fi == 1) else None))
                    wd_free[sl] = (s_pe, s_pe.n)
                dn_done = s_pe.n
                P.wait("act", s_pe, dn_done)
                for oc in range(8):
                    ocg = op_ * 8 + oc
                    pst, col = ps8b(oc)
                    if oc > 0:
                        P.drain("act")
                    P.op("act", lambda e, ocg=ocg, pst=pst, col=col: e.activation(
                        out=ffo[:, ocg * 256:(ocg + 1) * 256],
                        in_=pst[:, col:col + 256],
                        func=AF.Copy, scale=wscdp_sb[:, ocg:ocg + 1]),
                        inc=(s_act if oc == 7 else None))
                down_ev = s_act.n
            z_free[zsl] = dn_done
            if w + 1 < NW3:
                emit_butterfly(w + 1)
            for tau in range(2):
                i_own = w * 2 + tau
                if x2b_free is not None:
                    P.wait("sp", *x2b_free)
                P.op("sp", lambda e, i_own=i_own: e.dma_start(
                    out=x2b[:, :], in_=x2d[i_own * 128:(i_own + 1) * 128, :]),
                    inc=s_x2r1, inck=16)
                x2ld = s_x2r1.n
                P.wait("pe", s_act, down_ev)
                P.wait("pe", s_act, atm3_free)
                for j in range(16):
                    P.op("pe", lambda e, j=j, tau=tau: e.transpose(
                        (psA if j < 8 else psB)[:, (j % 8) * 128:(j % 8) * 128 + 128],
                        ffo[:, j * 256 + tau * 128:j * 256 + (tau + 1) * 128],
                        identb[:, :]), inc=(s_pe if j == 15 else None))
                tp4 = s_pe.n
                P.wait("act", s_pe, tp4)
                P.wait("act", s_out, out_dma)
                P.op("act", lambda e: e.copy(out=atm3[:, 0:1024], in_=psA[:, :]),
                     inc=None)
                P.drain("act")
                P.op("act", lambda e: e.copy(out=atm3[:, 1024:2048], in_=psB[:, :]),
                     inc=s_act)
                atm3_ev = s_act.n
                atm3_free = s_act.n
                P.wait("dve", s_act, atm3_ev)
                for j in range(4):
                    P.op("dve", lambda e, j=j: e.bn_stats(
                        out=bnst[:, j * 6:(j + 1) * 6],
                        in_=atm3[:, j * 512:(j + 1) * 512]), inc=None)
                P.drain("dve")
                P.op("dve", lambda e: e.bn_aggr(out=mvt[:, 0:2], in_=bnst[:, :]),
                     inc=s_dve)
                bn4 = s_dve.n
                P.wait("act", s_dve, bn4)
                P.op("act", lambda e: e.activation(out=rstdt[:, 0:1], in_=mvt[:, 1:2],
                                                   func=AF.Ln, bias=fc[:, 0:1]), inc=None)
                P.drain("act")
                P.op("act", lambda e: e.activation(out=rstdt[:, 1:2], in_=rstdt[:, 0:1],
                                                   func=AF.Exp, scale=-0.5), inc=s_act)
                rstd4 = s_act.n
                P.wait("dve", s_act, rstd4)
                P.wait("dve", s_x2r1, x2ld)
                P.op("dve", lambda e: e.tensor_scalar(
                    out=atm3[:, :], in0=atm3[:, :], scalar1=mvt[:, 0:1],
                    scalar2=rstdt[:, 1:2], op0=ALU.subtract, op1=ALU.mult),
                    inc=None)
                P.drain("dve")
                P.op("dve", lambda e: e.tensor_tensor(
                    out=atm3[:, :], in0=atm3[:, :], in1=x2b[:, :], op=ALU.add),
                    inc=s_dve)
                x3_done = s_dve.n
                x2b_free = (s_dve, x3_done)
                P.wait("sp", s_dve, x3_done)
                P.op("sp", lambda e, i_own=i_own: e.dma_start(
                    out=out[i_own * 128:(i_own + 1) * 128, :], in_=atm3[:, :]),
                    inc=s_out, inck=16)
                out_dma = s_out.n

        P.wait("sp", s_out, s_out.n)
        P.wait("pool", s_po, s_po.n)

        import os
        _tr = float(os.environ.get("K_TRUNC", "1.0"))
        if _tr < 1.0:
            for k in P.q:
                P.q[k] = P.q[k][:int(len(P.q[k]) * _tr)]
        for k in P.q:
            _trk = float(os.environ.get("K_TRUNC_" + k.upper(), "1.0"))
            if _trk < 1.0:
                P.q[k] = P.q[k][:int(len(P.q[k]) * _trk)]
        with nc.Block() as block:
            @block.tensor
            def _(e):
                for f in P.q["pe"]:
                    f(e)

            @block.scalar
            def _(e):
                for f in P.q["act"]:
                    f(e)

            @block.vector
            def _(e):
                for f in P.q["dve"]:
                    f(e)

            @block.gpsimd
            def _(e):
                for f in P.q["pool"]:
                    f(e)

            @block.sync
            def _(e):
                for f in P.q["sp"]:
                    f(e)

    return nc


def _host_inputs(inputs, T=8192):
    OWN = T // 4
    x = np.ascontiguousarray(np.asarray(inputs["L0_attn_x"], dtype=np.float32)
                             .reshape(2 * T, D))
    q_a_w = np.ascontiguousarray(inputs["L0_attn_q_a_w"], dtype=np.float32)
    q_b_w = np.asarray(inputs["L0_attn_q_b_w"], dtype=np.float32)
    kv_a_w = np.ascontiguousarray(inputs["L0_attn_kv_a_w"], dtype=np.float32)
    kv_b_w = np.asarray(inputs["L0_attn_kv_b_w"], dtype=np.float32)
    o_w = np.ascontiguousarray(inputs["L0_attn_o_w"], dtype=np.float32)
    w_up = np.ascontiguousarray(inputs["L0_ffn_W_up"], dtype=np.float32)
    w_gate = np.ascontiguousarray(inputs["L0_ffn_W_gate"], dtype=np.float32)
    w_down = np.ascontiguousarray(inputs["L0_ffn_W_down"], dtype=np.float32)
    wq = np.asarray(inputs["L0_attn_q_a_ln_w"], dtype=np.float32)
    wkv = np.asarray(inputs["L0_attn_kv_a_ln_w"], dtype=np.float32)
    su_ug = np.asarray(inputs["L0_ffn_SU_ug"], dtype=np.float32)
    su_dp = np.asarray(inputs["L0_ffn_SU_dp"], dtype=np.float32)
    wsc_ug = np.asarray(inputs["L0_ffn_Wscale_ug"], dtype=np.float32)
    wsc_dp = np.asarray(inputs["L0_ffn_Wscale_dp"], dtype=np.float32)

    inv = (1.0 / THETA ** (np.arange(0, ROPE, 2) / ROPE)).astype(np.float64)
    tpos = np.arange(T, dtype=np.float64)
    fr = np.outer(tpos, inv)
    cos_t = np.ascontiguousarray(np.cos(fr).astype(np.float32))
    sin_t = np.ascontiguousarray(np.sin(fr).astype(np.float32))

    def hmat(m):
        Hm = np.array([[1.0]], dtype=np.float32)
        while Hm.shape[0] < m:
            Hm = np.block([[Hm, Hm], [Hm, -Hm]])
        return Hm
    H128m = np.ascontiguousarray(hmat(128).astype(np.float32))
    ones_b = np.ones((128, 128), dtype=ml_dtypes.bfloat16)
    ident_b = np.eye(128, dtype=ml_dtypes.bfloat16)
    ident_f = np.eye(128, dtype=np.float32)

    def cols(v, npart=128):
        return np.ascontiguousarray(v.reshape(-1, npart).T.astype(np.float32))

    NCH = OWN // 512
    LIMS = [16 * (s + 1) for s in range(NCH)]

    in_maps = []
    for r in range(NC_):
        b = r // 4
        j = r % 4
        chunks = [(s * 4 + j) if s % 2 == 0 else ((s + 1) * 4 - 1 - j)
                  for s in range(NCH)]
        own_rows = np.concatenate([np.arange(c * 512, (c + 1) * 512)
                                   for c in chunks])
        xb = x[b * T:(b + 1) * T]
        mk = np.zeros((128, NCH * 16 * 512), dtype=ml_dtypes.bfloat16)
        for s, c in enumerate(chunks):
            for ti in range(16):
                kt = LIMS[s] - 16 + ti
                krow = kt * 128 + np.arange(128)[:, None]
                qcol = c * 512 + np.arange(512)[None, :]
                mk[:, (s * 16 + ti) * 512:(s * 16 + ti + 1) * 512] = (
                    krow <= qcol).astype(ml_dtypes.bfloat16)
        cos16 = np.tile(cos_t[own_rows], (1, 16))
        sin16 = np.tile(sin_t[own_rows], (1, 16))
        mp = {
            "x_own": np.ascontiguousarray(xb[own_rows]),
            "x_batch": np.ascontiguousarray(xb),
            "q_a_w": q_a_w, "q_b_w": np.ascontiguousarray(q_b_w),
            "kv_a_w": kv_a_w, "kv_b_w": np.ascontiguousarray(kv_b_w),
            "o_w": o_w,
            "w_up": w_up, "w_gate": w_gate, "w_down": w_down,
            "wq_cols": cols(wq), "wkv_cols": cols(wkv),
            "su_ug_cols": cols(su_ug), "su_dp_cols": cols(su_dp),
            "wsc_ug_cols": cols(wsc_ug), "wsc_dp_cols": cols(wsc_dp),
            "cos_all": cos_t, "sin_all": sin_t,
            "cos16_own": np.ascontiguousarray(cos16),
            "sin16_own": np.ascontiguousarray(sin16),
            "h128": H128m, "ones_bf": ones_b, "ident_bf": ident_b,
            "ident_f32": ident_f, "masks_in": np.ascontiguousarray(mk),
            "fconst": np.ascontiguousarray(np.broadcast_to(np.array(
                [EPS_LN, EPS_RMS, math.log(ATT_SCALE), 0.0], dtype=np.float32),
                (128, 4))),
        }
        in_maps.append(mp)
    return in_maps


def _assemble(shards, T):
    OWN = T // 4
    NCH = OWN // 512
    full = np.empty((2, T, D), dtype=np.float32)
    for r in range(NC_):
        b = r // 4
        j = r % 4
        chunks = [(s * 4 + j) if s % 2 == 0 else ((s + 1) * 4 - 1 - j)
                  for s in range(NCH)]
        for s, c in enumerate(chunks):
            full[b, c * 512:(c + 1) * 512] = shards[r][s * 512:(s + 1) * 512]
    return full


def kernel(**inputs):
    T = 8192
    in_maps = _host_inputs(inputs, T=T)
    nc = build_program(T=T, debug=False)
    res = run_bass_kernel_spmd(nc, in_maps, core_ids=list(range(NC_)))
    shards = [res.results[r]["out"] for r in range(NC_)]
    return _assemble(shards, T)


# version 12 (final)
# speedup vs baseline: 1.2315x; 1.0044x over previous
import math
import numpy as np
import ml_dtypes
import concourse.bass as bass
import concourse.mybir as mybir
from concourse.bass_utils import run_bass_kernel_spmd

F32 = mybir.dt.float32
BF16 = mybir.dt.bfloat16
AF = mybir.ActivationFunctionType
ALU = mybir.AluOpType
AX = mybir.AxisListType

D = 2048
H = 16
NOPE = 128
ROPE = 64
QK = 192
VD = 128
KVL = 512
QL = 1536
INTER = 8192
EPS_LN = 1e-5
EPS_RMS = 1e-6
ATT_SCALE = QK ** -0.5
THETA = 5e5

NC_ = 8
GROUPS = [[0, 1, 2, 3], [4, 5, 6, 7]]
HPC = 4

BF = 2
FP = 4


class Sem:
    def __init__(self, h):
        self.h = h
        self.n = 0

    def bump(self, k=1):
        self.n += k
        return self.n


class Prog:

    ENGS = ("pe", "act", "dve", "pool", "sp")

    def __init__(self):
        self.q = {e: [] for e in self.ENGS}

    def wait(self, eng, sem, val):
        if val > 0:
            self.q[eng].append(lambda e, h=sem.h, v=val: e.wait_ge(h, v))

    def op(self, eng, build, inc=None, inck=1):
        if inc is None:
            self.q[eng].append(build)
            return None
        v = inc.bump(inck)
        def f(e, b=build, h=inc.h, k=inck):
            inst = b(e)
            inst.then_inc(h, k)
        self.q[eng].append(f)
        return v

    def drain(self, eng):
        self.q[eng].append(lambda e: e.drain())


def ceil_div(a, b):
    return (a + b - 1) // b


def build_program(T=8192, debug=False, taps=()):
    OWN = T // 4
    NT = OWN // 128
    NW = T // 512
    NKT = T // 128
    NB = OWN // 512
    QBLK = T // 2048
    assert OWN % 512 == 0 and T % 2048 == 0

    nc = bass.Bass("TRN2", target_bir_lowering=False, debug=debug)

    def din(name, shape, dt):
        return nc.dram_tensor(name, shape, dt, kind="ExternalInput")

    x_own = din("x_own", [OWN, D], F32)
    x_batch = din("x_batch", [T, D], F32)
    q_a_w = din("q_a_w", [D, QL], F32)
    q_b_w = din("q_b_w", [QL, H * QK], F32)
    kv_a_w = din("kv_a_w", [D, KVL + ROPE], F32)
    kv_b_w = din("kv_b_w", [KVL, H * 256], F32)
    o_w = din("o_w", [D, D], F32)
    w_up = din("w_up", [INTER, D], F32)
    w_gate = din("w_gate", [INTER, D], F32)
    w_down = din("w_down", [D, INTER], F32)
    wq_cols = din("wq_cols", [128, QL // 128], F32)
    wkv_cols = din("wkv_cols", [128, KVL // 128], F32)
    su_ug_cols = din("su_ug_cols", [128, 16], F32)
    su_dp_cols = din("su_dp_cols", [128, 64], F32)
    wsc_ug_cols = din("wsc_ug_cols", [128, 128], F32)
    wsc_dp_cols = din("wsc_dp_cols", [128, 16], F32)
    cos_all = din("cos_all", [T, 32], F32)
    sin_all = din("sin_all", [T, 32], F32)
    cos16_own = din("cos16_own", [OWN, 512], F32)
    sin16_own = din("sin16_own", [OWN, 512], F32)
    h128 = din("h128", [128, 128], F32)
    ones_bf = din("ones_bf", [128, 128], BF16)
    ident_bf = din("ident_bf", [128, 128], BF16)
    ident_f32 = din("ident_f32", [128, 128], F32)
    masks_in = din("masks_in", [128, (OWN // 512) * 16 * 512], BF16)
    fconst = din("fconst", [128, 4], F32)

    out = nc.dram_tensor("out", [OWN, D], F32, kind="ExternalOutput")

    qspill = nc.dram_tensor("qspill", [OWN, H * QK], BF16)
    ospill = nc.dram_tensor("ospill", [H * VD, OWN], BF16)
    x2d = nc.dram_tensor("x2d", [OWN, D], F32)
    y2d = nc.dram_tensor("y2d", [INTER, OWN], BF16)
    wdTd = nc.dram_tensor("wdTd", [INTER, D], BF16)

    tap_outs = {}
    for tname in taps:
        src = {"qspill": qspill, "ospill": ospill, "x2d": x2d,
               "y2d": y2d, "wdTd": wdTd}[tname]
        tap_outs[tname] = (nc.dram_tensor("dbg_" + tname, list(src.shape),
                                          src.dtype, kind="ExternalOutput"), src)

    P = Prog()

    import contextlib
    ctx = contextlib.ExitStack()
    with ctx:
        AR1 = ctx.enter_context(nc.sbuf_tensor("AR1", [128, 24576], F32))
        AR2 = ctx.enter_context(nc.sbuf_tensor("AR2", [128, 20480], F32))

        def v32(ar, off, n):
            return ar[:, off:off + n]

        def vbf(ar, off, n):
            return ar.ap().bitcast(BF16)[:, 2 * off:2 * off + n]

        onesb = ctx.enter_context(nc.sbuf_tensor("onesb", [128, 128], BF16))
        identb = ctx.enter_context(nc.sbuf_tensor("identb", [128, 128], BF16))
        identf = ctx.enter_context(nc.sbuf_tensor("identf", [128, 128], F32))
        h128f = ctx.enter_context(nc.sbuf_tensor("h128f", [128, 128], F32))
        hsu_sb = ctx.enter_context(nc.sbuf_tensor("hsu_sb", [128, 16 * 128], BF16))
        h128b_sb = ctx.enter_context(nc.sbuf_tensor("h128b_sb", [128, 128], BF16))
        colscr = ctx.enter_context(nc.sbuf_tensor("colscr", [128, 256], F32))
        rowscr = ctx.enter_context(nc.sbuf_tensor("rowscr", [16, 512], F32))
        upsc_cols = ctx.enter_context(nc.sbuf_tensor("upsc_cols", [128, 64], F32))
        wscg_cols = ctx.enter_context(nc.sbuf_tensor("wscg_cols", [128, 64], F32))
        wscdp_sb = ctx.enter_context(nc.sbuf_tensor("wscdp_sb", [128, 16], F32))
        wqc_sb = ctx.enter_context(nc.sbuf_tensor("wqc_sb", [128, 12], F32))
        wkvc_sb = ctx.enter_context(nc.sbuf_tensor("wkvc_sb", [128, 4], F32))
        suug_sb = ctx.enter_context(nc.sbuf_tensor("suug_sb", [128, 16], F32))
        bnst = ctx.enter_context(nc.sbuf_tensor("bnst", [128, 24], F32))
        mvt = ctx.enter_context(nc.sbuf_tensor("mvt", [128, 2], F32))
        rstdt = ctx.enter_context(nc.sbuf_tensor("rstdt", [128, 4], F32))
        sstat = ctx.enter_context(nc.sbuf_tensor("sstat", [128, 16], F32))
        fc = ctx.enter_context(nc.sbuf_tensor("fc", [128, 4], F32))
        nb_bf = ctx.enter_context(nc.sbuf_tensor("nb_bf", [1, 16], BF16))
        onesf = ctx.enter_context(nc.sbuf_tensor("onesf", [128, 1], F32))
        recb = ctx.enter_context(nc.sbuf_tensor("recb", [1, 512], BF16))
        recb2 = ctx.enter_context(nc.sbuf_tensor("recb2", [1, 512], BF16))
        bcastf2 = ctx.enter_context(nc.sbuf_tensor("bcastf2", [128, 512], F32))

        g1 = ctx.enter_context(nc.psum_tensor("g1", [128, 1024], F32))
        g2 = ctx.enter_context(nc.psum_tensor("g2", [128, 1024], F32))
        g3 = ctx.enter_context(nc.psum_tensor("g3", [128, 512], F32))
        g4 = ctx.enter_context(nc.psum_tensor("g4", [128, 512], F32))
        g5 = ctx.enter_context(nc.psum_tensor("g5", [128, 512], F32))
        g6 = ctx.enter_context(nc.psum_tensor("g6", [128, 512], F32))
        psA = g3.ap().bitcast(BF16)
        psB = g4.ap().bitcast(BF16)
        ps1 = g5.ap()
        ps2 = g2.ap()[:, 0:512]
        ps3 = g6.ap()
        ps4 = g1.ap()[:, 0:512]

        def mksem(name):
            return Sem(ctx.enter_context(nc.semaphore(name)))
        s_c = mksem("s_c")
        s_x0, s_x1 = mksem("s_x0"), mksem("s_x1")
        s_wa = mksem("s_wa")
        s_mk = mksem("s_mk")
        s_kb = mksem("s_kb")
        s_pe, s_act, s_dve = mksem("s_pe"), mksem("s_act"), mksem("s_dve")
        s_out = mksem("s_out")
        s_po = mksem("s_po")
        s_bar = mksem("s_bar")
        sx = [s_x0, s_x1]
        s_qs0, s_qs1 = mksem("s_qs0"), mksem("s_qs1")
        s_cs0, s_cs1 = mksem("s_cs0"), mksem("s_cs1")
        s_kv = mksem("s_kv")
        s_qr0, s_qr1 = mksem("s_qr0"), mksem("s_qr1")
        s_c20 = mksem("s_c20")
        s_c21 = mksem("s_c21")
        s_oh0, s_oh1 = mksem("s_oh0"), mksem("s_oh1")
        s_ow = mksem("s_ow")
        s_wu0, s_wu1 = mksem("s_wu0"), mksem("s_wu1")
        s_y0, s_y1 = mksem("s_y0"), mksem("s_y1")
        s_wd0, s_wd1 = mksem("s_wd0"), mksem("s_wd1")
        s_x2r0, s_x2r1 = mksem("s_x2r0"), mksem("s_x2r1")
        s_wt0, s_wt1 = mksem("s_wt0"), mksem("s_wt1")
        s_y2w0, s_y2w1 = mksem("s_y2w0"), mksem("s_y2w1")
        s_wl0, s_wl1 = mksem("s_wl0"), mksem("s_wl1")
        s_pl = mksem("s_pl")

        ALL_SEMS = [s_c, s_x0, s_x1, s_wa, s_mk, s_kb, s_out, s_po,
                    s_qs0, s_qs1, s_cs0, s_cs1, s_kv, s_qr0, s_qr1, s_c20, s_c21,
                    s_oh0, s_oh1, s_ow, s_wu0, s_wu1, s_y0, s_y1,
                    s_wd0, s_wd1, s_x2r0, s_x2r1, s_wt0, s_wt1,
                    s_y2w0, s_y2w1, s_wl0, s_wl1]

        def barrier():
            for e in P.ENGS:
                P.op(e, lambda en: en.nop(), inc=s_bar)
            tgt = s_bar.n
            for e in P.ENGS:
                P.wait(e, s_bar, tgt)
                for s in ALL_SEMS + [s_pe, s_act, s_dve, s_pl]:
                    P.wait(e, s, s.n)

        LNS = math.log(ATT_SCALE)
        NCH = OWN // 512
        LIMS = [16 * (s + 1) for s in range(NCH)]

        P.op("sp", lambda e: e.dma_start(out=onesb[:, :], in_=ones_bf.ap()), inc=s_c, inck=16)
        P.op("sp", lambda e: e.dma_start(out=identb[:, :], in_=ident_bf.ap()), inc=s_c, inck=16)
        P.op("sp", lambda e: e.dma_start(out=identf[:, :], in_=ident_f32.ap()), inc=s_c, inck=16)
        P.op("sp", lambda e: e.dma_start(out=h128f[:, :], in_=h128.ap()), inc=s_c, inck=16)
        P.op("sp", lambda e: e.dma_start(out=wqc_sb[:, :], in_=wq_cols.ap()), inc=s_c, inck=16)
        P.op("sp", lambda e: e.dma_start(out=wkvc_sb[:, :], in_=wkv_cols.ap()), inc=s_c, inck=16)
        P.op("sp", lambda e: e.dma_start(out=suug_sb[:, :], in_=su_ug_cols.ap()), inc=s_c, inck=16)
        P.op("sp", lambda e: e.dma_start(out=upsc_cols[:, :], in_=wsc_ug_cols.ap()[:, 0:64]), inc=s_c, inck=16)
        P.op("sp", lambda e: e.dma_start(out=wscg_cols[:, :], in_=wsc_ug_cols.ap()[:, 64:128]), inc=s_c, inck=16)
        P.op("sp", lambda e: e.dma_start(out=wscdp_sb[:, :], in_=wsc_dp_cols.ap()), inc=s_c, inck=16)
        P.op("sp", lambda e: e.dma_start(out=colscr[:, 0:64], in_=su_dp_cols.ap()), inc=s_c, inck=16)
        P.op("sp", lambda e: e.dma_start(out=fc[:, :], in_=fconst.ap()), inc=s_c, inck=16)
        SC = s_c.n

        P.wait("dve", s_c, SC)
        P.op("dve", lambda e: e.memset(onesf[:, :], 1.0), inc=None)
        P.drain("dve")
        P.op("dve", lambda e: e.tensor_tensor(out=upsc_cols[:, :],
                                              in0=upsc_cols[:, :],
                                              in1=colscr[:, 0:64], op=ALU.mult),
             inc=s_dve)
        for c in range(16):
            P.op("dve", lambda e, c=c: e.tensor_scalar(
                out=hsu_sb[:, c * 128:(c + 1) * 128], in0=h128f[:, :],
                scalar1=suug_sb[:, c:c + 1], scalar2=1.0 / math.sqrt(2048.0),
                op0=ALU.mult, op1=ALU.mult), inc=s_dve)
        P.op("dve", lambda e: e.tensor_scalar(
            out=h128b_sb[:, :], in0=h128f[:, :], scalar1=1.0 / math.sqrt(8192.0),
            scalar2=None, op0=ALU.mult), inc=s_dve)
        consts_done = s_dve.n
        P.wait("pe", s_c, SC)
        P.wait("act", s_c, SC)

        qaw_bf = vbf(AR1, 0, 16 * 1536)
        qan_own = vbf(AR1, 12288, 12 * OWN)
        qatmp = v32(AR2, 4096, 1536)
        ln1tm = vbf(AR2, 5632, 2048)
        ln1fm = vbf(AR2, 6656, 2048)
        sqscr = v32(AR2, 7680, 512)

        P.op("pool", lambda e: e.dma_start(
            out=AR1.ap().bitcast(BF16)[:, 0:16 * 1536].rearrange(
                "p (c n) -> p c n", c=16),
            in_=q_a_w.ap().rearrange("(c p) n -> p c n", p=128)),
            inc=s_wa, inck=16)
        AW = s_wa.n

        xt_free = [None, None]
        fm_evac = 0
        qan_ev = 0
        for i in range(NT):
            sl = i % 2
            if xt_free[sl] is not None:
                P.wait("sp", *xt_free[sl])
            P.op("sp", lambda e, i=i, sl=sl: e.dma_start(
                out=AR2[:, sl * 2048:sl * 2048 + 2048],
                in_=x_own[i * 128:(i + 1) * 128, :]), inc=sx[sl], inck=16)
            x_ld = sx[sl].n
            P.wait("dve", sx[sl], x_ld)
            if i == 0:
                P.wait("dve", s_dve, consts_done)
            for j in range(4):
                P.op("dve", lambda e, sl=sl, j=j: e.bn_stats(
                    out=bnst[:, j * 6:(j + 1) * 6],
                    in_=AR2[:, sl * 2048 + j * 512:sl * 2048 + (j + 1) * 512]),
                    inc=None)
            P.drain("dve")
            P.op("dve", lambda e: e.bn_aggr(out=mvt[:, 0:2], in_=bnst[:, :]),
                 inc=s_dve)
            bn_done = s_dve.n
            P.wait("act", s_dve, bn_done)
            P.op("act", lambda e: e.activation(out=rstdt[:, 0:1], in_=mvt[:, 1:2],
                                               func=AF.Ln, bias=fc[:, 0:1]), inc=None)
            P.drain("act")
            P.op("act", lambda e: e.activation(out=rstdt[:, 1:2], in_=rstdt[:, 0:1],
                                               func=AF.Exp, scale=-0.5), inc=s_act)
            rstd_done = s_act.n
            P.wait("dve", s_act, rstd_done)
            P.op("dve", lambda e, sl=sl: e.tensor_scalar(
                out=ln1tm, in0=AR2[:, sl * 2048:sl * 2048 + 2048],
                scalar1=mvt[:, 0:1], scalar2=rstdt[:, 1:2],
                op0=ALU.subtract, op1=ALU.mult), inc=s_dve)
            ln1_done = s_dve.n
            xt_free[sl] = (s_dve, ln1_done)
            P.wait("pe", s_dve, ln1_done)
            if i > 0:
                P.wait("pe", s_act, fm_evac)
            for j in range(16):
                P.op("pe", lambda e, j=j: e.transpose(
                    (psA if j < 8 else psB)[:, (j % 8) * 128:(j % 8) * 128 + 128],
                    ln1tm[:, j * 128:(j + 1) * 128], identb[:, :]),
                    inc=(s_pe if j == 15 else None))
            tp_done = s_pe.n
            P.wait("act", s_pe, tp_done)
            P.op("act", lambda e: e.copy(out=ln1fm[:, 0:1024], in_=psA[:, :]), inc=None)
            P.drain("act")
            P.op("act", lambda e: e.copy(out=ln1fm[:, 1024:2048], in_=psB[:, :]),
                 inc=s_act)
            fm_evac = s_act.n
            P.wait("pe", s_act, fm_evac)
            if i == 0:
                P.wait("pe", s_wa, AW)
            for c in range(16):
                for sidx, pst in enumerate((ps1, ps3, ps4)):
                    P.op("pe", lambda e, c=c, sidx=sidx, pst=pst: e.matmul(
                        pst[:, :], ln1fm[:, c * 128:(c + 1) * 128],
                        qaw_bf[:, c * 1536 + sidx * 512:c * 1536 + (sidx + 1) * 512],
                        start=(c == 0), stop=(c == 15)),
                        inc=(s_pe if (c == 15 and sidx == 2) else None))
            qa_done = s_pe.n
            P.wait("act", s_pe, qa_done)
            P.op("act", lambda e: e.copy(out=qatmp[:, 0:512], in_=ps1[:, :]), inc=None)
            P.op("act", lambda e: e.copy(out=qatmp[:, 512:1024], in_=ps3[:, :]), inc=None)
            P.drain("act")
            P.op("act", lambda e: e.copy(out=qatmp[:, 1024:1536], in_=ps4[:, :]),
                 inc=s_act)
            qatmp_done = s_act.n
            P.wait("dve", s_act, qatmp_done)
            for k3 in range(3):
                P.op("dve", lambda e, k3=k3: e.scalar_tensor_tensor(
                    out=sqscr, in0=qatmp[:, k3 * 512:(k3 + 1) * 512], scalar=1.0,
                    in1=qatmp[:, k3 * 512:(k3 + 1) * 512],
                    op0=ALU.mult, op1=ALU.mult, accum_out=sstat[:, k3:k3 + 1]),
                    inc=None)
                P.drain("dve")
            P.op("dve", lambda e: e.tensor_tensor(
                out=sstat[:, 3:4], in0=sstat[:, 0:1], in1=sstat[:, 1:2],
                op=ALU.add), inc=None)
            P.drain("dve")
            P.op("dve", lambda e: e.tensor_tensor(
                out=sstat[:, 4:5], in0=sstat[:, 3:4], in1=sstat[:, 2:3],
                op=ALU.add), inc=s_dve)
            qssq_done = s_dve.n
            P.wait("act", s_dve, qssq_done)
            P.op("act", lambda e: e.activation(out=rstdt[:, 3:4], in_=sstat[:, 4:5],
                                               func=AF.Ln, scale=1.0 / QL,
                                               bias=fc[:, 1:2]), inc=None)
            P.drain("act")
            P.op("act", lambda e: e.activation(out=rstdt[:, 2:3], in_=rstdt[:, 3:4],
                                               func=AF.Exp, scale=-0.5, bias=fc[:, 2:3]),
                 inc=s_act)
            qrstd_done = s_act.n
            P.wait("dve", s_act, qrstd_done)
            P.op("dve", lambda e: e.tensor_scalar(
                out=ln1tm[:, 0:1536], in0=qatmp, scalar1=rstdt[:, 2:3],
                scalar2=None, op0=ALU.mult), inc=s_dve)
            qan_done = s_dve.n
            P.wait("pe", s_dve, qan_done)
            P.wait("pe", s_act, qan_ev)
            for j in range(12):
                P.op("pe", lambda e, j=j: e.transpose(
                    (psA if j < 8 else psB)[:, (j % 8) * 128:(j % 8) * 128 + 128],
                    ln1tm[:, j * 128:(j + 1) * 128], identb[:, :]),
                    inc=(s_pe if j == 11 else None))
            qtp_done = s_pe.n
            P.wait("act", s_pe, qtp_done)
            P.op("act", lambda e, i=i: e.copy(
                out=qan_own.rearrange("p (c t) -> p c t", c=12)[:, 0:8, i * 128:(i + 1) * 128],
                in_=psA[:, :].rearrange("p (c t) -> p c t", c=8)), inc=None)
            P.drain("act")
            P.op("act", lambda e, i=i: e.copy(
                out=qan_own.rearrange("p (c t) -> p c t", c=12)[:, 8:12, i * 128:(i + 1) * 128],
                in_=psB[:, 0:512].rearrange("p (c t) -> p c t", c=4)),
                inc=s_act)
            qan_ev = s_act.n
            fm_evac = s_act.n

        barrier()
        qbw_bf = vbf(AR2, 0, 12 * 3072)
        qtm = [vbf(AR1, s * 1536, 3072) for s in range(2)]
        ropescr = v32(AR1, 3072, 2048)
        ssq_all = v32(AR1, 5120, 256)
        cs_sl = [(v32(AR1, 5376 + s * 1024, 512), v32(AR1, 5888 + s * 1024, 512))
                 for s in range(2)]
        scs = [s_cs0, s_cs1]
        sqs = [s_qs0, s_qs1]

        P.op("pool", lambda e: e.dma_start(
            out=AR2.ap().bitcast(BF16)[:, 0:12 * 3072].rearrange(
                "p (c n) -> p c n", c=12),
            in_=q_b_w.ap().rearrange("(c p) n -> p c n", p=128)),
            inc=s_wa, inck=16)
        AW2 = s_wa.n
        P.wait("dve", s_wa, AW2)
        for c in range(12):
            P.op("dve", lambda e, c=c: e.tensor_scalar(
                out=qbw_bf[:, c * 3072:(c + 1) * 3072],
                in0=qbw_bf[:, c * 3072:(c + 1) * 3072],
                scalar1=wqc_sb[:, c:c + 1], scalar2=None, op0=ALU.mult),
                inc=(s_dve if c == 11 else None))
        qbw_ready = s_dve.n

        psq4 = [g3.ap()[:, 0:384], g4.ap()[:, 0:384],
                g5.ap()[:, 0:384], g6.ap()[:, 0:384]]
        qtm_free = [None, None]
        cs_free = [None, None]
        qev_done = 0
        for tq in range(NT):
            sl = tq % 2
            if cs_free[sl] is not None:
                P.wait("sp", *cs_free[sl])
            P.op("sp", lambda e, tq=tq, sl=sl: e.dma_start(
                out=cs_sl[sl][0][:, :],
                in_=cos16_own[tq * 128:(tq + 1) * 128, :]), inc=scs[sl], inck=16)
            P.op("sp", lambda e, tq=tq, sl=sl: e.dma_start(
                out=cs_sl[sl][1][:, :],
                in_=sin16_own[tq * 128:(tq + 1) * 128, :]), inc=scs[sl], inck=16)
            cs_ld = scs[sl].n
            if tq == 0:
                P.wait("pe", s_dve, qbw_ready)
            P.wait("pe", s_act, qev_done)
            if qtm_free[sl] is not None:
                P.wait("pe", *qtm_free[sl])
            for half in range(2):
                if half == 1:
                    P.wait("pe", s_act, s_act.n)
                for kv4 in range(4):
                    kv8 = half * 4 + kv4
                    for l in range(12):
                        P.op("pe", lambda e, kv4=kv4, kv8=kv8, l=l, tq=tq: e.matmul(
                            psq4[kv4],
                            qan_own[:, l * OWN + tq * 128:l * OWN + (tq + 1) * 128],
                            qbw_bf[:, l * 3072 + kv8 * 384:l * 3072 + (kv8 + 1) * 384],
                            start=(l == 0), stop=(l == 11)),
                            inc=(s_pe if (kv4 == 3 and l == 11) else None))
                qb_done = s_pe.n
                P.wait("act", s_pe, qb_done)
                if half == 0:
                    P.wait("act", s_act, qev_done)
                for kv4 in range(4):
                    kv8 = half * 4 + kv4
                    if kv4 == 3:
                        P.drain("act")
                    P.op("act", lambda e, kv4=kv4, kv8=kv8, sl=sl: e.copy(
                        out=qtm[sl][:, kv8 * 384:(kv8 + 1) * 384], in_=psq4[kv4]),
                        inc=(s_act if kv4 == 3 else None))
            qev_done = s_act.n
            P.wait("dve", s_act, qev_done)
            P.wait("dve", scs[sl], cs_ld)
            qa16 = qtm[sl].rearrange("p (h d) -> p h d", h=16)
            qe = qa16[:, :, 128:192:2]
            qo = qa16[:, :, 129:192:2]
            ct = cs_sl[sl][0].rearrange("p (h d) -> p h d", h=16)
            st_ = cs_sl[sl][1].rearrange("p (h d) -> p h d", h=16)
            P.op("dve", lambda e, qe=qe, ct=ct: e.tensor_tensor(
                out=AR1[:, 3072:3584].rearrange("p (h d) -> p h d", h=16),
                in0=qe, in1=ct, op=ALU.mult), inc=None)
            P.op("dve", lambda e, qo=qo, st_=st_: e.tensor_tensor(
                out=AR1[:, 3584:4096].rearrange("p (h d) -> p h d", h=16),
                in0=qo, in1=st_, op=ALU.mult), inc=None)
            P.op("dve", lambda e, qo=qo, ct=ct: e.tensor_tensor(
                out=AR1[:, 4096:4608].rearrange("p (h d) -> p h d", h=16),
                in0=qo, in1=ct, op=ALU.mult), inc=None)
            P.op("dve", lambda e, qe=qe, st_=st_: e.tensor_tensor(
                out=AR1[:, 4608:5120].rearrange("p (h d) -> p h d", h=16),
                in0=qe, in1=st_, op=ALU.mult), inc=None)
            P.drain("dve")
            P.op("dve", lambda e, qe=qe: e.tensor_tensor(
                out=qe,
                in0=AR1[:, 3072:3584].rearrange("p (h d) -> p h d", h=16),
                in1=AR1[:, 3584:4096].rearrange("p (h d) -> p h d", h=16),
                op=ALU.subtract), inc=None)
            P.op("dve", lambda e, qo=qo: e.tensor_tensor(
                out=qo,
                in0=AR1[:, 4096:4608].rearrange("p (h d) -> p h d", h=16),
                in1=AR1[:, 4608:5120].rearrange("p (h d) -> p h d", h=16),
                op=ALU.add), inc=None)
            P.drain("dve")
            for hh in range(16):
                P.op("dve", lambda e, hh=hh, sl=sl, tq=tq: e.scalar_tensor_tensor(
                    out=ropescr[:, (hh % 4) * 512:(hh % 4) * 512 + 192],
                    in0=qtm[sl][:, hh * 192:(hh + 1) * 192], scalar=1.0,
                    in1=qtm[sl][:, hh * 192:(hh + 1) * 192],
                    op0=ALU.mult, op1=ALU.mult,
                    accum_out=ssq_all[:, tq * 16 + hh:tq * 16 + hh + 1]),
                    inc=(s_dve if hh == 15 else None))
                if hh % 4 == 3:
                    P.drain("dve")
            rope_done = s_dve.n
            cs_free[sl] = (s_dve, rope_done)
            P.wait("sp", s_dve, rope_done)
            P.op("sp", lambda e, tq=tq, sl=sl: e.dma_start(
                out=qspill[tq * 128:(tq + 1) * 128, :], in_=qtm[sl][:, :]),
                inc=sqs[sl], inck=16)
            qtm_free[sl] = (sqs[sl], sqs[sl].n)

        a2d = s_dve.n
        P.wait("dve", s_dve, a2d)
        for hh in range(16):
            P.op("dve", lambda e, hh=hh: e.reduce_max(
                out=colscr[:, 64 + hh:65 + hh],
                in_=ssq_all[:, 0:NT * 16].rearrange(
                    "p (t h) -> p h t", h=16)[:, hh, :],
                axis=AX.X), inc=(s_dve if hh == 15 else None))
        mq1 = s_dve.n
        P.wait("pe", s_dve, mq1)
        P.op("pe", lambda e: e.transpose(g5[0:16, 0:128], colscr[:, 64:80],
                                         identf[:, :]), inc=s_pe)
        mq2 = s_pe.n
        P.wait("dve", s_pe, mq2)
        P.op("dve", lambda e: e.reduce_max(out=colscr[0:16, 80:81],
                                           in_=g5.ap()[0:16, 0:128],
                                           axis=AX.X), inc=s_dve)
        mq3 = s_dve.n
        P.wait("pe", s_dve, mq3)
        P.op("pe", lambda e: e.transpose(g6[0:1, 0:16], colscr[0:16, 80:81],
                                         identf[0:16, 0:16]), inc=s_pe)
        mq4 = s_pe.n
        P.wait("act", s_pe, mq4)
        P.op("act", lambda e: e.activation(out=rowscr[0:1, 0:16],
                                           in_=g6.ap()[0:1, 0:16],
                                           func=AF.Ln, bias=fc[0:1, 3:4]), inc=None)
        P.drain("act")
        P.op("act", lambda e: e.activation(out=rowscr[0:1, 16:32],
                                           in_=rowscr[0:1, 0:16],
                                           func=AF.Exp, scale=0.5), inc=s_act)
        maxq_done = s_act.n

        barrier()
        kvcfm = [vbf(AR2, c * (T // 2), T) for c in range(4)]
        kropeF = vbf(AR2, 2 * T, T)
        kvaw_bf = vbf(AR1, 0, 16 * 576)
        kvtmp = v32(AR1, 8704, 512)
        ln1tmK = vbf(AR1, 9216, 2048)
        ln1fmK = vbf(AR1, 10240, 2048)
        kvc_tm = vbf(AR1, 11264, 512)
        krope_tm = vbf(AR1, 11520, 64)
        sqscrK = v32(AR1, 11552, 512)
        ropescrK = v32(AR1, 12064, 128)
        cosb = v32(AR1, 12288, NKT * 32)
        sinb = v32(AR1, 12288 + NKT * 32, NKT * 32)

        P.op("pool", lambda e: e.dma_start(
            out=AR1.ap().bitcast(BF16)[:, 0:16 * 576].rearrange(
                "p (c n) -> p c n", c=16),
            in_=kv_a_w.ap().rearrange("(c p) n -> p c n", p=128)),
            inc=s_wa, inck=16)
        AW3 = s_wa.n
        P.op("sp", lambda e: e.dma_start(
            out=cosb.rearrange("p (n c) -> p n c", c=32),
            in_=cos_all.ap().rearrange("(n p) c -> p n c", p=128)),
            inc=s_kv, inck=16)
        P.op("sp", lambda e: e.dma_start(
            out=sinb.rearrange("p (n c) -> p n c", c=32),
            in_=sin_all.ap().rearrange("(n p) c -> p n c", p=128)),
            inc=s_kv, inck=16)
        CS_LD = s_kv.n

        xt_free = [None, None]
        fm_evacK = 0
        kvst_done = 0
        for i in range(NKT):
            sl = i % 2
            if xt_free[sl] is not None:
                P.wait("sp", *xt_free[sl])
            P.op("sp", lambda e, i=i, sl=sl: e.dma_start(
                out=AR1[:, 4608 + sl * 2048:4608 + sl * 2048 + 2048],
                in_=x_batch[i * 128:(i + 1) * 128, :]), inc=sx[sl], inck=16)
            x_ld = sx[sl].n
            P.wait("dve", sx[sl], x_ld)
            for j in range(4):
                P.op("dve", lambda e, sl=sl, j=j: e.bn_stats(
                    out=bnst[:, j * 6:(j + 1) * 6],
                    in_=AR1[:, 4608 + sl * 2048 + j * 512:4608 + sl * 2048 + (j + 1) * 512]),
                    inc=None)
            P.drain("dve")
            P.op("dve", lambda e: e.bn_aggr(out=mvt[:, 0:2], in_=bnst[:, :]),
                 inc=s_dve)
            bn_done = s_dve.n
            P.wait("act", s_dve, bn_done)
            P.op("act", lambda e: e.activation(out=rstdt[:, 0:1], in_=mvt[:, 1:2],
                                               func=AF.Ln, bias=fc[:, 0:1]), inc=None)
            P.drain("act")
            P.op("act", lambda e: e.activation(out=rstdt[:, 1:2], in_=rstdt[:, 0:1],
                                               func=AF.Exp, scale=-0.5), inc=s_act)
            rstd_done = s_act.n
            P.wait("dve", s_act, rstd_done)
            P.op("dve", lambda e, sl=sl: e.tensor_scalar(
                out=ln1tmK, in0=AR1[:, 4608 + sl * 2048:4608 + sl * 2048 + 2048],
                scalar1=mvt[:, 0:1], scalar2=rstdt[:, 1:2],
                op0=ALU.subtract, op1=ALU.mult), inc=s_dve)
            ln1_done = s_dve.n
            xt_free[sl] = (s_dve, ln1_done)
            P.wait("pe", s_dve, ln1_done)
            if i > 0:
                P.wait("pe", s_act, fm_evacK)
            for j in range(16):
                P.op("pe", lambda e, j=j: e.transpose(
                    (psA if j < 8 else psB)[:, (j % 8) * 128:(j % 8) * 128 + 128],
                    ln1tmK[:, j * 128:(j + 1) * 128], identb[:, :]),
                    inc=(s_pe if j == 15 else None))
            tp_done = s_pe.n
            P.wait("act", s_pe, tp_done)
            P.op("act", lambda e: e.copy(out=ln1fmK[:, 0:1024], in_=psA[:, :]), inc=None)
            P.drain("act")
            P.op("act", lambda e: e.copy(out=ln1fmK[:, 1024:2048], in_=psB[:, :]),
                 inc=s_act)
            fm_evacK = s_act.n
            P.wait("pe", s_act, fm_evacK)
            if i == 0:
                P.wait("pe", s_wa, AW3)
            for c in range(16):
                P.op("pe", lambda e, c=c: e.matmul(
                    ps1[:, :], ln1fmK[:, c * 128:(c + 1) * 128],
                    kvaw_bf[:, c * 576:c * 576 + 512],
                    start=(c == 0), stop=(c == 15)), inc=None)
            for c in range(16):
                P.op("pe", lambda e, c=c: e.matmul(
                    ps2[:, 0:64], ln1fmK[:, c * 128:(c + 1) * 128],
                    kvaw_bf[:, c * 576 + 512:c * 576 + 576],
                    start=(c == 0), stop=(c == 15)),
                    inc=(s_pe if c == 15 else None))
            kva_done = s_pe.n
            P.wait("act", s_pe, kva_done)
            P.op("act", lambda e: e.copy(out=kvtmp, in_=ps1[:, :]), inc=s_act)
            kvtmp_done = s_act.n
            P.wait("dve", s_act, kvtmp_done)
            P.op("dve", lambda e: e.scalar_tensor_tensor(
                out=sqscrK, in0=kvtmp, scalar=1.0, in1=kvtmp,
                op0=ALU.mult, op1=ALU.mult, accum_out=rstdt[:, 2:3]), inc=s_dve)
            kvssq_done = s_dve.n
            P.wait("act", s_dve, kvssq_done)
            P.op("act", lambda e: e.activation(out=rstdt[:, 3:4], in_=rstdt[:, 2:3],
                                               func=AF.Ln, scale=1.0 / KVL,
                                               bias=fc[:, 1:2]), inc=None)
            P.drain("act")
            P.op("act", lambda e: e.activation(out=rstdt[:, 2:3], in_=rstdt[:, 3:4],
                                               func=AF.Exp, scale=-0.5), inc=s_act)
            kvrstd_done = s_act.n
            P.wait("dve", s_act, kvrstd_done)
            P.op("dve", lambda e: e.tensor_scalar(
                out=kvc_tm, in0=kvtmp, scalar1=rstdt[:, 2:3], scalar2=None,
                op0=ALU.mult), inc=s_dve)
            kvc_done = s_dve.n
            if i == 0:
                P.wait("dve", s_kv, CS_LD)
            ctK = cosb[:, i * 32:(i + 1) * 32]
            stK = sinb[:, i * 32:(i + 1) * 32]
            ke = ps2[:, 0:64:2]
            ko = ps2[:, 1:64:2]
            P.op("dve", lambda e, ke=ke, ctK=ctK: e.tensor_tensor(
                out=ropescrK[:, 0:32], in0=ke, in1=ctK, op=ALU.mult), inc=None)
            P.op("dve", lambda e, ko=ko, stK=stK: e.tensor_tensor(
                out=ropescrK[:, 32:64], in0=ko, in1=stK, op=ALU.mult), inc=None)
            P.drain("dve")
            P.op("dve", lambda e: e.tensor_tensor(
                out=krope_tm[:, 0:64:2], in0=ropescrK[:, 0:32],
                in1=ropescrK[:, 32:64], op=ALU.subtract), inc=None)
            P.op("dve", lambda e, ko=ko, ctK=ctK: e.tensor_tensor(
                out=ropescrK[:, 64:96], in0=ko, in1=ctK, op=ALU.mult), inc=None)
            P.op("dve", lambda e, ke=ke, stK=stK: e.tensor_tensor(
                out=ropescrK[:, 96:128], in0=ke, in1=stK, op=ALU.mult), inc=None)
            P.drain("dve")
            P.op("dve", lambda e: e.tensor_tensor(
                out=krope_tm[:, 1:64:2], in0=ropescrK[:, 64:96],
                in1=ropescrK[:, 96:128], op=ALU.add), inc=s_dve)
            krope_done = s_dve.n
            P.wait("pe", s_dve, krope_done)
            if i > 0:
                P.wait("pe", s_act, kvst_done)
            for j in range(4):
                P.op("pe", lambda e, j=j: e.transpose(
                    psA[:, j * 128:(j + 1) * 128], kvc_tm[:, j * 128:(j + 1) * 128],
                    identb[:, :]), inc=None)
            P.op("pe", lambda e: e.transpose(
                psA[0:64, 512:640], krope_tm[:, 0:64], identb[:, :]),
                inc=s_pe)
            kvtp_done = s_pe.n
            P.wait("act", s_pe, kvtp_done)
            for cc in range(4):
                P.op("act", lambda e, cc=cc, i=i: e.copy(
                    out=kvcfm[cc][:, i * 128:(i + 1) * 128],
                    in_=psA[:, cc * 128:(cc + 1) * 128]), inc=None)
            P.drain("act")
            P.op("act", lambda e, i=i: e.copy(
                out=kropeF[0:64, i * 128:(i + 1) * 128],
                in_=psA[0:64, 512:640]), inc=s_act)
            kvst_done = s_act.n

        barrier()
        V2 = vbf(AR1, 0, NKT * 256)
        K_h = [vbf(AR1, 8192 + s * 4096, T) for s in range(2)]
        kvb2f = vbf(AR1, 16384, 4 * 512)
        masksl = vbf(AR1, 17408, 8 * 512)
        qtw = [vbf(AR1, 19456 + s * 384, 768) for s in range(2)]
        Qn_w = [vbf(AR1, 20224 + s * 256, 512) for s in range(2)]
        Qr_w = [vbf(AR1, 20736 + s * 256, 512) for s in range(2)]
        expS = [vbf(AR1, 21248 + s * 512, 1024) for s in range(2)]
        bcastf = v32(AR1, 22272, 512)
        recf = bcastf
        recf2 = bcastf2
        dacc = [v32(AR1, 22784 + s * 512, 512) for s in range(2)]
        outh_sl = [vbf(AR1, 23808 + s * 256, 512) for s in range(2)]
        sqr = [s_qr0, s_qr1]
        assert 23808 + 512 <= 24576

        SQRT_QK = math.sqrt(float(QK))
        exp_bank_free = [0, 0]
        v2_evac = 0
        kh_evac = 0
        g5_free = 0
        g12_expfree = [0, 0]
        expS_free = [0, 0]
        qtw_free = [None, None]
        qw_tp_ev = 0
        mask_used = 0
        kvb_fold_done = 0
        kvb_pe_used = 0
        outh_free = [None, None]
        psO_free = [0, 0]
        g3_ev = 0
        for hp in range(8):
            if hp == 0:
                P.op("pool", lambda e, hp=hp: e.dma_start(
                    out=kvb2f.rearrange("p (c n) -> p c n", c=4),
                    in_=kv_b_w.ap().rearrange("(c p) n -> p c n", p=128)[
                        :, :, hp * 512:(hp + 1) * 512]), inc=s_kb, inck=16)
            KB = s_kb.n
            P.wait("dve", s_kb, KB)
            for cc in range(4):
                P.op("dve", lambda e, cc=cc: e.tensor_scalar(
                    out=kvb2f[:, cc * 512:(cc + 1) * 512],
                    in0=kvb2f[:, cc * 512:(cc + 1) * 512],
                    scalar1=wkvc_sb[:, cc:cc + 1], scalar2=None, op0=ALU.mult),
                    inc=(s_dve if cc == 3 else None))
            kvb_fold_done = s_dve.n
            P.wait("pe", s_dve, kvb_fold_done)
            P.wait("pe", s_dve, g5_free)
            for kt in range(NKT):
                psV = (g3, g4)[kt % 2]
                P.wait("pe", s_act, exp_bank_free[kt % 2])
                for cc in range(4):
                    P.op("pe", lambda e, kt=kt, cc=cc, psV=psV: e.matmul(
                        psV[:, 0:256],
                        kvcfm[cc][:, kt * 128:(kt + 1) * 128],
                        kvb2f[:, cc * 512:(cc + 1) * 512].rearrange(
                            "p (h z) -> p h z", h=2)[:, :, 128:256],
                        start=(cc == 0), stop=(cc == 3)),
                        inc=(s_pe if cc == 3 else None))
                vkt = s_pe.n
                P.wait("act", s_pe, vkt)
                P.op("act", lambda e, kt=kt, psV=psV: e.copy(
                    out=V2[:, kt * 256:(kt + 1) * 256], in_=psV.ap()[:, 0:256]),
                    inc=s_act)
                v2_evac = s_act.n
                exp_bank_free[kt % 2] = s_act.n
            for hl in range(2):
                for win in range(T // 512):
                    psK = (g3, g4)[win % 2]
                    P.wait("pe", s_act, exp_bank_free[win % 2])
                    for cc in range(4):
                        P.op("pe", lambda e, win=win, cc=cc, psK=psK, hl=hl: e.matmul(
                            psK[:, :],
                            kvb2f[:, cc * 512 + hl * 256:cc * 512 + hl * 256 + 128],
                            kvcfm[cc][:, win * 512:(win + 1) * 512],
                            start=(cc == 0), stop=(cc == 3)),
                            inc=(s_pe if cc == 3 else None))
                    kkt = s_pe.n
                    P.wait("act", s_pe, kkt)
                    P.op("act", lambda e, win=win, psK=psK, hl=hl: e.copy(
                        out=K_h[hl][:, win * 512:(win + 1) * 512], in_=psK.ap()),
                        inc=s_act)
                    kh_evac = s_act.n
                    exp_bank_free[win % 2] = s_act.n
            kvb_pe_used = s_pe.n
            if hp + 1 < 8:
                P.wait("pool", s_pe, kvb_pe_used)
                P.wait("pool", s_dve, kvb_fold_done)
                P.op("pool", lambda e, hp=hp: e.dma_start(
                    out=kvb2f.rearrange("p (c n) -> p c n", c=4),
                    in_=kv_b_w.ap().rearrange("(c p) n -> p c n", p=128)[
                        :, :, (hp + 1) * 512:(hp + 2) * 512]), inc=s_kb, inck=16)
            P.wait("dve", s_act, kh_evac)
            if hp == 0:
                P.op("dve", lambda e: e.memset(colscr[:, 8:9], 0.0), inc=None)
                P.drain("dve")
                P.op("dve", lambda e: e.reduce_max(
                    out=colscr[0:64, 8:9], in_=kropeF[0:64, :], axis=AX.X,
                    apply_absolute_value=True), inc=None)
                P.drain("dve")
                P.wait("dve", s_act, maxq_done)
            for hl in range(2):
                h = hp * 2 + hl
                P.op("dve", lambda e, hl=hl: e.reduce_max(
                    out=colscr[:, 9:10], in_=K_h[hl][:, :], axis=AX.X,
                    apply_absolute_value=True), inc=None)
                P.drain("dve")
                P.op("dve", lambda e: e.tensor_tensor(
                    out=colscr[:, 10:11], in0=colscr[:, 9:10],
                    in1=colscr[:, 8:9], op=ALU.max), inc=s_dve)
                mk1 = s_dve.n
                P.wait("pe", s_dve, mk1)
                P.wait("pe", s_act, s_act.n)
                P.op("pe", lambda e: e.transpose(
                    g6[0:1, 0:128], colscr[:, 10:11], identf[:, :]), inc=s_pe)
                mk2 = s_pe.n
                P.wait("dve", s_pe, mk2)
                P.op("dve", lambda e: e.reduce_max(
                    out=rowscr[0:1, 40:41], in_=g6.ap()[0:1, 0:128], axis=AX.X),
                    inc=None)
                P.drain("dve")
                P.op("dve", lambda e, h=h: e.tensor_tensor(
                    out=rowscr[0:1, 41:42], in0=rowscr[0:1, 16 + h:17 + h],
                    in1=rowscr[0:1, 40:41], op=ALU.mult), inc=None)
                P.drain("dve")
                P.op("dve", lambda e, hl=hl: e.tensor_scalar(
                    out=nb_bf[0:1, hl:hl + 1], in0=rowscr[0:1, 41:42],
                    scalar1=-SQRT_QK, scalar2=None, op0=ALU.mult), inc=s_dve)
                negc_done = s_dve.n
                P.wait("pe", s_dve, negc_done)
                P.wait("pe", s_dve, psO_free[1])
                P.op("pe", lambda e, hl=hl: e.matmul(
                    g6[:, 0:1], onesb[0:1, 0:128], nb_bf[0:1, hl:hl + 1],
                    start=True, stop=True), inc=s_pe)
                negb_mm = s_pe.n
                P.wait("act", s_pe, negb_mm)
                P.op("act", lambda e, hl=hl: e.copy(out=colscr[:, 12 + hl:13 + hl],
                                                    in_=g6.ap()[:, 0:1]), inc=s_act)
            negc_bc = s_act.n

            for w in range(NCH):
                lim = LIMS[w]
                kmax2 = lim // 2
                mstart = lim - 16
                for hl in range(2):
                    h = hp * 2 + hl
                    sl = hl
                    if qtw_free[sl] is not None:
                        P.wait("sp", *qtw_free[sl])
                    P.op("sp", lambda e, h=h, w=w, sl=sl: e.dma_start(
                        out=qtw[sl].rearrange("p (tt d) -> p tt d", tt=4),
                        in_=qspill[w * 512:(w + 1) * 512,
                                   h * 192:(h + 1) * 192].rearrange(
                            "(tt p) d -> p tt d", p=128)), inc=sqr[sl], inck=16)
                    q_ld = sqr[sl].n
                    P.wait("pe", sqr[sl], q_ld)
                    P.wait("pe", s_act, qw_tp_ev)
                    if w == 0 and hl == 0:
                        P.wait("pe", s_act, max(kh_evac, v2_evac, negc_bc))
                        P.wait("pe", s_dve, g5_free)
                    for tt in range(4):
                        P.op("pe", lambda e, tt=tt, sl=sl: e.transpose(
                            psA[:, tt * 128:(tt + 1) * 128],
                            qtw[sl][:, tt * 192:tt * 192 + 128], identb[:, :]),
                            inc=None)
                    for tt in range(4):
                        P.op("pe", lambda e, tt=tt, sl=sl: e.transpose(
                            psB[0:64, tt * 128:(tt + 1) * 128],
                            qtw[sl][:, tt * 192 + 128:(tt + 1) * 192], identb[:, :]),
                            inc=(s_pe if tt == 3 else None))
                    qtp2 = s_pe.n
                    qtw_free[sl] = (s_pe, qtp2)
                    P.wait("act", s_pe, qtp2)
                    P.op("act", lambda e, hl=hl: e.copy(out=Qn_w[hl][:, :],
                                                        in_=psA[:, 0:512]),
                         inc=None)
                    P.drain("act")
                    P.op("act", lambda e, hl=hl: e.copy(out=Qr_w[hl][0:64, :],
                                                        in_=psB[0:64, 0:512]),
                         inc=s_act)
                    qw_tp_ev = s_act.n

                P.wait("pe", s_act, qw_tp_ev)
                for dw in range(16):
                    P.op("pe", lambda e: e.matmul(
                        g3[:, :], onesb[:, :], hsu_sb[:, 0:512],
                        start=True, stop=True),
                        inc=(s_pe if dw == 15 else None))
                warm_done = s_pe.n
                mk_ld = [0, 0]
                att_scores = {}
                att_expd = {}
                att_maskd = {}
                att_dacc = {}
                psS_h = (g1, g2)
                psO_h = (g5, g6)
                mhalf_loaded = [False, False]

                def load_mask_half(mh, w=w):
                    P.wait("sp", s_dve, s_dve.n)
                    P.op("sp", lambda e, w=w, mh=mh: e.dma_start(
                        out=masksl[:, :],
                        in_=masks_in.ap()[:, w * 8192 + mh * 4096:
                                          w * 8192 + (mh + 1) * 4096]),
                        inc=s_mk, inck=16)
                    mk_ld[mh] = s_mk.n

                def emit_attnv(hl, j, kmax2=kmax2):
                    first = (j == 0)
                    last = (j == kmax2 - 1)
                    P.wait("pe", s_act, att_expd[(hl, j)])
                    if first:
                        P.wait("pe", s_dve, psO_free[hl])
                    if (hl, j) in att_maskd:
                        P.wait("pe", s_dve, att_maskd[(hl, j)])
                    for sub in range(2):
                        kt = 2 * j + sub
                        P.op("pe", lambda e, kt=kt, hl=hl, sub=sub,
                             first=first, last=last: e.matmul(
                            psO_h[hl][:, :],
                            V2[:, kt * 256 + hl * 128:kt * 256 + (hl + 1) * 128],
                            expS[hl][:, sub * 512:(sub + 1) * 512],
                            start=(first and sub == 0),
                            stop=(last and sub == 1)),
                            inc=(s_pe if sub == 1 else None))
                    expS_free[hl] = s_pe.n

                for kg in range(kmax2):
                    need_mh0 = (2 * kg + 1 >= mstart) and not mhalf_loaded[0]
                    need_mh1 = (2 * kg + 1 >= mstart + 8) and not mhalf_loaded[1]
                    if need_mh0:
                        load_mask_half(0)
                        mhalf_loaded[0] = True
                    if need_mh1:
                        load_mask_half(1)
                        mhalf_loaded[1] = True
                    for hl in range(2):
                        psS = psS_h[hl]
                        if kg >= 1:
                            emit_attnv(hl, kg - 1)
                        P.wait("pe", s_act, g12_expfree[hl])
                        if kg == 0:
                            P.wait("pe", s_act, qw_tp_ev)
                        for sub in range(2):
                            kt = 2 * kg + sub
                            P.op("pe", lambda e, kt=kt, psS=psS, sub=sub, hl=hl: e.matmul(
                                psS[:, sub * 512:(sub + 1) * 512],
                                K_h[hl][:, kt * 128:(kt + 1) * 128], Qn_w[hl][:, :],
                                start=True, stop=False), inc=None)
                            P.op("pe", lambda e, kt=kt, psS=psS, sub=sub, hl=hl: e.matmul(
                                psS[:, sub * 512:(sub + 1) * 512],
                                kropeF[0:64, kt * 128:(kt + 1) * 128],
                                Qr_w[hl][0:64, :],
                                start=False, stop=True),
                                inc=(s_pe if sub == 1 else None))
                        att_scores[(hl, kg)] = s_pe.n
                        P.wait("act", s_pe, att_scores[(hl, kg)])
                        P.wait("act", s_pe, expS_free[hl])
                        if (hl, kg - 1) in att_dacc:
                            P.wait("act", s_dve, att_dacc[(hl, kg - 1)])
                        P.op("act", lambda e, hl=hl, psS=psS: e.activation(
                            out=expS[hl][:, :], in_=psS[:, :], func=AF.Exp,
                            bias=colscr[:, 12 + hl:13 + hl]), inc=s_act)
                        att_expd[(hl, kg)] = s_act.n
                        g12_expfree[hl] = s_act.n
                        diag_subs = [sub for sub in range(2)
                                     if 2 * kg + sub >= mstart]
                        P.wait("dve", s_act, att_expd[(hl, kg)])
                        if diag_subs:
                            for sub in diag_subs:
                                kt = 2 * kg + sub
                                pat = kt - mstart
                                P.wait("dve", s_mk, mk_ld[pat // 8])
                                P.op("dve", lambda e, hl=hl, sub=sub, pat=pat: e.tensor_tensor(
                                    out=expS[hl][:, sub * 512:(sub + 1) * 512],
                                    in0=expS[hl][:, sub * 512:(sub + 1) * 512],
                                    in1=masksl[:, (pat % 8) * 512:(pat % 8) * 512 + 512],
                                    op=ALU.mult),
                                    inc=s_dve)
                            att_maskd[(hl, kg)] = s_dve.n
                            P.drain("dve")
                        if kg == 0:
                            P.op("dve", lambda e, hl=hl: e.tensor_copy(
                                out=dacc[hl][:, :], in_=expS[hl][:, 0:512]),
                                inc=None)
                        else:
                            P.op("dve", lambda e, hl=hl: e.tensor_tensor(
                                out=dacc[hl][:, :], in0=dacc[hl][:, :],
                                in1=expS[hl][:, 0:512], op=ALU.add), inc=None)
                        P.drain("dve")
                        P.op("dve", lambda e, hl=hl: e.tensor_tensor(
                            out=dacc[hl][:, :], in0=dacc[hl][:, :],
                            in1=expS[hl][:, 512:1024], op=ALU.add), inc=s_dve)
                        P.drain("dve")
                        att_dacc[(hl, kg)] = s_dve.n
                mask_used = s_dve.n
                for hl in range(2):
                    emit_attnv(hl, kmax2 - 1)
                dnb = (g3, g4)
                recfb = (recf, recf2)
                recbb = (recb, recb2)
                bcfb = (bcastf, bcastf2)
                dn_red = [0, 0]
                for hl in range(2):
                    P.wait("pe", s_dve, att_dacc[(hl, kmax2 - 1)])
                    if hl == 0:
                        P.wait("pe", s_act, qw_tp_ev)
                        P.wait("pe", s_act, g3_ev)
                    P.op("pe", lambda e, hl=hl: e.matmul(
                        dnb[hl][0:1, 0:512], onesf[:, 0:1], dacc[hl][:, :],
                        start=True, stop=True), inc=s_pe)
                    dn_red[hl] = s_pe.n
                recb_d = [0, 0]
                for hl in range(2):
                    P.wait("dve", s_pe, dn_red[hl])
                    P.drain("dve")
                    P.op("dve", lambda e, hl=hl: e.reciprocal(
                        out=recfb[hl][0:1, :], in_=dnb[hl].ap()[0:1, 0:512]),
                        inc=s_dve)
                    P.drain("dve")
                    P.op("dve", lambda e, hl=hl: e.tensor_copy(
                        out=recbb[hl][0:1, :], in_=recfb[hl][0:1, :]),
                        inc=s_dve)
                    recb_d[hl] = s_dve.n
                bc_ev = [0, 0]
                for hl in range(2):
                    P.wait("pe", s_dve, recb_d[hl])
                    P.op("pe", lambda e, hl=hl: e.matmul(
                        dnb[hl][:, 0:512], onesb[0:1, 0:128],
                        recbb[hl][0:1, :], start=True, stop=True), inc=s_pe)
                    bc_mm = s_pe.n
                    P.wait("act", s_pe, bc_mm)
                    P.op("act", lambda e, hl=hl: e.copy(
                        out=bcfb[hl][:, :], in_=dnb[hl].ap()[:, 0:512]),
                        inc=s_act)
                    bc_ev[hl] = s_act.n
                g3_ev = bc_ev[1]
                for hl in range(2):
                    h = hp * 2 + hl
                    attv_done = expS_free[hl]
                    P.wait("dve", s_act, bc_ev[hl])
                    P.wait("dve", s_pe, attv_done)
                    if outh_free[hl] is not None:
                        P.wait("dve", *outh_free[hl])
                    P.op("dve", lambda e, hl=hl: e.tensor_tensor(
                        out=outh_sl[hl][:, :], in0=psO_h[hl].ap()[:, :],
                        in1=bcfb[hl][:, :], op=ALU.mult), inc=s_dve)
                    g5_free = s_dve.n
                    psO_free[hl] = s_dve.n
                    P.wait("sp", s_dve, g5_free)
                    sc2h = (s_c20, s_c21)[hl]
                    P.op("sp", lambda e, h=h, w=w, hl=hl: e.dma_start(
                        out=ospill[h * 128:(h + 1) * 128, w * 512:(w + 1) * 512],
                        in_=outh_sl[hl][:, :]), inc=sc2h, inck=16)
                    outh_free[hl] = (sc2h, sc2h.n)
                qw_tp_ev = bc_ev[1]

        barrier()
        oww = vbf(AR1, 0, 16 * 2048)
        ohw = [vbf(AR1, 16384 + s * 2048, 16 * 256) for s in range(2)]
        aof32 = v32(AR2, 12288, 4096)
        xb2 = [v32(AR2, s * 2048, 2048) for s in range(2)]
        atm = v32(AR2, 4096, 2048)
        lnout = v32(AR2, 6144, 2048)
        x2t = [v32(AR2, 8192 + s * 2048, 2048) for s in range(2)]
        soh = [s_oh0, s_oh1]
        sx2 = [s_x2r0, s_x2r1]

        P.op("pool", lambda e: e.dma_start(
            out=oww.rearrange("p (c n) -> p c n", c=16),
            in_=o_w.ap().rearrange("(c p) n -> p c n", p=128)),
            inc=s_ow, inck=16)
        OW_LD = s_ow.n
        def psdc(dc):
            if dc < 4:
                return g1, (dc % 4) * 256
            if dc < 8:
                return g2, (dc % 4) * 256
            return (g3, g4, g5, g6)[(dc - 8) // 2], ((dc - 8) % 2) * 256

        aofb = [aof32, v32(AR2, 16384, 4096)]
        atm2 = [atm, v32(AR1, 20480, 2048)]
        ohw_free = [None, None]
        x2t_free = [None, None]
        xb2_free = [None, None]
        aev = [0]
        atmu = [0, 0]
        pend_b2 = [None]

        def b2_chain(w):
            sl = w % 2
            for tau in range(2):
                i_own = w * 2 + tau
                if xb2_free[sl] is not None and tau == 0:
                    P.wait("sp", *xb2_free[sl])
                P.op("sp", lambda e, w=w, sl=sl, tau=tau: e.dma_start(
                    out=AR2[:, sl * 2048:sl * 2048 + 2048],
                    in_=x_own[w * 256 + tau * 128:w * 256 + (tau + 1) * 128, :]),
                    inc=sx2[sl], inck=16)
                xld = sx2[sl].n
                P.wait("pe", s_act, aev[0])
                P.wait("pe", s_dve, atmu[tau])
                for dc in range(16):
                    P.op("pe", lambda e, dc=dc, tau=tau, sl=sl: e.transpose(
                        (g1 if dc < 8 else g2)[:, (dc % 8) * 128:(dc % 8) * 128 + 128],
                        aofb[sl][:, dc * 256 + tau * 128:dc * 256 + (tau + 1) * 128],
                        identf[:, :]), inc=(s_pe if dc == 15 else None))
                tp2 = s_pe.n
                P.wait("act", s_pe, tp2)
                P.op("act", lambda e, tau=tau: e.copy(
                    out=atm2[tau][:, 0:1024], in_=g1[:, :]), inc=None)
                P.drain("act")
                P.op("act", lambda e, tau=tau: e.copy(
                    out=atm2[tau][:, 1024:2048], in_=g2[:, :]), inc=s_act)
                atm_ev = s_act.n
                aev[0] = s_act.n
                P.wait("dve", s_act, atm_ev)
                for j in range(4):
                    P.op("dve", lambda e, j=j, tau=tau: e.bn_stats(
                        out=bnst[:, j * 6:(j + 1) * 6],
                        in_=atm2[tau][:, j * 512:(j + 1) * 512]), inc=None)
                P.drain("dve")
                P.op("dve", lambda e: e.bn_aggr(out=mvt[:, 0:2], in_=bnst[:, :]),
                     inc=s_dve)
                bn2 = s_dve.n
                P.wait("act", s_dve, bn2)
                P.op("act", lambda e: e.activation(out=rstdt[:, 0:1], in_=mvt[:, 1:2],
                                                   func=AF.Ln, bias=fc[:, 0:1]), inc=None)
                P.drain("act")
                P.op("act", lambda e: e.activation(out=rstdt[:, 1:2], in_=rstdt[:, 0:1],
                                                   func=AF.Exp, scale=-0.5), inc=s_act)
                rstd2 = s_act.n
                P.wait("dve", s_act, rstd2)
                P.op("dve", lambda e, tau=tau: e.tensor_scalar(
                    out=lnout[:, :], in0=atm2[tau][:, :], scalar1=mvt[:, 0:1],
                    scalar2=rstdt[:, 1:2], op0=ALU.subtract, op1=ALU.mult),
                    inc=s_dve)
                atmu[tau] = s_dve.n
                P.drain("dve")
                P.wait("dve", sx2[sl], xld)
                if x2t_free[sl] is not None:
                    P.wait("dve", *x2t_free[sl])
                P.op("dve", lambda e, sl=sl: e.tensor_tensor(
                    out=x2t[sl][:, :], in0=lnout[:, :],
                    in1=AR2[:, sl * 2048:sl * 2048 + 2048], op=ALU.add),
                    inc=s_dve)
                x2_done = s_dve.n
                xb2_free[sl] = (s_dve, x2_done)
                P.wait("sp", s_dve, x2_done)
                P.op("sp", lambda e, i_own=i_own, sl=sl: e.dma_start(
                    out=x2d[i_own * 128:(i_own + 1) * 128, :], in_=x2t[sl][:, :]),
                    inc=s_out, inck=16)
                x2t_free[sl] = (s_out, s_out.n)

        def load_ohw(w):
            sl = w % 2
            if ohw_free[sl] is not None:
                P.wait("sp", *ohw_free[sl])
            P.op("sp", lambda e, w=w, sl=sl: e.dma_start(
                out=ohw[sl].rearrange("p (hh t) -> p hh t", hh=16),
                in_=ospill[:, :].rearrange("(hh p) t -> p hh t", p=128)[
                    :, :, w * 256:(w + 1) * 256]), inc=soh[sl], inck=16)
            return soh[sl].n

        oh_lds = [0] * (OWN // 256)
        oh_lds[0] = load_ohw(0)
        for w in range(OWN // 256):
            sl = w % 2
            if w > 0:
                P.wait("pe", s_act, aev[0])
                for dw in range(16):
                    P.op("pe", lambda e: e.matmul(
                        g1[:, 0:512], onesb[:, :], hsu_sb[:, 0:512],
                        start=True, stop=True),
                        inc=(s_pe if dw == 15 else None))
            P.wait("pe", soh[sl], oh_lds[w])
            if w == 0:
                P.wait("pe", s_ow, OW_LD)
            for dc in range(16):
                pst, col = psdc(dc)
                for hh in range(16):
                    P.op("pe", lambda e, dc=dc, hh=hh, pst=pst, col=col, sl=sl: e.matmul(
                        pst[:, col:col + 256],
                        oww[:, hh * 2048 + dc * 128:hh * 2048 + (dc + 1) * 128],
                        ohw[sl][:, hh * 256:(hh + 1) * 256],
                        start=(hh == 0), stop=(hh == 15)),
                        inc=(s_pe if (dc == 15 and hh == 15) else None))
            ow_done = s_pe.n
            ohw_free[sl] = (s_pe, ow_done)
            P.wait("act", s_pe, ow_done)
            P.op("act", lambda e, sl=sl: e.copy(out=aofb[sl][:, 0:1024], in_=g1[:, :]), inc=None)
            P.op("act", lambda e, sl=sl: e.copy(out=aofb[sl][:, 1024:2048], in_=g2[:, :]), inc=None)
            P.op("act", lambda e, sl=sl: e.copy(out=aofb[sl][:, 2048:2560], in_=g3[:, :]), inc=None)
            P.op("act", lambda e, sl=sl: e.copy(out=aofb[sl][:, 2560:3072], in_=g4[:, :]), inc=None)
            P.op("act", lambda e, sl=sl: e.copy(out=aofb[sl][:, 3072:3584], in_=g5[:, :]), inc=None)
            P.drain("act")
            P.op("act", lambda e, sl=sl: e.copy(out=aofb[sl][:, 3584:4096], in_=g6[:, :]),
                 inc=s_act)
            aev[0] = s_act.n
            if w + 1 < OWN // 256:
                oh_lds[w + 1] = load_ohw(w + 1)
            if pend_b2[0] is not None:
                b2_chain(pend_b2[0])
            pend_b2[0] = w
        b2_chain(pend_b2[0])

        barrier()
        wdnat = [vbf(AR1, s * 4096, 8192) for s in range(2)]
        wdTs = [vbf(AR1, 8192 + s * 4096, 8192) for s in range(2)]
        swd = [s_wd0, s_wd1]
        wdn_free = [None, None]
        wdts_free = [None, None]
        tpd_bank = [0, 0]
        for ob in range(16):
            sl = ob % 2
            if wdn_free[sl] is not None:
                P.wait("pool", *wdn_free[sl])
            P.op("pool", lambda e, ob=ob, sl=sl: e.dma_start(
                out=wdnat[sl][:, :], in_=w_down[ob * 128:(ob + 1) * 128, :]),
                inc=swd[sl], inck=16)
            wd_ld = swd[sl].n
            P.wait("pe", swd[sl], wd_ld)
            if wdts_free[sl] is not None:
                P.wait("pe", *wdts_free[sl])
            tpd = 0
            for grp in range(8):
                pst = (psA, psB)[grp % 2]
                P.wait("pe", s_act, tpd_bank[grp % 2])
                for j in range(8):
                    fc_ = grp * 8 + j
                    P.op("pe", lambda e, fc_=fc_, j=j, pst=pst, sl=sl: e.transpose(
                        pst[:, j * 128:(j + 1) * 128],
                        wdnat[sl][:, fc_ * 128:(fc_ + 1) * 128], identb[:, :]),
                        inc=(s_pe if j == 7 else None))
                tg = s_pe.n
                P.wait("act", s_pe, tg)
                P.op("act", lambda e, grp=grp, pst=pst, sl=sl: e.copy(
                    out=wdTs[sl][:, grp * 1024:(grp + 1) * 1024], in_=pst[:, :]),
                    inc=s_act)
                tpd = s_act.n
                tpd_bank[grp % 2] = s_act.n
            wdn_free[sl] = (s_pe, s_pe.n)
            swt = [s_wt0, s_wt1]
            P.wait("sp", s_act, tpd)
            P.op("sp", lambda e, ob=ob, sl=sl: e.dma_start(
                out=wdTd[:, ob * 128:(ob + 1) * 128].rearrange(
                    "(fc p) o -> p fc o", p=128),
                in_=wdTs[sl].rearrange("p (fc o) -> p fc o", fc=64)),
                inc=swt[sl], inck=16)
            wdts_free[sl] = (swt[sl], swt[sl].n)

        barrier()
        NWC = OWN // 512
        h_sb = vbf(AR1, 0, 16 * OWN)
        ln3fm = vbf(AR1, 16384, 16 * 512)
        ln3tm = vbf(AR1, 20480, 2048)
        xb3 = v32(AR1, 21504, 2048)
        fA = [v32(AR2, c * 512, 512) for c in range(16)]
        fB = [v32(AR2, 8192 + c * 512, 512) for c in range(16)]

        xb3_free = None
        ln3_ev = 0
        fw_ev_bank = [0, 0]
        for w in range(NWC):
            for tau in range(4):
                i_own = w * 4 + tau
                if xb3_free is not None:
                    P.wait("sp", *xb3_free)
                P.op("sp", lambda e, i_own=i_own: e.dma_start(
                    out=xb3[:, :], in_=x2d[i_own * 128:(i_own + 1) * 128, :]),
                    inc=s_x2r0, inck=16)
                x2ld = s_x2r0.n
                P.wait("dve", s_x2r0, x2ld)
                for j in range(4):
                    P.op("dve", lambda e, j=j: e.bn_stats(
                        out=bnst[:, j * 6:(j + 1) * 6],
                        in_=xb3[:, j * 512:(j + 1) * 512]), inc=None)
                P.drain("dve")
                P.op("dve", lambda e: e.bn_aggr(out=mvt[:, 0:2], in_=bnst[:, :]),
                     inc=s_dve)
                bn3 = s_dve.n
                P.wait("act", s_dve, bn3)
                P.op("act", lambda e: e.activation(out=rstdt[:, 0:1], in_=mvt[:, 1:2],
                                                   func=AF.Ln, bias=fc[:, 0:1]), inc=None)
                P.drain("act")
                P.op("act", lambda e: e.activation(out=rstdt[:, 1:2], in_=rstdt[:, 0:1],
                                                   func=AF.Exp, scale=-0.5), inc=s_act)
                rstd3 = s_act.n
                P.wait("dve", s_act, rstd3)
                P.op("dve", lambda e: e.tensor_scalar(
                    out=ln3tm[:, :], in0=xb3[:, :], scalar1=mvt[:, 0:1],
                    scalar2=rstdt[:, 1:2], op0=ALU.subtract, op1=ALU.mult),
                    inc=s_dve)
                ln3_done = s_dve.n
                xb3_free = (s_dve, ln3_done)
                P.wait("pe", s_dve, ln3_done)
                P.wait("pe", s_act, ln3_ev)
                for j in range(16):
                    P.op("pe", lambda e, j=j: e.transpose(
                        (psA if j < 8 else psB)[:, (j % 8) * 128:(j % 8) * 128 + 128],
                        ln3tm[:, j * 128:(j + 1) * 128], identb[:, :]),
                        inc=(s_pe if j == 15 else None))
                tp3 = s_pe.n
                P.wait("act", s_pe, tp3)
                P.op("act", lambda e, tau=tau: e.copy(
                    out=ln3fm.rearrange("p (c t) -> p c t", c=16)[
                        :, 0:8, tau * 128:(tau + 1) * 128],
                    in_=psA[:, :].rearrange("p (c t) -> p c t", c=8)), inc=None)
                P.drain("act")
                P.op("act", lambda e, tau=tau: e.copy(
                    out=ln3fm.rearrange("p (c t) -> p c t", c=16)[
                        :, 8:16, tau * 128:(tau + 1) * 128],
                    in_=psB[:, :].rearrange("p (c t) -> p c t", c=8)), inc=s_act)
                ln3_ev = s_act.n
            P.wait("pe", s_act, ln3_ev)
            if w > 0:
                P.wait("act", s_dve, h_done)
            for c in range(16):
                pst = (g5, g6)[c % 2]
                P.wait("pe", s_act, fw_ev_bank[c % 2])
                P.op("pe", lambda e, c=c, pst=pst: e.matmul(
                    pst[:, :], hsu_sb[:, c * 128:(c + 1) * 128],
                    ln3fm[:, c * 512:(c + 1) * 512], start=True, stop=True),
                    inc=s_pe)
                fmm = s_pe.n
                P.wait("act", s_pe, fmm)
                P.op("act", lambda e, c=c, pst=pst: e.copy(
                    out=fA[c][:, :], in_=pst.ap()), inc=s_act)
                fw_ev_bank[c % 2] = s_act.n
            P.wait("dve", s_act, s_act.n)
            cur, nxt = fA, fB
            for st in range(4):
                step = 1 << st
                for c in range(16):
                    pa, pb = c & ~step, c | step
                    if st == 3:
                        dst = h_sb[:, c * OWN + w * 512:c * OWN + w * 512 + 512]
                    else:
                        dst = nxt[c][:, :]
                    opk = ALU.subtract if (c & step) else ALU.add
                    last_bt = (st == 3 and c == 15)
                    P.op("dve", lambda e, dst=dst, pa=pa, pb=pb, cur=cur, opk=opk: e.tensor_tensor(
                        out=dst, in0=cur[pa][:, :], in1=cur[pb][:, :],
                        op=opk), inc=(s_dve if last_bt else None))
                P.drain("dve")
                cur, nxt = nxt, cur
            h_done = s_dve.n

        barrier()
        wnat_up = [vbf(AR2, s * 1024, 2048) for s in range(2)]
        wnat_gt = [vbf(AR2, 2048 + s * 1024, 2048) for s in range(2)]
        uplhsT = [vbf(AR2, 4096 + s * 1024, 2048) for s in range(2)]
        gtlhsT = [vbf(AR2, 6144 + s * 1024, 2048) for s in range(2)]
        sgate = [v32(AR2, 8192, 512), v32(AR2, 11520, 512)]
        upsb = [v32(AR2, 8704, 512), v32(AR2, 12032, 512)]
        midb = [vbf(AR2, 9216, 512), vbf(AR2, 12544, 512)]
        y2st = [vbf(AR2, 9472 + s * 1024, 2048) for s in range(2)]
        swu = [s_wu0, s_wu1]
        sy2w = [s_y2w0, s_y2w1]
        wn_free = [None, None]
        lh_free = [None, None]
        y2_free = [None, None]
        c2_tp_bank = [0, 0]
        ug_done = [0, 0]
        silu_d = [0, 0]
        mid_done = [0, 0]
        ymm_done = [0, 0]
        y2e = [0]
        pend = [None]

        def emit_ymm(w, s2, ysl):
            P.wait("pe", s_dve, mid_done[s2])
            P.wait("pe", s_act, y2e[0])
            P.op("pe", lambda e, s2=s2: e.matmul(
                g1[:, 0:512], h128b_sb[:, :], midb[s2][:, :],
                start=True, stop=True), inc=s_pe)
            ymm_done[s2] = s_pe.n
            P.wait("act", s_pe, ymm_done[s2])
            P.op("act", lambda e, w=w, ysl=ysl: e.copy(
                out=y2st[ysl][:, w * 512:(w + 1) * 512],
                in_=g1.ap()[:, 0:512]), inc=s_act)
            y2e[0] = s_act.n
        for c2 in range(64):
            sl = c2 % 2
            if wn_free[sl] is not None:
                P.wait("pool", *wn_free[sl])
            P.op("pool", lambda e, c2=c2, sl=sl: e.dma_start(
                out=wnat_up[sl][:, :], in_=w_up[c2 * 128:(c2 + 1) * 128, :]),
                inc=swu[sl], inck=16)
            P.op("pool", lambda e, c2=c2, sl=sl: e.dma_start(
                out=wnat_gt[sl][:, :], in_=w_gate[c2 * 128:(c2 + 1) * 128, :]),
                inc=swu[sl], inck=16)
            wu_ld = swu[sl].n
            P.wait("pe", swu[sl], wu_ld)
            if lh_free[sl] is not None:
                P.wait("pe", *lh_free[sl])
            for j in range(16):
                P.wait("pe", s_act, c2_tp_bank[j % 2])
                P.op("pe", lambda e, j=j, sl=sl: e.transpose(
                    (psA, psB)[j % 2][:, (j // 2) * 128:(j // 2) * 128 + 128],
                    wnat_up[sl][:, j * 128:(j + 1) * 128], identb[:, :]),
                    inc=(s_pe if j >= 14 else None))
            tpu = s_pe.n
            P.wait("act", s_pe, tpu)
            P.op("act", lambda e, sl=sl: e.copy(
                out=uplhsT[sl].rearrange("p (c t) -> p c t", c=16)[:, 0:16:2, :],
                in_=psA[:, :].rearrange("p (c t) -> p c t", c=8)), inc=None)
            P.drain("act")
            P.op("act", lambda e, sl=sl: e.copy(
                out=uplhsT[sl].rearrange("p (c t) -> p c t", c=16)[:, 1:16:2, :],
                in_=psB[:, :].rearrange("p (c t) -> p c t", c=8)), inc=s_act)
            c2_tp_bank[0] = s_act.n
            c2_tp_bank[1] = s_act.n
            for j in range(16):
                P.wait("pe", s_act, c2_tp_bank[j % 2])
                P.op("pe", lambda e, j=j, sl=sl: e.transpose(
                    (psA, psB)[j % 2][:, (j // 2) * 128:(j // 2) * 128 + 128],
                    wnat_gt[sl][:, j * 128:(j + 1) * 128], identb[:, :]),
                    inc=(s_pe if j >= 14 else None))
            tpg = s_pe.n
            wn_free[sl] = (s_pe, tpg)
            P.wait("act", s_pe, tpg)
            P.op("act", lambda e, sl=sl: e.copy(
                out=gtlhsT[sl].rearrange("p (c t) -> p c t", c=16)[:, 0:16:2, :],
                in_=psA[:, :].rearrange("p (c t) -> p c t", c=8)), inc=None)
            P.drain("act")
            P.op("act", lambda e, sl=sl: e.copy(
                out=gtlhsT[sl].rearrange("p (c t) -> p c t", c=16)[:, 1:16:2, :],
                in_=psB[:, :].rearrange("p (c t) -> p c t", c=8)), inc=s_act)
            c2_tp_bank[0] = s_act.n
            c2_tp_bank[1] = s_act.n
            lh_ready = s_act.n
            ysl = c2 % 2
            if y2_free[ysl] is not None:
                P.wait("act", *y2_free[ysl])
            for w in range(NWC):
                s2 = w % 2
                pu, pg = ((g5.ap(), g6.ap()),
                          (g2.ap()[:, 0:512], g2.ap()[:, 512:1024]))[s2]
                P.wait("pe", s_act, lh_ready)
                if c2 == 0 and w == 0:
                    P.wait("pe", s_dve, h_done)
                P.wait("pe", s_dve, mid_done[s2])
                P.wait("pe", s_act, silu_d[s2])
                for c in range(16):
                    P.op("pe", lambda e, c=c, sl=sl, w=w, pu=pu: e.matmul(
                        pu[:, :], uplhsT[sl][:, c * 128:(c + 1) * 128],
                        h_sb[:, c * OWN + w * 512:c * OWN + (w + 1) * 512],
                        start=(c == 0), stop=(c == 15)), inc=None)
                for c in range(16):
                    P.op("pe", lambda e, c=c, sl=sl, w=w, pg=pg: e.matmul(
                        pg[:, :], gtlhsT[sl][:, c * 128:(c + 1) * 128],
                        h_sb[:, c * OWN + w * 512:c * OWN + (w + 1) * 512],
                        start=(c == 0), stop=(c == 15)),
                        inc=(s_pe if c == 15 else None))
                ug_done[s2] = s_pe.n
                if pend[0] is not None:
                    emit_ymm(*pend[0])
                P.wait("act", s_pe, ug_done[s2])
                P.op("act", lambda e, c2=c2, pg=pg, s2=s2: e.activation(
                    out=sgate[s2][:, :], in_=pg[:, :], func=AF.Silu,
                    scale=wscg_cols[:, c2:c2 + 1]), inc=s_act)
                silu_d[s2] = s_act.n
                P.wait("dve", s_pe, ug_done[s2])
                P.wait("dve", s_pe, ymm_done[s2])
                P.op("dve", lambda e, c2=c2, pu=pu, s2=s2: e.tensor_scalar(
                    out=upsb[s2][:, :], in0=pu[:, :],
                    scalar1=upsc_cols[:, c2:c2 + 1],
                    scalar2=None, op0=ALU.mult), inc=None)
                P.drain("dve")
                P.wait("dve", s_act, silu_d[s2])
                P.op("dve", lambda e, s2=s2: e.tensor_tensor(
                    out=midb[s2][:, :], in0=upsb[s2][:, :], in1=sgate[s2][:, :],
                    op=ALU.mult), inc=s_dve)
                mid_done[s2] = s_dve.n
                pend[0] = (w, s2, ysl)
            emit_ymm(*pend[0])
            pend[0] = None
            P.wait("sp", s_act, y2e[0])
            P.op("sp", lambda e, c2=c2, ysl=ysl: e.dma_start(
                out=y2d[c2 * 128:(c2 + 1) * 128, 0:OWN],
                in_=y2st[ysl][:, 0:OWN]), inc=sy2w[ysl], inck=16)
            y2_free[ysl] = (sy2w[ysl], sy2w[ysl].n)

        barrier()
        NW3 = OWN // 256
        z_sb = [vbf(AR1, s * 8192, 64 * 256) for s in range(2)]
        ffo = vbf(AR1, 16384, 16 * 256)
        atm3 = v32(AR1, 18432, 2048)
        x2b = v32(AR1, 20480, 2048)
        wdl = [vbf(AR1, 22528 + s * 1024, 2048) for s in range(2)]
        y2w = vbf(AR2, 0, 64 * 256)
        pA = [vbf(AR2, 8192 + c * 128, 256) for c in range(64)]
        pB = [vbf(AR2, c * 128, 256) for c in range(64)]
        swdl = [s_wl0, s_wl1]
        DVE_CH = 64

        def ps8b(oc):
            if oc < 2:
                return g1, oc * 512
            if oc < 4:
                return g2, (oc - 2) * 512
            return (g3, g4, g5, g6)[oc - 4], 0

        y2w_free = None
        wd_free = [None, None]
        z_free = [0, 0]
        down_ev = 0
        x2b_free = None
        atm3_free = 0
        out_dma = 0
        dn_done = 0
        bt_done = [(0, 0), (0, 0)]

        def emit_butterfly(w):
            nonlocal y2w_free
            zsl = w % 2
            col0 = w * 256
            if y2w_free is not None:
                P.wait("act", *y2w_free)
            P.op("act", lambda e, col0=col0: e.dma_start(
                out=y2w.rearrange("p (c t) -> p c t", c=64),
                in_=y2d[:, col0:col0 + 256].rearrange(
                    "(c p) t -> p c t", p=128)), inc=s_y0, inck=16)
            yld = s_y0.n
            P.wait("dve", s_y0, yld)
            P.wait("pool", s_y0, yld)
            P.wait("dve", s_pe, z_free[zsl])
            P.wait("pool", s_pe, z_free[zsl])
            for st in range(6):
                step = 1 << st
                P.wait("dve", s_pl, s_pl.n)
                P.wait("pool", s_dve, s_dve.n)
                for c in range(64):
                    eng = "dve" if c < DVE_CH else "pool"
                    pa_, pb_ = c & ~step, c | step
                    if st == 0:
                        i0 = y2w[:, pa_ * 256:(pa_ + 1) * 256]
                        i1 = y2w[:, pb_ * 256:(pb_ + 1) * 256]
                    else:
                        src_ = pA if st % 2 == 1 else pB
                        i0 = src_[pa_][:, :]
                        i1 = src_[pb_][:, :]
                    if st == 5:
                        dst = z_sb[zsl][:, c * 256:(c + 1) * 256]
                    else:
                        dst_ = pA if st % 2 == 0 else pB
                        dst = dst_[c][:, :]
                    opk = ALU.subtract if (c & step) else ALU.add
                    lastd = (c == DVE_CH - 1)
                    lastp = (c == 63)
                    P.op(eng, lambda e, dst=dst, i0=i0, i1=i1, opk=opk: e.tensor_tensor(
                        out=dst, in0=i0, in1=i1, op=opk),
                        inc=(s_dve if (eng == "dve" and lastd) else
                             (s_pl if (eng == "pool" and lastp) else None)))
                P.drain("dve")
                P.drain("pool")
            y2w_free = (s_dve, s_dve.n)
            bt_done[zsl] = (s_dve.n, s_pl.n)
            bt_done[zsl] = (s_dve.n, s_pl.n)

        emit_butterfly(0)
        for w in range(NW3):
            zsl = w % 2
            P.wait("pe", s_dve, bt_done[zsl][0])
            P.wait("pe", s_pl, bt_done[zsl][1])
            for op_ in range(2):
                P.wait("pe", s_act, down_ev)
                if op_ == 0:
                    P.wait("pe", s_act, atm3_free)
                for f2g in range(32):
                    sl = f2g % 2
                    if wd_free[sl] is not None:
                        P.wait("sp", *wd_free[sl])
                    P.op("sp", lambda e, f2g=f2g, sl=sl, op_=op_: e.dma_start(
                        out=wdl[sl][:, 0:2048].rearrange("p (f n) -> p f n", f=2),
                        in_=wdTd[f2g * 256:(f2g + 1) * 256,
                                 op_ * 1024:(op_ + 1) * 1024].rearrange(
                            "(f p) n -> p f n", p=128)),
                        inc=swdl[sl], inck=16)
                    wld = swdl[sl].n
                    P.wait("pe", swdl[sl], wld)
                    for fi in range(2):
                        f2 = f2g * 2 + fi
                        for oc in range(8):
                            pst, col = ps8b(oc)
                            P.op("pe", lambda e, f2=f2, fi=fi, oc=oc, pst=pst,
                                 col=col, sl=sl, zsl=zsl: e.matmul(
                                pst[:, col:col + 256],
                                wdl[sl][:, fi * 1024 + oc * 128:fi * 1024 + (oc + 1) * 128],
                                z_sb[zsl][:, f2 * 256:(f2 + 1) * 256],
                                start=(f2 == 0), stop=(f2 == 63)),
                                inc=(s_pe if (oc == 7 and fi == 1) else None))
                    wd_free[sl] = (s_pe, s_pe.n)
                dn_done = s_pe.n
                P.wait("act", s_pe, dn_done)
                for oc in range(8):
                    ocg = op_ * 8 + oc
                    pst, col = ps8b(oc)
                    if oc > 0:
                        P.drain("act")
                    P.op("act", lambda e, ocg=ocg, pst=pst, col=col: e.activation(
                        out=ffo[:, ocg * 256:(ocg + 1) * 256],
                        in_=pst[:, col:col + 256],
                        func=AF.Copy, scale=wscdp_sb[:, ocg:ocg + 1]),
                        inc=(s_act if oc == 7 else None))
                down_ev = s_act.n
            z_free[zsl] = dn_done
            if w + 1 < NW3:
                emit_butterfly(w + 1)
            for tau in range(2):
                i_own = w * 2 + tau
                if x2b_free is not None:
                    P.wait("sp", *x2b_free)
                P.op("sp", lambda e, i_own=i_own: e.dma_start(
                    out=x2b[:, :], in_=x2d[i_own * 128:(i_own + 1) * 128, :]),
                    inc=s_x2r1, inck=16)
                x2ld = s_x2r1.n
                P.wait("pe", s_act, down_ev)
                P.wait("pe", s_act, atm3_free)
                for j in range(16):
                    P.op("pe", lambda e, j=j, tau=tau: e.transpose(
                        (psA if j < 8 else psB)[:, (j % 8) * 128:(j % 8) * 128 + 128],
                        ffo[:, j * 256 + tau * 128:j * 256 + (tau + 1) * 128],
                        identb[:, :]), inc=(s_pe if j == 15 else None))
                tp4 = s_pe.n
                P.wait("act", s_pe, tp4)
                P.wait("act", s_out, out_dma)
                P.op("act", lambda e: e.copy(out=atm3[:, 0:1024], in_=psA[:, :]),
                     inc=None)
                P.drain("act")
                P.op("act", lambda e: e.copy(out=atm3[:, 1024:2048], in_=psB[:, :]),
                     inc=s_act)
                atm3_ev = s_act.n
                atm3_free = s_act.n
                P.wait("dve", s_act, atm3_ev)
                for j in range(4):
                    P.op("dve", lambda e, j=j: e.bn_stats(
                        out=bnst[:, j * 6:(j + 1) * 6],
                        in_=atm3[:, j * 512:(j + 1) * 512]), inc=None)
                P.drain("dve")
                P.op("dve", lambda e: e.bn_aggr(out=mvt[:, 0:2], in_=bnst[:, :]),
                     inc=s_dve)
                bn4 = s_dve.n
                P.wait("act", s_dve, bn4)
                P.op("act", lambda e: e.activation(out=rstdt[:, 0:1], in_=mvt[:, 1:2],
                                                   func=AF.Ln, bias=fc[:, 0:1]), inc=None)
                P.drain("act")
                P.op("act", lambda e: e.activation(out=rstdt[:, 1:2], in_=rstdt[:, 0:1],
                                                   func=AF.Exp, scale=-0.5), inc=s_act)
                rstd4 = s_act.n
                P.wait("dve", s_act, rstd4)
                P.wait("dve", s_x2r1, x2ld)
                P.op("dve", lambda e: e.tensor_scalar(
                    out=atm3[:, :], in0=atm3[:, :], scalar1=mvt[:, 0:1],
                    scalar2=rstdt[:, 1:2], op0=ALU.subtract, op1=ALU.mult),
                    inc=None)
                P.drain("dve")
                P.op("dve", lambda e: e.tensor_tensor(
                    out=atm3[:, :], in0=atm3[:, :], in1=x2b[:, :], op=ALU.add),
                    inc=s_dve)
                x3_done = s_dve.n
                x2b_free = (s_dve, x3_done)
                P.wait("sp", s_dve, x3_done)
                P.op("sp", lambda e, i_own=i_own: e.dma_start(
                    out=out[i_own * 128:(i_own + 1) * 128, :], in_=atm3[:, :]),
                    inc=s_out, inck=16)
                out_dma = s_out.n

        P.wait("sp", s_out, s_out.n)
        P.wait("pool", s_po, s_po.n)

        import os
        _tr = float(os.environ.get("K_TRUNC", "1.0"))
        if _tr < 1.0:
            for k in P.q:
                P.q[k] = P.q[k][:int(len(P.q[k]) * _tr)]
        for k in P.q:
            _trk = float(os.environ.get("K_TRUNC_" + k.upper(), "1.0"))
            if _trk < 1.0:
                P.q[k] = P.q[k][:int(len(P.q[k]) * _trk)]
        with nc.Block() as block:
            @block.tensor
            def _(e):
                for f in P.q["pe"]:
                    f(e)

            @block.scalar
            def _(e):
                for f in P.q["act"]:
                    f(e)

            @block.vector
            def _(e):
                for f in P.q["dve"]:
                    f(e)

            @block.gpsimd
            def _(e):
                for f in P.q["pool"]:
                    f(e)

            @block.sync
            def _(e):
                for f in P.q["sp"]:
                    f(e)

    return nc


def _host_inputs(inputs, T=8192):
    OWN = T // 4
    x = np.ascontiguousarray(np.asarray(inputs["L0_attn_x"], dtype=np.float32)
                             .reshape(2 * T, D))
    q_a_w = np.ascontiguousarray(inputs["L0_attn_q_a_w"], dtype=np.float32)
    q_b_w = np.asarray(inputs["L0_attn_q_b_w"], dtype=np.float32)
    kv_a_w = np.ascontiguousarray(inputs["L0_attn_kv_a_w"], dtype=np.float32)
    kv_b_w = np.asarray(inputs["L0_attn_kv_b_w"], dtype=np.float32)
    o_w = np.ascontiguousarray(inputs["L0_attn_o_w"], dtype=np.float32)
    w_up = np.ascontiguousarray(inputs["L0_ffn_W_up"], dtype=np.float32)
    w_gate = np.ascontiguousarray(inputs["L0_ffn_W_gate"], dtype=np.float32)
    w_down = np.ascontiguousarray(inputs["L0_ffn_W_down"], dtype=np.float32)
    wq = np.asarray(inputs["L0_attn_q_a_ln_w"], dtype=np.float32)
    wkv = np.asarray(inputs["L0_attn_kv_a_ln_w"], dtype=np.float32)
    su_ug = np.asarray(inputs["L0_ffn_SU_ug"], dtype=np.float32)
    su_dp = np.asarray(inputs["L0_ffn_SU_dp"], dtype=np.float32)
    wsc_ug = np.asarray(inputs["L0_ffn_Wscale_ug"], dtype=np.float32)
    wsc_dp = np.asarray(inputs["L0_ffn_Wscale_dp"], dtype=np.float32)

    inv = (1.0 / THETA ** (np.arange(0, ROPE, 2) / ROPE)).astype(np.float64)
    tpos = np.arange(T, dtype=np.float64)
    fr = np.outer(tpos, inv)
    cos_t = np.ascontiguousarray(np.cos(fr).astype(np.float32))
    sin_t = np.ascontiguousarray(np.sin(fr).astype(np.float32))

    def hmat(m):
        Hm = np.array([[1.0]], dtype=np.float32)
        while Hm.shape[0] < m:
            Hm = np.block([[Hm, Hm], [Hm, -Hm]])
        return Hm
    H128m = np.ascontiguousarray(hmat(128).astype(np.float32))
    ones_b = np.ones((128, 128), dtype=ml_dtypes.bfloat16)
    ident_b = np.eye(128, dtype=ml_dtypes.bfloat16)
    ident_f = np.eye(128, dtype=np.float32)

    def cols(v, npart=128):
        return np.ascontiguousarray(v.reshape(-1, npart).T.astype(np.float32))

    NCH = OWN // 512
    LIMS = [16 * (s + 1) for s in range(NCH)]

    in_maps = []
    for r in range(NC_):
        b = r // 4
        j = r % 4
        chunks = [(s * 4 + j) if s % 2 == 0 else ((s + 1) * 4 - 1 - j)
                  for s in range(NCH)]
        own_rows = np.concatenate([np.arange(c * 512, (c + 1) * 512)
                                   for c in chunks])
        xb = x[b * T:(b + 1) * T]
        mk = np.zeros((128, NCH * 16 * 512), dtype=ml_dtypes.bfloat16)
        for s, c in enumerate(chunks):
            for ti in range(16):
                kt = LIMS[s] - 16 + ti
                krow = kt * 128 + np.arange(128)[:, None]
                qcol = c * 512 + np.arange(512)[None, :]
                mk[:, (s * 16 + ti) * 512:(s * 16 + ti + 1) * 512] = (
                    krow <= qcol).astype(ml_dtypes.bfloat16)
        cos16 = np.tile(cos_t[own_rows], (1, 16))
        sin16 = np.tile(sin_t[own_rows], (1, 16))
        mp = {
            "x_own": np.ascontiguousarray(xb[own_rows]),
            "x_batch": np.ascontiguousarray(xb),
            "q_a_w": q_a_w, "q_b_w": np.ascontiguousarray(q_b_w),
            "kv_a_w": kv_a_w, "kv_b_w": np.ascontiguousarray(kv_b_w),
            "o_w": o_w,
            "w_up": w_up, "w_gate": w_gate, "w_down": w_down,
            "wq_cols": cols(wq), "wkv_cols": cols(wkv),
            "su_ug_cols": cols(su_ug), "su_dp_cols": cols(su_dp),
            "wsc_ug_cols": cols(wsc_ug), "wsc_dp_cols": cols(wsc_dp),
            "cos_all": cos_t, "sin_all": sin_t,
            "cos16_own": np.ascontiguousarray(cos16),
            "sin16_own": np.ascontiguousarray(sin16),
            "h128": H128m, "ones_bf": ones_b, "ident_bf": ident_b,
            "ident_f32": ident_f, "masks_in": np.ascontiguousarray(mk),
            "fconst": np.ascontiguousarray(np.broadcast_to(np.array(
                [EPS_LN, EPS_RMS, math.log(ATT_SCALE), 0.0], dtype=np.float32),
                (128, 4))),
        }
        in_maps.append(mp)
    return in_maps


def _assemble(shards, T):
    OWN = T // 4
    NCH = OWN // 512
    full = np.empty((2, T, D), dtype=np.float32)
    for r in range(NC_):
        b = r // 4
        j = r % 4
        chunks = [(s * 4 + j) if s % 2 == 0 else ((s + 1) * 4 - 1 - j)
                  for s in range(NCH)]
        for s, c in enumerate(chunks):
            full[b, c * 512:(c + 1) * 512] = shards[r][s * 512:(s + 1) * 512]
    return full


def kernel(**inputs):
    T = 8192
    in_maps = _host_inputs(inputs, T=T)
    nc = build_program(T=T, debug=False)
    res = run_bass_kernel_spmd(nc, in_maps, core_ids=list(range(NC_)))
    shards = [res.results[r]["out"] for r in range(NC_)]
    return _assemble(shards, T)
